# version 11
# baseline (speedup 1.0000x reference)
import math
import numpy as np
import ml_dtypes
from contextlib import ExitStack

import concourse.bass as bass
import concourse.mybir as mybir
from concourse.bass_utils import run_bass_kernel_spmd

F32 = mybir.dt.float32
BF16 = mybir.dt.bfloat16
F16 = mybir.dt.float16
I32 = mybir.dt.int32
AF = mybir.ActivationFunctionType
ALU = mybir.AluOpType
AX = mybir.AxisListType

AWQ_ORDER = [0, 4, 1, 5, 2, 6, 3, 7]
NCORES = 8
EPS = 1e-5


def make_cfg(D, I, S, B):
    c = {}
    c["D"], c["I"], c["S"], c["B"] = D, I, S, B
    c["DK"] = D // 128
    c["NT"] = I // 128
    c["H"] = D // 128
    c["NQT"] = S // 128
    c["SLOTS"] = c["NQT"] // 2
    c["R"] = c["SLOTS"] * 128
    c["RH"] = c["R"] // 2
    c["DKC"] = c["DK"] // NCORES
    c["DTL"] = (D // NCORES) // 128
    c["CT"] = 3 * c["DK"]
    c["SNT"] = 4 if c["NT"] % 4 == 0 else 2
    c["NSTG"] = c["NT"] // c["SNT"]
    c["DBLK"] = 4 if (c["DK"] % (4 * c["DTL"]) == 0 and c["DK"] >= 4 * c["DTL"]) else 2
    c["DPB"] = c["DK"] // c["DBLK"]
    c["CPB"] = c["DPB"] // c["DTL"]
    c["KCH"] = c["DK"] // 2
    c["SPH"] = c["SLOTS"] // 2
    c["QW"] = c["SPH"] * 128
    assert c["DK"] % NCORES == 0 and c["NT"] % c["SNT"] == 0
    assert c["SLOTS"] % 2 == 0 and c["DPB"] % c["DTL"] == 0
    return c


CFG = make_cfg(4096, 11008, 2048, 4)


def alibi_slopes(n_heads):
    cp2 = 2 ** math.floor(math.log2(n_heads))
    base = 2 ** (-(2 ** (-(math.log2(cp2) - 3))))
    slopes = base ** np.arange(1, cp2 + 1, dtype=np.float64)
    if cp2 != n_heads:
        extra_base = 2 ** (-(2 ** (-(math.log2(2 * cp2) - 3))))
        n_rem = min(cp2, n_heads - cp2)
        extra = extra_base ** np.arange(1, 1 + 2 * n_rem, 2, dtype=np.float64)
        slopes = np.concatenate([slopes, extra])
    return slopes.astype(np.float32)


class Sem:
    def __init__(self, stack, nc, name):
        self.h = stack.enter_context(nc.semaphore(name))
        self.n = 0

    def inc(self, instr, k=1):
        instr.then_inc(self.h, k)
        self.n += k
        return self.n

    def wait(self, eng, target=None):
        t = self.n if target is None else target
        if t > 0:
            eng.wait_ge(self.h, t)


class SemP:

    def __init__(self, stack, nc, name):
        self.s = [Sem(stack, nc, f"{name}_a"), Sem(stack, nc, f"{name}_b")]

    def inc(self, slot, instr, k=16):
        return self.s[slot % 2].inc(instr, k)

    def wait(self, eng, slot, target):
        self.s[slot % 2].wait(eng, target)

    def totals(self):
        return (self.s[0].n, self.s[1].n)

    def wait_totals(self, eng, t=None):
        if t is None:
            self.s[0].wait(eng)
            self.s[1].wait(eng)
        else:
            self.s[0].wait(eng, t[0])
            self.s[1].wait(eng, t[1])


def build_program(cfg, debug_stop=None):
    D, I, S = cfg["D"], cfg["I"], cfg["S"]
    DK, NT, H, NQT = cfg["DK"], cfg["NT"], cfg["H"], cfg["NQT"]
    SLOTS, R, RH = cfg["SLOTS"], cfg["R"], cfg["RH"]
    DKC, DTL, CT = cfg["DKC"], cfg["DTL"], cfg["CT"]
    SNT, NSTG, DBLK, DPB, CPB = cfg["SNT"], cfg["NSTG"], cfg["DBLK"], cfg["DPB"], cfg["CPB"]
    KCH, SPH, QW = cfg["KCH"], cfg["SPH"], cfg["QW"]
    GPS = DBLK * DPB
    slopes = alibi_slopes(H)
    TS = 128 * 128

    nc = bass.Bass("TRN2", target_bir_lowering=False, dynamic_dma_scratch_size=4096)

    xin = nc.dram_tensor("xin", [D, R], F32, kind="ExternalInput")
    gq = nc.dram_tensor("gq", [DKC * 128, NT * 16], I32, kind="ExternalInput")
    gs = nc.dram_tensor("gs", [DKC, NT * 128], F32, kind="ExternalInput")
    gz = nc.dram_tensor("gz", [DKC, NT * 16], I32, kind="ExternalInput")
    uq = nc.dram_tensor("uq", [DKC * 128, NT * 16], I32, kind="ExternalInput")
    us = nc.dram_tensor("us", [DKC, NT * 128], F32, kind="ExternalInput")
    uz = nc.dram_tensor("uz", [DKC, NT * 16], I32, kind="ExternalInput")
    dq = nc.dram_tensor("dq", [I, DTL * 16], I32, kind="ExternalInput")
    ds = nc.dram_tensor("ds", [NT, DTL * 128], F32, kind="ExternalInput")
    dz = nc.dram_tensor("dz", [NT, DTL * 16], I32, kind="ExternalInput")
    wat = nc.dram_tensor("wat", [DKC * 128, 3 * D], F32, kind="ExternalInput")
    wpr = nc.dram_tensor("wpr", [DKC * 128, D], F32, kind="ExternalInput")
    bacols = nc.dram_tensor("bacols", [128, CT], F32, kind="ExternalInput")
    bpcols = nc.dram_tensor("bpcols", [128, DK], F32, kind="ExternalInput")
    ident_in = nc.dram_tensor("ident_in", [128, 128], BF16, kind="ExternalInput")
    masks_in = nc.dram_tensor("masks_in", [128, SLOTS * 2 * 128], BF16, kind="ExternalInput")
    qsb_in = nc.dram_tensor("qsb_in", [1, R], F32, kind="ExternalInput")
    kcol_in = nc.dram_tensor("kcol_in", [128, NQT], F32, kind="ExternalInput")

    yout = nc.dram_tensor("yout", [D, R], F32, kind="ExternalOutput")

    xres = nc.dram_tensor("xres", [D, R], F32)
    wg_sl = nc.dram_tensor("wg_sl", [DKC * NT * 128, 128], BF16)
    wg_all = nc.dram_tensor("wg_all", [DK * NT * 128, 128], BF16, addr_space="Shared")
    wu_sl = nc.dram_tensor("wu_sl", [DKC * NT * 128, 128], BF16)
    wu_all = nc.dram_tensor("wu_all", [DK * NT * 128, 128], BF16, addr_space="Shared")
    wd_sl = nc.dram_tensor("wd_sl", [NT * DTL * 128, 128], BF16)
    wd_all = nc.dram_tensor("wd_all", [NCORES * NT * DTL * 128, 128], BF16, addr_space="Shared")
    wat_sl = nc.dram_tensor("wat_sl", [DKC * CT * 128, 128], BF16)
    wat_all = nc.dram_tensor("wat_all", [DK * CT * 128, 128], BF16, addr_space="Shared")
    wpr_sl = nc.dram_tensor("wpr_sl", [DKC * DK * 128, 128], BF16)
    wpr_all = nc.dram_tensor("wpr_all", [DK * DK * 128, 128], BF16, addr_space="Shared")
    zs_g = nc.dram_tensor("zs_g", [DKC, NT * 128], F32)
    zs_u = nc.dram_tensor("zs_u", [DKC, NT * 128], F32)
    zs_d = nc.dram_tensor("zs_d", [NT, DTL * 128], F32)
    q_own = nc.dram_tensor("q_own", [D, R], BF16)
    kt_own = nc.dram_tensor("kt_own", [D, R], BF16)
    kt_all = nc.dram_tensor("kt_all", [2 * D, R], BF16)
    v_own = nc.dram_tensor("v_own", [R, D], BF16)
    v_all = nc.dram_tensor("v_all", [2 * R, D], BF16)
    m_bnc = nc.dram_tensor("m_bnc", [1, R], F32)
    r_bnc = nc.dram_tensor("r_bnc", [1, R], F32)
    c_bnc = nc.dram_tensor("c_bnc", [1, 1], F32)
    rh_bnc = nc.dram_tensor("rh_bnc", [H, R], F32)

    PAIRS = [[0, 1], [2, 3], [4, 5], [6, 7]]
    ALL8 = [list(range(NCORES))]

    with ExitStack() as st:
        _uid = [0]

        def un(n):
            _uid[0] += 1
            return f"{n}_{_uid[0]}"

        sb = lambda name, shape, dt: st.enter_context(nc.sbuf_tensor(name, shape, dt))
        sem = lambda name: Sem(st, nc, name)
        semp = lambda name: SemP(st, nc, name)

        acc = sb("acc", [128, DK * R], F16)
        m_rep = sb("m_rep", [128, R], F32)
        r_rep = sb("r_rep", [128, R], F32)
        ident = sb("ident_sb", [128, 128], BF16)
        ones_col = sb("ones_col", [128, 1], BF16)
        ones_row = sb("ones_row", [1, 128], BF16)
        ba_sb = sb("ba_sb", [128, CT], F32)
        bp_sb = sb("bp_sb", [128, DK], F32)
        lnrow_a = sb("lnrow_a", [1, R], F32)
        lnrow_b = sb("lnrow_b", [1, R], F32)

        s_deq = semp("s_deq")
        s_dx = semp("s_dx")
        s_dw = semp("s_dw")
        s_dwd = semp("s_dwd")
        s_out = semp("s_out")
        s_rb = semp("s_rb")
        s_sto = semp("s_sto")
        s_sv = semp("s_sv")
        s_zs = semp("s_zs")
        s_bnc = sem("s_bnc")
        s_g = sem("s_g"); s_u = sem("s_u"); s_eg = sem("s_eg"); s_eu = sem("s_eu")
        s_h = sem("s_h"); s_sil = sem("s_sil"); s_dmm = sem("s_dmm")
        s_dev = sem("s_dev"); s_dac = sem("s_dac"); s_sums = sem("s_sums")
        s_lnv = sem("s_lnv"); s_lna = sem("s_lna"); s_fp = sem("s_fp")
        s_q = sem("s_q"); s_qe = sem("s_qe"); s_e1 = sem("s_e1")
        s_vt = sem("s_vt"); s_vs = sem("s_vs"); s_pj = sem("s_pj")
        s_p1 = sem("s_p1"); s_m = sem("s_m"); s_st = sem("s_st")
        s_ex = sem("s_ex"); s_mk = sem("s_mk"); s_os = sem("s_os")
        s_pv = sem("s_pv"); s_rr = sem("s_rr"); s_pc = sem("s_pc")
        s_b2 = sem("s_b2"); s_dqv = sem("s_dqv"); ch_v = sem("ch_v")
        s_ba = sem("s_ba"); s_baa = sem("s_baa")
        s_cc = [sem(f"s_cc{i}") for i in range(8)]

        with nc.Block() as blk:
            @blk.sync
            def _(sy):
                s_deq.inc(0, sy.dma_start(ident[:, :], ident_in[:, :]))
                s_deq.inc(0, sy.dma_start(ba_sb[:, :], bacols[:, :]))
                s_deq.inc(0, sy.dma_start(bp_sb[:, :], bpcols[:, :]))

            @blk.vector
            def _(v):
                s_deq.wait_totals(v)
                v.memset(ones_col[:, :], 1.0)
                v.memset(ones_row[:, :], 1.0)

        zs_specs = [(gz, gs, zs_g, DKC, NT * 128, 1.0),
                    (uz, us, zs_u, DKC, NT * 128, 1.0),
                    (dz, ds, zs_d, NT, DTL * 128, 0.5)]
        with ExitStack() as ps:
            psb = lambda n, sh, dt: ps.enter_context(nc.sbuf_tensor(un(n), sh, dt))
            W0 = min(2752, max(NT * 128, DTL * 128))
            srow = [psb("srow", [1, W0], F32) for i in range(2)]
            zrow = [psb("zrow", [1, W0 // 8], I32) for i in range(2)]
            zi = psb("zi", [1, W0], I32)
            zf = psb("zf", [1, W0], F32)
            zs = [psb("zs", [1, W0], F32) for i in range(2)]
            groups = []
            for (zdr, sdr, odr, ngr, w, scl) in zs_specs:
                GB = max(1, min(ngr, W0 // w))
                for g0 in range(0, ngr, GB):
                    gb = min(GB, ngr - g0)
                    wtot = gb * w
                    for c0 in range(0, wtot, W0):
                        groups.append((zdr, sdr, odr, g0, gb, w, c0,
                                       min(W0, wtot - c0), scl))
            with nc.Block() as blk:
                b_dqv = s_dqv.n
                b_zs = s_zs.totals()
                ld_t = {}
                zst_t = {}

                @blk.sync
                def _(sy):
                    for gi, (zdr, sdr, odr, g0, gb, w, c0, cur, scl) in enumerate(groups):
                        if gi >= 2:
                            s_dqv.wait(sy, b_dqv + (gi - 1))
                        s_deq.inc(gi, sy.dma_start(
                            srow[gi % 2][0:1, 0:cur],
                            bass.AP(sdr, g0 * w + c0, [[1, 1], [1, cur]])))
                        ld_t[gi] = s_deq.inc(gi, sy.dma_start(
                            zrow[gi % 2][0:1, 0:cur // 8],
                            bass.AP(zdr, (g0 * w + c0) // 8, [[1, 1], [1, cur // 8]])))

                @blk.vector
                def _(v):
                    for gi, (zdr, sdr, odr, g0, gb, w, c0, cur, scl) in enumerate(groups):
                        s_deq.wait(v, gi, ld_t[gi])
                        if gi >= 2:
                            s_zs.wait(v, gi, b_zs[gi % 2] + 16 * ((gi - 2) // 2 + 1))
                        b = gi % 2
                        t0 = 0
                        for pos in range(8):
                            sh = 4 * AWQ_ORDER[pos]
                            t0 = ch_v.inc(v.tensor_scalar(
                                bass.AP(zi, pos, [[W0, 1], [8, cur // 8]]),
                                zrow[b][0:1, 0:cur // 8], sh, 15,
                                ALU.logical_shift_right, ALU.bitwise_and))
                        ch_v.wait(v, t0)
                        t1 = ch_v.inc(v.tensor_copy(zf[0:1, 0:cur], zi[0:1, 0:cur]))
                        ch_v.wait(v, t1)
                        ins = v.tensor_tensor(zs[b][0:1, 0:cur], zf[0:1, 0:cur],
                                              srow[b][0:1, 0:cur], ALU.mult)
                        if scl != 1.0:
                            t2 = ch_v.inc(ins)
                            ch_v.wait(v, t2)
                            ins = v.tensor_scalar(zs[b][0:1, 0:cur], zs[b][0:1, 0:cur],
                                                  scl, None, ALU.mult)
                        s_dqv.inc(ins)

                @blk.scalar
                def _(sc):
                    for gi, (zdr, sdr, odr, g0, gb, w, c0, cur, scl) in enumerate(groups):
                        s_dqv.wait(sc, b_dqv + (gi + 1))
                        zst_t[gi] = s_zs.inc(gi, sc.dma_start(
                            bass.AP(odr, g0 * w + c0, [[1, 1], [1, cur]]),
                            zs[gi % 2][0:1, 0:cur]))
        zs_total = s_zs.totals()

        def emit_dequant_q(qdr, sdr, zsdr, odr, ngroups, width, scl, name):
            GB = max(1, min(ngroups, 2048 // width)) if width <= 1024 else 1
            CW = min(width * GB, 2560)
            with ExitStack() as ps:
                psb = lambda n, sh, dt: ps.enter_context(nc.sbuf_tensor(un(n), sh, dt))
                qwb = [psb("qw", [128, CW // 8], I32) for i in range(2)]
                srep = [psb("sr", [128, CW], F32) for i in range(2)]
                zsrep = [psb("zr", [128, CW], F32) for i in range(2)]
                ti = psb("ti", [128, CW], I32)
                tf = psb("tf", [128, CW], F32)
                wbf = [psb("wb", [128, CW], BF16) for i in range(2)]
                chunks = []
                for g0 in range(0, ngroups, GB):
                    gb = min(GB, ngroups - g0)
                    wtot = gb * width
                    for c0 in range(0, wtot, CW):
                        chunks.append((g0, gb, c0, min(CW, wtot - c0)))
                with nc.Block() as blk:
                    b_dqv = s_dqv.n
                    b_sto = s_sto.totals()
                    ld_t = {}

                    @blk.sync
                    def _(sy):
                        s_zs.wait_totals(sy, zs_total)
                        s_sto.wait_totals(sy, b_sto)
                        for ci, (g0, gb, c0, cur) in enumerate(chunks):
                            if ci >= 2:
                                s_dqv.wait(sy, b_dqv + (ci - 1))
                            if gb > 1:
                                s_deq.inc(ci, sy.dma_start(
                                    bass.AP(qwb[ci % 2], 0,
                                            [[CW // 8, 128], [width // 8, gb],
                                             [1, width // 8]]),
                                    bass.AP(qdr, g0 * 128 * (width // 8),
                                            [[width // 8, 128],
                                             [128 * (width // 8), gb],
                                             [1, width // 8]])))
                            else:
                                s_deq.inc(ci, sy.dma_start(
                                    qwb[ci % 2][:, 0:cur // 8],
                                    bass.AP(qdr, g0 * 128 * (width // 8) + c0 // 8,
                                            [[width // 8, 128], [1, cur // 8]])))
                            s_deq.inc(ci, sy.dma_start(
                                srep[ci % 2][:, 0:cur],
                                bass.AP(sdr, g0 * width + c0, [[0, 128], [1, cur]])))
                            ld_t[ci] = s_deq.inc(ci, sy.dma_start(
                                zsrep[ci % 2][:, 0:cur],
                                bass.AP(zsdr, g0 * width + c0, [[0, 128], [1, cur]])))

                    @blk.vector
                    def _(v):
                        s_sto.wait_totals(v, b_sto)
                        for ci, (g0, gb, c0, cur) in enumerate(chunks):
                            s_deq.wait(v, ci, ld_t[ci])
                            if ci >= 2:
                                s_sto.wait(v, ci, b_sto[ci % 2] + 16 * (ci // 2))
                            b = ci % 2
                            t0 = 0
                            for pos in range(8):
                                sh = 4 * AWQ_ORDER[pos]
                                t0 = ch_v.inc(v.tensor_scalar(
                                    bass.AP(ti, pos, [[CW, 128], [8, cur // 8]]),
                                    qwb[b][:, 0:cur // 8], sh, 15,
                                    ALU.logical_shift_right, ALU.bitwise_and))
                            ch_v.wait(v, t0)
                            t1 = ch_v.inc(v.tensor_scalar(tf[:, 0:cur], ti[:, 0:cur],
                                                          scl, None, ALU.mult))
                            ch_v.wait(v, t1)
                            t2 = ch_v.inc(v.tensor_tensor(tf[:, 0:cur], tf[:, 0:cur],
                                                          srep[b][:, 0:cur], ALU.mult))
                            ch_v.wait(v, t2)
                            s_dqv.inc(v.tensor_tensor(wbf[b][:, 0:cur], tf[:, 0:cur],
                                                      zsrep[b][:, 0:cur], ALU.subtract))

                    @blk.scalar
                    def _(sc):
                        s_sto.wait_totals(sc, b_sto)
                        for ci, (g0, gb, c0, cur) in enumerate(chunks):
                            s_dqv.wait(sc, b_dqv + (ci + 1))
                            nt0 = g0 * (width // 128) + c0 // 128
                            nts = cur // 128
                            s_sto.inc(ci, sc.dma_start(
                                bass.AP(odr, nt0 * TS,
                                        [[128, 128], [TS, nts], [1, 128]]),
                                bass.AP(wbf[ci % 2], 0,
                                        [[CW, 128], [128, nts], [1, 128]])))

        def emit_cast(wdr, odr, ncols, name):
            CW = min(ncols * 128, 6144)
            nch = (ncols * 128 + CW - 1) // CW
            with ExitStack() as ps:
                psb = lambda n, sh, dt: ps.enter_context(nc.sbuf_tensor(un(n), sh, dt))
                wf = [psb("wf", [128, CW], F32) for i in range(2)]
                wb = [psb("wb", [128, CW], BF16) for i in range(2)]
                chunks = [(g, ch) for g in range(DKC) for ch in range(nch)]
                with nc.Block() as blk:
                    b_dqv = s_dqv.n
                    b_sto = s_sto.totals()
                    ld_t = {}

                    @blk.sync
                    def _(sy):
                        s_sto.wait_totals(sy, b_sto)
                        for ci, (g, ch) in enumerate(chunks):
                            c0 = ch * CW
                            cur = min(CW, ncols * 128 - c0)
                            if ci >= 2:
                                s_dqv.wait(sy, b_dqv + (ci - 1))
                            ld_t[ci] = s_deq.inc(ci, sy.dma_start(
                                wf[ci % 2][:, 0:cur],
                                wdr[g * 128:(g + 1) * 128, c0:c0 + cur]))

                    @blk.vector
                    def _(v):
                        s_sto.wait_totals(v, b_sto)
                        for ci, (g, ch) in enumerate(chunks):
                            c0 = ch * CW
                            cur = min(CW, ncols * 128 - c0)
                            s_deq.wait(v, ci, ld_t[ci])
                            if ci >= 2:
                                s_sto.wait(v, ci, b_sto[ci % 2] + 16 * (ci // 2))
                            s_dqv.inc(v.tensor_copy(wb[ci % 2][:, 0:cur],
                                                    wf[ci % 2][:, 0:cur]))

                    @blk.scalar
                    def _(sc):
                        s_sto.wait_totals(sc, b_sto)
                        for ci, (g, ch) in enumerate(chunks):
                            c0 = ch * CW
                            cur = min(CW, ncols * 128 - c0)
                            s_dqv.wait(sc, b_dqv + (ci + 1))
                            nt0 = c0 // 128
                            nts = cur // 128
                            s_sto.inc(ci, sc.dma_start(
                                bass.AP(odr, (g * ncols + nt0) * TS,
                                        [[128, 128], [TS, nts], [1, 128]]),
                                bass.AP(wb[ci % 2], 0,
                                        [[CW, 128], [128, nts], [1, 128]])))

        def issue_ag(cci, src_t, dst_t):
            done = s_sto.totals()
            with nc.Block() as blk:
                @blk.gpsimd
                def _(g):
                    s_sto.wait_totals(g, done)
                    s_cc[cci].inc(g.collective_compute(
                        "AllGather", ALU.bypass, replica_groups=ALL8,
                        ins=[src_t.ap().opt()], outs=[dst_t.ap().opt()]))

        emit_dequant_q(gq, gs, zs_g, wg_sl, DKC, NT * 128, 1.0, "g")
        issue_ag(0, wg_sl, wg_all)
        emit_dequant_q(uq, us, zs_u, wu_sl, DKC, NT * 128, 1.0, "u")
        issue_ag(1, wu_sl, wu_all)
        emit_dequant_q(dq, ds, zs_d, wd_sl, NT, DTL * 128, 0.5, "d")
        issue_ag(2, wd_sl, wd_all)

        def emit_casts_late():
            emit_cast(wat, wat_sl, CT, "a")
            issue_ag(3, wat_sl, wat_all)
            emit_cast(wpr, wpr_sl, DK, "p")
            issue_ag(4, wpr_sl, wpr_all)

        def emit_pass1(src, src_wait=None):
            with ExitStack() as ps:
                psb = lambda n, sh, dt: ps.enter_context(nc.sbuf_tensor(un(n), sh, dt))
                xt = [psb("p1x", [128, R], F32) for i in range(2)]
                tb = [psb("p1b", [128, R], BF16) for i in range(2)]
                with nc.Block() as blk, \
                        nc.psum_tensor(un("ps_sumA"), [1, RH], F32) as ps_sumA, \
                        nc.psum_tensor(un("ps_sumB"), [1, RH], F32) as ps_sumB:
                    b_lnv, b_sums, b_bnc = s_lnv.n, s_sums.n, s_bnc.n
                    ld_t = {}

                    @blk.sync
                    def _(sy):
                        if src_wait is not None:
                            s_out.wait_totals(sy, src_wait)
                        for dt in range(DK):
                            if dt >= 2:
                                s_lnv.wait(sy, b_lnv + dt - 1)
                            ld_t[dt] = s_dx.inc(dt, sy.dma_start(
                                xt[dt % 2][:, :], src[dt * 128:(dt + 1) * 128, :]))

                    @blk.vector
                    def _(v):
                        for dt in range(DK):
                            s_dx.wait(v, dt, ld_t[dt])
                            if dt >= 2:
                                s_sums.wait(v, b_sums + dt - 1)
                            s_lnv.inc(v.tensor_copy(tb[dt % 2][:, :], xt[dt % 2][:, :]))
                        s_sums.wait(v, b_sums + DK)
                        t9 = ch_v.inc(v.tensor_scalar(lnrow_a[0:1, 0:RH], ps_sumA[0:1, :],
                                                      -1.0 / D, None, ALU.mult))
                        ch_v.wait(v, t9)
                        s_lnv.inc(v.tensor_scalar(lnrow_a[0:1, RH:R], ps_sumB[0:1, :],
                                                  -1.0 / D, None, ALU.mult))

                    @blk.tensor
                    def _(t):
                        for dt in range(DK):
                            s_lnv.wait(t, b_lnv + dt + 1)
                            t.matmul(ps_sumA[0:1, :], ones_col[:, 0:1],
                                     tb[dt % 2][:, 0:RH], start=(dt == 0),
                                     stop=(dt == DK - 1))
                            s_sums.inc(t.matmul(ps_sumB[0:1, :], ones_col[:, 0:1],
                                                tb[dt % 2][:, RH:R], start=(dt == 0),
                                                stop=(dt == DK - 1)))

                    @blk.scalar
                    def _(sc):
                        s_lnv.wait(sc, b_lnv + DK + 1)
                        ch_v.wait(sc, 0)
                        s_bnc.inc(sc.dma_start(m_bnc[0:1, :], lnrow_a[0:1, :]), 16)
                        s_bnc.wait(sc, b_bnc + 16)
                        s_bnc.inc(sc.dma_start(
                            m_rep[:, :], bass.AP(m_bnc, 0, [[0, 128], [1, R]])), 16)
            return s_bnc.n

        def emit_pass2(src, xb, mrep_t):
            with ExitStack() as ps:
                psb = lambda n, sh, dt: ps.enter_context(nc.sbuf_tensor(un(n), sh, dt))
                xt = [psb("p2x", [128, R], F32) for i in range(2)]
                sq = [psb("p2s", [128, R], BF16) for i in range(2)]
                with nc.Block() as blk, \
                        nc.psum_tensor(un("ps_ssqA"), [1, RH], F32) as ps_ssqA, \
                        nc.psum_tensor(un("ps_ssqB"), [1, RH], F32) as ps_ssqB:
                    b_lnv, b_lna, b_sums = s_lnv.n, s_lna.n, s_sums.n
                    ld_t = {}

                    @blk.sync
                    def _(sy):
                        for dt in range(DK):
                            if dt >= 2:
                                s_lnv.wait(sy, b_lnv + dt - 1)
                            ld_t[dt] = s_dx.inc(dt, sy.dma_start(
                                xt[dt % 2][:, :], src[dt * 128:(dt + 1) * 128, :]))

                    @blk.vector
                    def _(v):
                        s_bnc.wait(v, mrep_t)
                        for dt in range(DK):
                            s_dx.wait(v, dt, ld_t[dt])
                            s_lnv.inc(v.tensor_tensor(xb[:, dt * R:(dt + 1) * R],
                                                      xt[dt % 2][:, :], m_rep[:, :],
                                                      ALU.add))
                        s_sums.wait(v, b_sums + DK)
                        t9 = ch_v.inc(v.tensor_scalar(lnrow_a[0:1, 0:RH], ps_ssqA[0:1, :],
                                                      1.0 / D, EPS, ALU.mult, ALU.add))
                        ch_v.wait(v, t9)
                        s_lnv.inc(v.tensor_scalar(lnrow_a[0:1, RH:R], ps_ssqB[0:1, :],
                                                  1.0 / D, EPS, ALU.mult, ALU.add))

                    @blk.scalar
                    def _(sc):
                        for dt in range(DK):
                            s_lnv.wait(sc, b_lnv + dt + 1)
                            if dt >= 2:
                                s_sums.wait(sc, b_sums + dt - 1)
                            s_lna.inc(sc.activation(sq[dt % 2][:, :],
                                                    xb[:, dt * R:(dt + 1) * R],
                                                    AF.Square))
                        s_lnv.wait(sc, b_lnv + DK + 1)
                        s_lna.inc(sc.activation(lnrow_b[0:1, :], lnrow_a[0:1, :],
                                                AF.Sqrt))

                    @blk.tensor
                    def _(t):
                        for dt in range(DK):
                            s_lna.wait(t, b_lna + dt + 1)
                            t.matmul(ps_ssqA[0:1, :], ones_col[:, 0:1],
                                     sq[dt % 2][:, 0:RH], start=(dt == 0),
                                     stop=(dt == DK - 1))
                            s_sums.inc(t.matmul(ps_ssqB[0:1, :], ones_col[:, 0:1],
                                                sq[dt % 2][:, RH:R], start=(dt == 0),
                                                stop=(dt == DK - 1)))

                with nc.Block() as blk:
                    b_lnv2, b_bnc2 = s_lnv.n, s_bnc.n

                    @blk.vector
                    def _(v):
                        s_lna.wait(v)
                        s_lnv.inc(v.reciprocal(lnrow_a[0:1, :], lnrow_b[0:1, :]))

                    @blk.scalar
                    def _(sc):
                        s_lnv.wait(sc, b_lnv2 + 1)
                        s_bnc.inc(sc.dma_start(r_bnc[0:1, :], lnrow_a[0:1, :]), 16)
                        s_bnc.wait(sc, b_bnc2 + 16)
                        s_bnc.inc(sc.dma_start(
                            r_rep[:, :], bass.AP(r_bnc, 0, [[0, 128], [1, R]])), 16)

                rrep_t = s_bnc.n
                with nc.Block() as blk:
                    @blk.vector
                    def _(v):
                        s_bnc.wait(v, rrep_t)
                        for dt in range(DK):
                            v.tensor_tensor(xb[:, dt * R:(dt + 1) * R],
                                            xb[:, dt * R:(dt + 1) * R],
                                            r_rep[:, :], ALU.mult)
            return s_bnc.n

        def emit_final(src, dst, src_wait=None):
            with ExitStack() as ps:
                psb = lambda n, sh, dt: ps.enter_context(nc.sbuf_tensor(un(n), sh, dt))
                xt = [psb("fpx", [128, R], F32) for i in range(2)]
                to = [psb("fpo", [128, R], F32) for i in range(2)]
                with nc.Block() as blk:
                    b_fp = s_fp.n
                    b_out = s_out.totals()
                    ld_t = {}

                    @blk.sync
                    def _(sy):
                        if src_wait is not None:
                            s_out.wait_totals(sy, src_wait)
                        for dt in range(DK):
                            if dt >= 2:
                                s_fp.wait(sy, b_fp + dt - 1)
                            ld_t[dt] = s_dx.inc(dt, sy.dma_start(
                                xt[dt % 2][:, :], src[dt * 128:(dt + 1) * 128, :]))
                            if dt >= 1:
                                s_fp.wait(sy, b_fp + dt)
                                s_out.inc(dt - 1, sy.dma_start(
                                    dst[(dt - 1) * 128:dt * 128, :],
                                    to[(dt - 1) % 2][:, :]))
                        s_fp.wait(sy, b_fp + DK)
                        s_out.inc(DK - 1, sy.dma_start(
                            dst[(DK - 1) * 128:DK * 128, :], to[(DK - 1) % 2][:, :]))

                    @blk.vector
                    def _(v):
                        for dt in range(DK):
                            s_dx.wait(v, dt, ld_t[dt])
                            if dt >= 2:
                                s_out.wait(v, dt, b_out[dt % 2] + 16 * (dt // 2))
                            s_fp.inc(v.tensor_tensor(to[dt % 2][:, :], xt[dt % 2][:, :],
                                                     acc[:, dt * R:(dt + 1) * R],
                                                     ALU.add))
            return s_out.totals()

        def emit_ffn(xb, rrep_t):
            with ExitStack() as ps:
                psb = lambda n, sh, dt: ps.enter_context(nc.sbuf_tensor(un(n), sh, dt))
                gsl = [psb("gsl", [128, DK * 128], BF16) for i in range(2)]
                usl = [psb("usl", [128, DK * 128], BF16) for i in range(2)]
                dsl = [psb("dsl", [128, SNT * DPB * 128], BF16) for i in range(2)]
                hstg = [psb("hst", [128, SNT * R], BF16) for i in range(2)]
                sil = [psb("sil", [128, R], BF16) for i in range(2)]
                tmp_u = [psb("tmu", [128, R], BF16) for i in range(2)]
                tmp_d = [psb("tmd", [128, R], BF16) for i in range(2)]
                with ExitStack() as pps, nc.Block() as blk:
                    P_gA = pps.enter_context(nc.psum_tensor(un("P_gA"), [128, RH], F32))
                    P_gB = pps.enter_context(nc.psum_tensor(un("P_gB"), [128, RH], F32))
                    P_uA = pps.enter_context(nc.psum_tensor(un("P_uA"), [128, RH], F32))
                    P_uB = pps.enter_context(nc.psum_tensor(un("P_uB"), [128, RH], F32))
                    P_dA = [pps.enter_context(nc.psum_tensor(un("P_dA"), [128, RH], F32))
                            for i in range(2)]
                    P_dB = [pps.enter_context(nc.psum_tensor(un("P_dB"), [128, RH], F32))
                            for i in range(2)]
                    bs = dict(g=s_g.n, u=s_u.n, eg=s_eg.n, eu=s_eu.n, h=s_h.n,
                              sil=s_sil.n, dmm=s_dmm.n, dev=s_dev.n, dac=s_dac.n)
                    w_t, wd_t = {}, {}

                    @blk.sync
                    def _(sy):
                        s_cc[0].wait(sy, 1)

                        def down_slabs(stg):
                            for b in range(DBLK):
                                j2 = stg * DBLK + b
                                if j2 == 0:
                                    s_cc[2].wait(sy, 1)
                                if j2 >= 2:
                                    s_dmm.wait(sy, bs["dmm"] + DPB * (j2 - 1))
                                for cc in range(CPB):
                                    c = b * CPB + cc
                                    wd_t[j2] = s_dwd.inc(j2, sy.dma_start(
                                        bass.AP(dsl[j2 % 2], cc * SNT * DTL * 128,
                                                [[SNT * DPB * 128, 128],
                                                 [128, SNT * DTL], [1, 128]]),
                                        bass.AP(wd_all, ((c * NT + stg * SNT) * DTL) * TS,
                                                [[128, 128], [TS, SNT * DTL], [1, 128]])))

                        for stg in range(NSTG):
                            for nt in range(stg * SNT, (stg + 1) * SNT):
                                if nt >= 2:
                                    s_g.wait(sy, bs["g"] + nt - 1)
                                    s_u.wait(sy, bs["u"] + nt - 1)
                                for hf in range(2):
                                    s_dw.inc(nt, sy.dma_start(
                                        bass.AP(gsl[nt % 2], hf * KCH * 128,
                                                [[DK * 128, 128], [128, KCH], [1, 128]]),
                                        bass.AP(wg_all, (hf * KCH * NT + nt) * TS,
                                                [[128, 128], [NT * TS, KCH], [1, 128]])))
                                if nt == 0:
                                    s_cc[1].wait(sy, 1)
                                for hf in range(2):
                                    w_t[nt] = s_dw.inc(nt, sy.dma_start(
                                        bass.AP(usl[nt % 2], hf * KCH * 128,
                                                [[DK * 128, 128], [128, KCH], [1, 128]]),
                                        bass.AP(wu_all, (hf * KCH * NT + nt) * TS,
                                                [[128, 128], [NT * TS, KCH], [1, 128]])))
                            if stg >= 1:
                                down_slabs(stg - 1)
                        down_slabs(NSTG - 1)

                    @blk.tensor
                    def _(t):
                        jj = [0]

                        def down_groups(stg):
                            for b in range(DBLK):
                                j2 = stg * DBLK + b
                                for dtb in range(DPB):
                                    j = jj[0]
                                    s_dwd.wait(t, j2, wd_t[j2])
                                    s_h.wait(t, bs["h"] + (stg + 1) * SNT)
                                    if j >= 2:
                                        s_dev.wait(t, bs["dev"] + 2 * (j - 1))
                                    ins = None
                                    for il in range(SNT):
                                        dcol = ((dtb // DTL) * SNT * DTL
                                                + il * DTL + (dtb % DTL))
                                        lhsT = dsl[j2 % 2][:, dcol * 128:
                                                           (dcol + 1) * 128]
                                        t.matmul(P_dA[j % 2][:, :], lhsT,
                                                 hstg[stg % 2][:, il * R: il * R + RH],
                                                 start=(il == 0), stop=(il == SNT - 1))
                                        ins = t.matmul(
                                            P_dB[j % 2][:, :], lhsT,
                                            hstg[stg % 2][:, il * R + RH:(il + 1) * R],
                                            start=(il == 0), stop=(il == SNT - 1))
                                    s_dmm.inc(ins)
                                    jj[0] += 1

                        for stg in range(NSTG):
                            for nt in range(stg * SNT, (stg + 1) * SNT):
                                s_dw.wait(t, nt, w_t[nt])
                                if nt >= 1:
                                    s_sil.wait(t, bs["sil"] + 2 * nt)
                                ins = None
                                for kc in range(DK):
                                    lhsT = gsl[nt % 2][:, kc * 128:(kc + 1) * 128]
                                    t.matmul(P_gA[:, :], lhsT,
                                             xb[:, kc * R: kc * R + RH],
                                             start=(kc == 0), stop=(kc == DK - 1))
                                    ins = t.matmul(P_gB[:, :], lhsT,
                                                   xb[:, kc * R + RH:(kc + 1) * R],
                                                   start=(kc == 0), stop=(kc == DK - 1))
                                s_g.inc(ins)
                                if nt >= 1:
                                    s_eu.wait(t, bs["eu"] + 2 * nt)
                                for kc in range(DK):
                                    lhsT = usl[nt % 2][:, kc * 128:(kc + 1) * 128]
                                    t.matmul(P_uA[:, :], lhsT,
                                             xb[:, kc * R: kc * R + RH],
                                             start=(kc == 0), stop=(kc == DK - 1))
                                    ins = t.matmul(P_uB[:, :], lhsT,
                                                   xb[:, kc * R + RH:(kc + 1) * R],
                                                   start=(kc == 0), stop=(kc == DK - 1))
                                s_u.inc(ins)
                            if stg >= 1:
                                down_groups(stg - 1)
                        down_groups(NSTG - 1)

                    @blk.scalar
                    def _(sc):
                        jd = [0]

                        def devacs(stg):
                            for b in range(DBLK):
                                for dtb in range(DPB):
                                    j = jd[0]
                                    s_dmm.wait(sc, bs["dmm"] + j + 1)
                                    if j >= 2:
                                        s_dac.wait(sc, bs["dac"] + j - 1)
                                    s_dev.inc(sc.activation(tmp_d[j % 2][:, 0:RH],
                                                            P_dA[j % 2][:, :], AF.Copy))
                                    s_dev.inc(sc.activation(tmp_d[j % 2][:, RH:R],
                                                            P_dB[j % 2][:, :], AF.Copy))
                                    jd[0] += 1

                        for stg in range(NSTG):
                            for nt in range(stg * SNT, (stg + 1) * SNT):
                                s_g.wait(sc, bs["g"] + nt + 1)
                                if nt >= 2:
                                    s_h.wait(sc, bs["h"] + nt - 1)
                                s_sil.inc(sc.activation(sil[nt % 2][:, 0:RH],
                                                        P_gA[:, :], AF.Silu))
                                s_sil.inc(sc.activation(sil[nt % 2][:, RH:R],
                                                        P_gB[:, :], AF.Silu))
                                s_u.wait(sc, bs["u"] + nt + 1)
                                s_eu.inc(sc.activation(tmp_u[nt % 2][:, 0:RH],
                                                       P_uA[:, :], AF.Copy))
                                s_eu.inc(sc.activation(tmp_u[nt % 2][:, RH:R],
                                                       P_uB[:, :], AF.Copy))
                            if stg >= 1:
                                devacs(stg - 1)
                        devacs(NSTG - 1)

                    @blk.vector
                    def _(v):
                        ja = [0]

                        def daccs(stg):
                            for b in range(DBLK):
                                for dtb in range(DPB):
                                    j = ja[0]
                                    gdt = b * DPB + dtb
                                    s_dev.wait(v, bs["dev"] + 2 * (j + 1))
                                    if stg == 0:
                                        ins = v.tensor_copy(acc[:, gdt * R:(gdt + 1) * R],
                                                            tmp_d[j % 2][:, :])
                                    else:
                                        ins = v.tensor_tensor(
                                            acc[:, gdt * R:(gdt + 1) * R],
                                            acc[:, gdt * R:(gdt + 1) * R],
                                            tmp_d[j % 2][:, :], ALU.add)
                                    s_dac.inc(ins)
                                    ja[0] += 1

                        for stg in range(NSTG):
                            for nt in range(stg * SNT, (stg + 1) * SNT):
                                s_sil.wait(v, bs["sil"] + 2 * (nt + 1))
                                s_eu.wait(v, bs["eu"] + 2 * (nt + 1))
                                stg_of = nt // SNT
                                if stg_of >= 2:
                                    s_dmm.wait(v, bs["dmm"] + GPS * (stg_of - 1))
                                il = nt - stg_of * SNT
                                s_h.inc(v.tensor_tensor(
                                    hstg[stg_of % 2][:, il * R:(il + 1) * R],
                                    sil[nt % 2][:, :], tmp_u[nt % 2][:, :], ALU.mult))
                            if stg >= 1:
                                daccs(stg - 1)
                        daccs(NSTG - 1)

        def emit_qkv(xb, rrep_t):
            VCT0 = 2 * DK
            cts = list(range(DK, 2 * DK)) + list(range(DK)) + list(range(2 * DK, CT))
            with ExitStack() as ps:
                psb = lambda n, sh, dt: ps.enter_context(nc.sbuf_tensor(un(n), sh, dt))
                asl = [psb("asl", [128, DK * 128], BF16) for i in range(2)]
                qkt = [psb("qkt", [128, R], BF16) for i in range(2)]
                vti = [psb("vti", [128, R], BF16) for i in range(2)]
                vst = [psb("vst", [128, R], BF16) for i in range(2)]
                with ExitStack() as pps, nc.Block() as blk:
                    PA = [pps.enter_context(nc.psum_tensor(un("PqA"), [128, RH], F32))
                          for i in range(2)]
                    PB = [pps.enter_context(nc.psum_tensor(un("PqB"), [128, RH], F32))
                          for i in range(2)]
                    P_t = pps.enter_context(nc.psum_tensor(un("P_t"), [128, R], BF16))
                    b = dict(q=s_q.n, qe=s_qe.n, e1=s_e1.n, vt=s_vt.n, vs=s_vs.n)
                    kst_t = [None]
                    b_sto = s_sto.totals()
                    kst_t2 = (
                        (b_sto[0] + 16 * ((DK + 1) // 2), b_sto[1] + 16 * (DK // 2)),
                        (s_sv.totals()[0] + 16 * ((DK + 1) // 2),
                         s_sv.totals()[1] + 16 * (DK // 2)),
                    )
                    b_sv = s_sv.totals()
                    wat_t = {}

                    @blk.sync
                    def _(sy):
                        s_cc[3].wait(sy, 1)
                        for pi, ct in enumerate(cts):
                            if pi >= 2:
                                s_q.wait(sy, b["q"] + pi - 1)
                            for hf in range(2):
                                wat_t[pi] = s_dw.inc(pi, sy.dma_start(
                                    bass.AP(asl[pi % 2], hf * KCH * 128,
                                            [[DK * 128, 128], [128, KCH], [1, 128]]),
                                    bass.AP(wat_all, (hf * KCH * CT + ct) * TS,
                                            [[128, 128], [CT * TS, KCH], [1, 128]])))

                    @blk.tensor
                    def _(t):
                        def transposes(vi):
                            vct = VCT0 + vi
                            s_qe.wait(t, b["qe"] + vct + 1)
                            if vi >= 1:
                                s_vs.wait(t, b["vs"] + vi)
                            ins = None
                            for sl in range(SLOTS):
                                ins = t.transpose(P_t[:, sl * 128:(sl + 1) * 128],
                                                  vti[vi % 2][:, sl * 128:(sl + 1) * 128],
                                                  ident[:, :])
                            s_vt.inc(ins)

                        for pi, ct in enumerate(cts):
                            s_dw.wait(t, pi, wat_t[pi])
                            if pi >= 2:
                                s_qe.wait(t, b["qe"] + pi - 1)
                            ins = None
                            for kc in range(DK):
                                lhsT = asl[pi % 2][:, kc * 128:(kc + 1) * 128]
                                t.matmul(PA[pi % 2][:, :], lhsT,
                                         xb[:, kc * R: kc * R + RH],
                                         start=(kc == 0), stop=(kc == DK - 1))
                                ins = t.matmul(PB[pi % 2][:, :], lhsT,
                                               xb[:, kc * R + RH:(kc + 1) * R],
                                               start=(kc == 0), stop=(kc == DK - 1))
                            s_q.inc(ins)
                            if pi > VCT0:
                                transposes(pi - 1 - VCT0)
                        transposes(DK - 1)

                    @blk.vector
                    def _(v):
                        s_bnc.wait(v, rrep_t)

                        def vstrip(vi):
                            s_vt.wait(v, b["vt"] + vi + 1)
                            if vi >= 2:
                                s_sv.wait(v, vi, b_sv[vi % 2] + 16 * (vi // 2))
                            s_vs.inc(v.tensor_copy(vst[vi % 2][:, :], P_t[:, :]))

                        for pi, ct in enumerate(cts):
                            s_q.wait(v, b["q"] + pi + 1)
                            qsc = (1.0 / 128.0) if ct < DK else 1.0
                            if pi < VCT0:
                                if pi >= 2:
                                    s_sto.wait(v, pi, b_sto[pi % 2] + 16 * (pi // 2))
                                dst = qkt[pi % 2]
                            else:
                                vi = pi - VCT0
                                if vi >= 2:
                                    s_vt.wait(v, b["vt"] + vi - 1)
                                dst = vti[vi % 2]
                            t1 = ch_v.inc(v.tensor_scalar(dst[:, 0:RH], PA[pi % 2][:, :],
                                                          ba_sb[:, ct:ct + 1], qsc,
                                                          ALU.add, ALU.mult))
                            ch_v.wait(v, t1)
                            s_qe.inc(v.tensor_scalar(dst[:, RH:R], PB[pi % 2][:, :],
                                                     ba_sb[:, ct:ct + 1], qsc,
                                                     ALU.add, ALU.mult))
                            if pi > VCT0 + 1:
                                vstrip(pi - 2 - VCT0)
                        vstrip(DK - 2)
                        vstrip(DK - 1)

                    @blk.gpsimd
                    def _(g):
                        kk = kst_t2[0]
                        s_sto.wait_totals(g, kk)
                        s_cc[5].inc(g.collective_compute(
                            "AllGather", ALU.bypass, replica_groups=PAIRS,
                            ins=[kt_own.ap().opt()], outs=[kt_all.ap().opt()]))
                        s_sv.wait_totals(g, kst_t2[1])
                        s_cc[6].inc(g.collective_compute(
                            "AllGather", ALU.bypass, replica_groups=PAIRS,
                            ins=[v_own.ap().opt()], outs=[v_all.ap().opt()]))

                    @blk.scalar
                    def _(sc):
                        s_sto.wait_totals(sc, b_sto)
                        s_sv.wait_totals(sc, b_sv)
                        for pi in range(VCT0):
                            ct = cts[pi]
                            s_qe.wait(sc, b["qe"] + pi + 1)
                            dstd = q_own if ct < DK else kt_own
                            r0 = (ct if ct < DK else ct - DK) * 128
                            s_sto.inc(pi, sc.dma_start(dstd[r0:r0 + 128, :],
                                                       qkt[pi % 2][:, :]))
                            if pi == DK - 1:
                                kst_t[0] = s_sto.totals()
                        for vi in range(DK):
                            s_vs.wait(sc, b["vs"] + vi + 1)
                            s_sv.inc(vi, sc.dma_start(
                                bass.AP(v_own, vi * 128,
                                        [[D, 128], [128 * D, SLOTS], [1, 128]]),
                                bass.AP(vst[vi % 2], 0,
                                        [[R, 128], [128, SLOTS], [1, 128]])))
            return kst_t[0], s_sto.totals(), s_sv.totals()

        def emit_proj(attn_sb):
            with ExitStack() as ps:
                psb = lambda n, sh, dt: ps.enter_context(nc.sbuf_tensor(un(n), sh, dt))
                psl = [psb("psl", [128, DK * 128], BF16) for i in range(2)]
                with ExitStack() as pps, nc.Block() as blk:
                    PA = [pps.enter_context(nc.psum_tensor(un("PpA"), [128, RH], F32))
                          for i in range(2)]
                    PB = [pps.enter_context(nc.psum_tensor(un("PpB"), [128, RH], F32))
                          for i in range(2)]
                    b = dict(q=s_q.n, pj=s_pj.n)
                    wpr_t = {}

                    @blk.sync
                    def _(sy):
                        s_cc[4].wait(sy, 1)
                        for ct in range(DK):
                            if ct >= 2:
                                s_q.wait(sy, b["q"] + ct - 1)
                            for hf in range(2):
                                wpr_t[ct] = s_dw.inc(ct, sy.dma_start(
                                    bass.AP(psl[ct % 2], hf * KCH * 128,
                                            [[DK * 128, 128], [128, KCH], [1, 128]]),
                                    bass.AP(wpr_all, (hf * KCH * DK + ct) * TS,
                                            [[128, 128], [DK * TS, KCH], [1, 128]])))

                    @blk.tensor
                    def _(t):
                        for ct in range(DK):
                            s_dw.wait(t, ct, wpr_t[ct])
                            if ct >= 2:
                                s_pj.wait(t, b["pj"] + 2 * (ct - 1))
                            ins = None
                            for kc in range(DK):
                                lhsT = psl[ct % 2][:, kc * 128:(kc + 1) * 128]
                                t.matmul(PA[ct % 2][:, :], lhsT,
                                         attn_sb[:, kc * R: kc * R + RH],
                                         start=(kc == 0), stop=(kc == DK - 1))
                                ins = t.matmul(PB[ct % 2][:, :], lhsT,
                                               attn_sb[:, kc * R + RH:(kc + 1) * R],
                                               start=(kc == 0), stop=(kc == DK - 1))
                            s_q.inc(ins)

                    @blk.scalar
                    def _(sc):
                        for ct in range(DK):
                            s_q.wait(sc, b["q"] + ct + 1)
                            s_pj.inc(sc.activation(acc[:, ct * R: ct * R + RH],
                                                   PA[ct % 2][:, :], AF.Identity,
                                                   bias=bp_sb[:, ct:ct + 1]))
                            s_pj.inc(sc.activation(acc[:, ct * R + RH:(ct + 1) * R],
                                                   PB[ct % 2][:, :], AF.Identity,
                                                   bias=bp_sb[:, ct:ct + 1]))


        with nc.sbuf_tensor(un("xb1"), [128, DK * R], BF16) as xb:
            mrep_t = emit_pass1(xin)
            rrep_t = emit_pass2(xin, xb, mrep_t)
            emit_ffn(xb, rrep_t)
        emit_casts_late()
        fin1_t = emit_final(xin, xres)

        if debug_stop == "ffn1":
            with nc.Block() as blk:
                @blk.sync
                def _(sy):
                    s_out.wait_totals(sy, fin1_t)
            return nc

        with nc.sbuf_tensor(un("xb2"), [128, DK * R], BF16) as xb:
            mrep_t = emit_pass1(xres, src_wait=fin1_t)
            rrep_t = emit_pass2(xres, xb, mrep_t)
            kv_k_t, kv_sto_t, kv_sv_t = emit_qkv(xb, rrep_t)

        with ExitStack() as ascope:
            attn_sb = ascope.enter_context(
                nc.sbuf_tensor(un("attn_sb"), [128, DK * R], BF16))
            inner = ascope.enter_context(ExitStack())
            asb = lambda name, shape, dt: inner.enter_context(
                nc.sbuf_tensor(un(name), shape, dt))
            masks = asb("masks_sb", [128, SLOTS * 2 * 128], BF16)
            qsb = asb("qsb_sb", [1, R], F32)
            kcol = asb("kcol_sb", [128, NQT], F32)
            ktile = [asb("ktl", [128, NQT * 128], BF16) for i in range(2)]
            qtile = [asb("qtl", [128, R], BF16) for i in range(2)]
            vh = [asb("vh", [128, NQT * 128], BF16) for i in range(2)]
            ptile = asb("ptile", [128, NQT * QW], BF16)
            qsrow = [asb("qsrow", [1, R], BF16) for i in range(2)]
            bcols = [asb("bcols", [128, NQT], F32) for i in range(2)]
            rrep_a = [asb("rrep_a", [128, QW], F32) for i in range(2)]
            pvt = [asb("pvt", [128, QW], BF16) for i in range(2)]
            m_all = asb("m_all", [128, H * SLOTS], F32)
            c_bf = asb("c_bf", [128, 1], BF16)
            c_rep = asb("c_rep", [128, 1], F32)
            crow = asb("crow", [1, 1], F32)
            rrow = [asb("rrow", [1, QW], F32) for i in range(2)]

            with ExitStack() as pps, nc.Block() as blk:
                P1 = [pps.enter_context(
                    nc.psum_tensor(un("P1"), [128, NQT * 128], F32)) for i in range(2)]
                b = dict(p1=s_p1.n, m=s_m.n)
                hd_t = {}

                @blk.sync
                def _(sy):
                    s_cc[5].wait(sy, 1)
                    s_sto.wait_totals(sy, kv_sto_t)
                    s_deq.inc(0, sy.dma_start(masks[:, :], masks_in[:, :]))
                    s_deq.inc(1, sy.dma_start(qsb[:, :], qsb_in[:, :]))
                    s_deq.inc(1, sy.dma_start(kcol[:, :], kcol_in[:, :]))
                    for h in range(H):
                        if h >= 2:
                            s_p1.wait(sy, b["p1"] + SLOTS * (h - 1))
                        for pi in range(2):
                            s_dx.inc(h, sy.dma_start(
                                bass.AP(ktile[h % 2], pi * 128,
                                        [[NQT * 128, 128], [256, SLOTS], [1, 128]]),
                                bass.AP(kt_all, (pi * D + h * 128) * R,
                                        [[R, 128], [128, SLOTS], [1, 128]])))
                        hd_t[h] = s_dx.inc(h, sy.dma_start(
                            qtile[h % 2][:, :], q_own[h * 128:(h + 1) * 128, :]))

                @blk.tensor
                def _(t):
                    s_deq.wait_totals(t)
                    for h in range(H):
                        s_dx.wait(t, h, hd_t[h])
                        for s in range(SLOTS):
                            gi = h * SLOTS + s
                            if gi >= 2:
                                s_m.wait(t, b["m"] + gi - 1)
                            w = (2 * s + 2) * 128
                            lhsT = qtile[h % 2][:, s * 128:(s + 1) * 128]
                            ins = None
                            for c0 in range((w + 511) // 512):
                                cw = min(512, w - c0 * 512)
                                ins = t.matmul(P1[gi % 2][:, c0 * 512:c0 * 512 + cw],
                                               lhsT,
                                               ktile[h % 2][:, c0 * 512:c0 * 512 + cw])
                            s_p1.inc(ins)

                @blk.vector
                def _(v):
                    for h in range(H):
                        for s in range(SLOTS):
                            gi = h * SLOTS + s
                            s_p1.wait(v, b["p1"] + gi + 1)
                            w = (2 * s + 2) * 128
                            s_m.inc(v.tensor_reduce(m_all[:, gi:gi + 1],
                                                    P1[gi % 2][:, 0:w], AX.X, ALU.max))

            with ExitStack() as pps, nc.Block() as blk:
                Pc = pps.enter_context(nc.psum_tensor(un("Pc"), [1, 128], BF16))
                b = dict(m=s_m.n, p1=s_p1.n, bnc=s_bnc.n)

                @blk.vector
                def _(v):
                    s_m.wait(v, b["m"])
                    s_m.inc(v.tensor_reduce(c_bf[:, 0:1], m_all[:, :], AX.X, ALU.max))
                    s_p1.wait(v, b["p1"] + 1)
                    s_m.inc(v.tensor_reduce(crow[0:1, 0:1], Pc[0:1, :], AX.X,
                                            ALU.max, negate=True))

                @blk.tensor
                def _(t):
                    s_m.wait(t, b["m"] + 1)
                    s_p1.inc(t.transpose(Pc[0:1, :], c_bf[:, 0:1], ident[:, :]))

                @blk.scalar
                def _(sc):
                    s_m.wait(sc, b["m"] + 2)
                    s_bnc.inc(sc.dma_start(c_bnc[0:1, :], crow[0:1, 0:1]), 16)
                    s_bnc.wait(sc, b["bnc"] + 16)
                    s_bnc.inc(sc.dma_start(c_rep[:, :],
                                           bass.AP(c_bnc, 0, [[0, 128], [1, 1]])), 16)

            with ExitStack() as pps, nc.Block() as blk:
                P2 = [pps.enter_context(nc.psum_tensor(un("P2"), [128, QW], F32))
                      for i in range(4)]
                Ps = [pps.enter_context(nc.psum_tensor(un("Ps"), [1, QW], F32))
                      for i in range(2)]
                Ppv = [pps.enter_context(nc.psum_tensor(un("Ppv"), [128, QW], F32))
                       for i in range(2)]
                b = dict(st=s_st.n, ex=s_ex.n, mk=s_mk.n, os=s_os.n, pv=s_pv.n,
                         rr=s_rr.n, pc=s_pc.n, bnc=s_bnc.n, b2=s_b2.n,
                         ba=s_ba.n, baa=s_baa.n)
                sched = []
                for h in range(H):
                    for qh in range(2):
                        kts = []
                        for kt in range(NQT):
                            smin = kt // 2
                            s0 = max(smin, qh * SPH)
                            s1 = min(SLOTS, (qh + 1) * SPH)
                            if s1 > s0:
                                kts.append((kt, s0, s1))
                        sched.append((h, qh, kts))
                flat = []
                for idx, (h, qh, kts) in enumerate(sched):
                    for ki, (kt, s0, s1) in enumerate(kts):
                        flat.append((idx, ki, kt, s0, s1))
                gpos_of = {(idx, ki): gp for gp, (idx, ki, kt, s0, s1) in enumerate(flat)}
                st_f, ba_f, baa_f, ex_f, mk_f, os_f, pv_f = [], [], [], [], [], [], []
                nst, nba, nbaa, nex, nmk, nos, npv = (b["st"], b["ba"], b["baa"],
                                                      b["ex"], b["mk"], b["os"], b["pv"])
                rr_t, pc_t = {}, {}
                nrr, npc = b["rr"], b["pc"]
                for gp, (idx, ki, kt, s0, s1) in enumerate(flat):
                    h, qh, kts = sched[idx]
                    nst += 1
                    st_f.append(nst)
                    ba_f.append(nba)
                    baa_f.append(nbaa)
                    nex += 1
                    ex_f.append(nex)
                    sm = kt // 2
                    if qh * SPH <= sm < (qh + 1) * SPH:
                        nmk += 1
                    mk_f.append(nmk)
                    nos += 1
                    os_f.append(nos)
                    npv += 1
                    pv_f.append(npv)
                    if ki == len(kts) - 1:
                        nrr += 1
                        rr_t[(idx, "recip")] = nrr
                        npc += 1
                        pc_t[idx] = npc
                        nrr += 1
                        rr_t[(idx, "evac")] = nrr
                pv_last = {}
                os_last = {}
                for gp, (idx, ki, kt, s0, s1) in enumerate(flat):
                    pv_last[idx] = pv_f[gp]
                    os_last[idx] = os_f[gp]
                bc_t, bc_w, hd_t = {}, {}, {}

                @blk.sync
                def _(sy):
                    s_cc[6].wait(sy, 1)
                    for idx, (h, qh, kts) in enumerate(sched):
                        if qh == 0:
                            if h >= 2:
                                s_pv.wait(sy, pv_last[2 * (h - 2) + 1])
                            for pi in range(2):
                                s_dx.inc(h, sy.dma_start(
                                    bass.AP(ktile[h % 2], pi * 128,
                                            [[NQT * 128, 128], [256, SLOTS], [1, 128]]),
                                    bass.AP(kt_all, (pi * D + h * 128) * R,
                                            [[R, 128], [128, SLOTS], [1, 128]])))
                            s_dx.inc(h, sy.dma_start(
                                qtile[h % 2][:, :], q_own[h * 128:(h + 1) * 128, :]))
                            for pi in range(2):
                                hd_t[h] = s_dx.inc(h, sy.dma_start(
                                    bass.AP(vh[h % 2], pi * 128,
                                            [[NQT * 128, 128], [256, SLOTS], [1, 128]]),
                                    bass.AP(v_all, pi * R * D + h * 128,
                                            [[D, 128], [128 * D, SLOTS], [1, 128]])))
                        s_rr.wait(sy, rr_t[(idx, "recip")])
                        bc_w[idx] = s_rb.inc(idx, sy.dma_start(
                            rh_bnc[h:h + 1, qh * QW:(qh + 1) * QW],
                            rrow[idx % 2][0:1, :]))
                        s_rb.wait(sy, idx, bc_w[idx])
                        if idx >= 2:
                            s_rr.wait(sy, rr_t[(idx - 2, "evac")])
                        bc_t[idx] = s_rb.inc(idx, sy.dma_start(
                            rrep_a[idx % 2][:, :],
                            bass.AP(rh_bnc, h * R + qh * QW, [[0, 128], [1, QW]])))

                @blk.tensor
                def _(t):
                    for idx, (h, qh, kts) in enumerate(sched):
                        s_dx.wait(t, h, hd_t[h])
                        s_b2.wait(t, b["b2"] + 2 * (h + 1))
                        for ki, (kt, s0, s1) in enumerate(kts):
                            w = (s1 - s0) * 128
                            gp = gpos_of[(idx, ki)]
                            if gp >= 4:
                                s_ex.wait(t, ex_f[gp - 4])
                            lhsT = ktile[h % 2][:, kt * 128:(kt + 1) * 128]
                            t.matmul(P2[gp % 4][:, 0:w], lhsT,
                                     qtile[h % 2][:, s0 * 128:s1 * 128],
                                     start=True, stop=False)
                            s_st.inc(t.matmul(
                                P2[gp % 4][:, 0:w], ones_row[0:1, :],
                                qsrow[h % 2][0:1, s0 * 128:s1 * 128],
                                start=False, stop=True))
                        for ki, (kt, s0, s1) in enumerate(kts):
                            w = (s1 - s0) * 128
                            gp = gpos_of[(idx, ki)]
                            s_mk.wait(t, mk_f[gp])
                            s_ex.wait(t, ex_f[gp])
                            if ki == 0 and idx >= 2:
                                s_rr.wait(t, rr_t[(idx - 2, "recip")])
                                s_pc.wait(t, pc_t[idx - 2])
                            off = (s0 - qh * SPH) * 128
                            pcol = kt * QW + off
                            s_os.inc(t.matmul(Ps[idx % 2][0:1, off:off + w],
                                              ones_col[:, 0:1],
                                              ptile[:, pcol:pcol + w],
                                              start=(ki == 0),
                                              stop=(ki == len(kts) - 1)))
                            s_pv.inc(t.matmul(Ppv[idx % 2][:, off:off + w],
                                              vh[h % 2][:, kt * 128:(kt + 1) * 128],
                                              ptile[:, pcol:pcol + w],
                                              start=(ki == 0),
                                              stop=(ki == len(kts) - 1)))

                @blk.vector
                def _(v):
                    for idx, (h, qh, kts) in enumerate(sched):
                        if qh == 0:
                            s_bnc.wait(v, b["bnc"])
                            if h >= 2:
                                s_ex.wait(v, ex_f[gpos_of[(2 * (h - 2) + 1,
                                                           len(sched[2 * (h - 2) + 1][2]) - 1)]])
                            s_b2.inc(v.tensor_scalar(qsrow[h % 2][0:1, :], qsb[0:1, :],
                                                     -float(slopes[h]), c_rep[0:1, 0:1],
                                                     ALU.mult, ALU.add))
                            s_b2.inc(v.tensor_scalar(bcols[h % 2][:, :], kcol[:, :],
                                                     float(slopes[h]), None, ALU.mult))
                        for ki, (kt, s0, s1) in enumerate(kts):
                            gp = gpos_of[(idx, ki)]
                            sm = kt // 2
                            if qh * SPH <= sm < (qh + 1) * SPH:
                                s_st.wait(v, st_f[gp])
                                poff = (sm - s0) * 128
                                mcol = (sm * 2 + (kt - 2 * sm)) * 128
                                s_mk.inc(v.tensor_tensor(
                                    P2[gp % 4][:, poff:poff + 128],
                                    P2[gp % 4][:, poff:poff + 128],
                                    masks[:, mcol:mcol + 128], ALU.add))
                        s_os.wait(v, os_last[idx])
                        if idx >= 2:
                            s_rb.wait(v, idx, bc_w[idx - 2])
                        s_rr.inc(v.reciprocal(rrow[idx % 2][0:1, :], Ps[idx % 2][0:1, :]))
                        s_pc.wait(v, pc_t[idx])
                        s_rb.wait(v, idx, bc_t[idx])
                        s_rr.inc(v.tensor_tensor(
                            attn_sb[:, h * R + qh * QW: h * R + (qh + 1) * QW],
                            pvt[idx % 2][:, :], rrep_a[idx % 2][:, :], ALU.mult))

                @blk.scalar
                def _(sc):
                    for idx, (h, qh, kts) in enumerate(sched):
                        for ki, (kt, s0, s1) in enumerate(kts):
                            gp = gpos_of[(idx, ki)]
                            s_st.wait(sc, st_f[gp])
                            s_mk.wait(sc, mk_f[gp])
                            s_b2.wait(sc, b["b2"] + 2 * (h + 1))
                            if idx >= 1 and ki == 0:
                                s_pv.wait(sc, pv_last[idx - 1])
                            w = (s1 - s0) * 128
                            pcol = kt * QW + (s0 - qh * SPH) * 128
                            s_ex.inc(sc.activation(ptile[:, pcol:pcol + w],
                                                   P2[gp % 4][:, 0:w], AF.Exp,
                                                   bias=bcols[h % 2][:, kt:kt + 1]))
                        s_pv.wait(sc, pv_last[idx])
                        if idx >= 2:
                            s_rr.wait(sc, rr_t[(idx - 2, "evac")])
                        s_pc.inc(sc.activation(pvt[idx % 2][:, :], Ppv[idx % 2][:, :],
                                               AF.Copy))

            inner.close()
            emit_proj(attn_sb)

        fin2_t = emit_final(xres, xres)

        with nc.sbuf_tensor(un("xb3"), [128, DK * R], BF16) as xb:
            mrep_t = emit_pass1(xres, src_wait=fin2_t)
            rrep_t = emit_pass2(xres, xb, mrep_t)
            emit_ffn(xb, rrep_t)
        out_t = emit_final(xres, yout)

        with nc.Block() as blk:
            @blk.sync
            def _(sy):
                s_out.wait_totals(sy, out_t)

    return nc


def shard_inputs(cfg, inp):
    D, I, S, B = cfg["D"], cfg["I"], cfg["S"], cfg["B"]
    DK, NT, NQT, SLOTS, R = cfg["DK"], cfg["NT"], cfg["NQT"], cfg["SLOTS"], cfg["R"]
    DKC, DTL, CT, H = cfg["DKC"], cfg["DTL"], cfg["CT"], cfg["H"]
    x = np.asarray(inp["L0_attn_hidden_states"], np.float32)
    gq = np.asarray(inp["L0_ffn_gate_qweight"])
    gs = np.asarray(inp["L0_ffn_gate_scales"], np.float32)
    gz = np.asarray(inp["L0_ffn_gate_qzeros"])
    uq = np.asarray(inp["L0_ffn_up_qweight"])
    us = np.asarray(inp["L0_ffn_up_scales"], np.float32)
    uz = np.asarray(inp["L0_ffn_up_qzeros"])
    dq = np.asarray(inp["L0_ffn_down_qweight"])
    dsc = np.asarray(inp["L0_ffn_down_scales"], np.float32)
    dz = np.asarray(inp["L0_ffn_down_qzeros"])
    wat = np.asarray(inp["L0_attn_w_attn"], np.float32)
    ba = np.asarray(inp["L0_attn_b_attn"], np.float32)
    wpr = np.asarray(inp["L0_attn_w_proj"], np.float32)
    bp = np.asarray(inp["L0_attn_b_proj"], np.float32)

    ident = np.eye(128, dtype=ml_dtypes.bfloat16)
    bac = np.ascontiguousarray(ba.reshape(CT, 128).T)
    bpc = np.ascontiguousarray(bp.reshape(DK, 128).T)

    in_maps = []
    for c in range(NCORES):
        b, p = c // 2, c % 2
        xr = x[b].reshape(NQT, 128, D)[p::2].reshape(R, D)
        xin_c = np.ascontiguousarray(xr.T)
        kp = np.arange(128)[:, None]
        qj = np.arange(128)[None, :]
        mtiles = []
        for s in range(SLOTS):
            for which in range(2):
                d = which - p
                mtiles.append(np.where(kp + d * 128 <= qj, 0.0, -30000.0))
        masks = np.concatenate(mtiles, axis=1).astype(ml_dtypes.bfloat16)
        qsbase = (((np.arange(R) // 128) * 2 + p) * 128 + 64).astype(np.float32)[None, :]
        kcolv = (np.arange(NQT)[None, :] * 128 + np.arange(128)[:, None]).astype(np.float32)
        im = {
            "xin": xin_c,
            "gq": np.ascontiguousarray(gq[c * DKC * 128:(c + 1) * DKC * 128]),
            "gs": np.ascontiguousarray(gs[c * DKC:(c + 1) * DKC]),
            "gz": np.ascontiguousarray(gz[c * DKC:(c + 1) * DKC]),
            "uq": np.ascontiguousarray(uq[c * DKC * 128:(c + 1) * DKC * 128]),
            "us": np.ascontiguousarray(us[c * DKC:(c + 1) * DKC]),
            "uz": np.ascontiguousarray(uz[c * DKC:(c + 1) * DKC]),
            "dq": np.ascontiguousarray(dq[:, c * DTL * 16:(c + 1) * DTL * 16]),
            "ds": np.ascontiguousarray(dsc[:, c * DTL * 128:(c + 1) * DTL * 128]),
            "dz": np.ascontiguousarray(dz[:, c * DTL * 16:(c + 1) * DTL * 16]),
            "wat": np.ascontiguousarray(wat[c * DKC * 128:(c + 1) * DKC * 128]),
            "wpr": np.ascontiguousarray(wpr[c * DKC * 128:(c + 1) * DKC * 128]),
            "bacols": bac,
            "bpcols": bpc,
            "ident_in": ident,
            "masks_in": masks,
            "qsb_in": qsbase,
            "kcol_in": kcolv,
        }
        in_maps.append(im)
    return in_maps


def assemble_output(cfg, results):
    D, S, B = cfg["D"], cfg["S"], cfg["B"]
    NQT, SLOTS = cfg["NQT"], cfg["SLOTS"]
    out = np.empty((B, S, D), np.float32)
    for c in range(NCORES):
        b, p = c // 2, c % 2
        y = np.asarray(results[c]["yout"])
        arr = np.ascontiguousarray(y.T).reshape(SLOTS, 128, D)
        out[b].reshape(NQT, 128, D)[p::2] = arr
    return out


def kernel(**inputs):
    cfg = CFG
    nc = build_program(cfg)
    in_maps = shard_inputs(cfg, inputs)
    res = run_bass_kernel_spmd(nc, in_maps, core_ids=list(range(NCORES)))
    return assemble_output(cfg, res.results)


# version 21
# speedup vs baseline: 1.0548x; 1.0548x over previous
import math
import numpy as np
import ml_dtypes
from contextlib import ExitStack

import concourse.bass as bass
import concourse.mybir as mybir
from concourse.bass_utils import run_bass_kernel_spmd

F32 = mybir.dt.float32
BF16 = mybir.dt.bfloat16
F16 = mybir.dt.float16
I32 = mybir.dt.int32
AF = mybir.ActivationFunctionType
ALU = mybir.AluOpType
AX = mybir.AxisListType

AWQ_ORDER = [0, 4, 1, 5, 2, 6, 3, 7]
NCORES = 8
EPS = 1e-5


def make_cfg(D, I, S, B):
    c = {}
    c["D"], c["I"], c["S"], c["B"] = D, I, S, B
    c["DK"] = D // 128
    c["NT"] = I // 128
    c["H"] = D // 128
    c["NQT"] = S // 128
    c["SLOTS"] = c["NQT"] // 2
    c["R"] = c["SLOTS"] * 128
    c["RH"] = c["R"] // 2
    c["DKC"] = c["DK"] // NCORES
    c["DTL"] = (D // NCORES) // 128
    c["CT"] = 3 * c["DK"]
    c["SNT"] = 4 if c["NT"] % 4 == 0 else 2
    c["NSTG"] = c["NT"] // c["SNT"]
    c["DBLK"] = 4 if (c["DK"] % (4 * c["DTL"]) == 0 and c["DK"] >= 4 * c["DTL"]) else 2
    c["DPB"] = c["DK"] // c["DBLK"]
    c["CPB"] = c["DPB"] // c["DTL"]
    c["KCH"] = c["DK"] // 2
    c["SPH"] = c["SLOTS"] // 2
    c["QW"] = c["SPH"] * 128
    assert c["DK"] % NCORES == 0 and c["NT"] % c["SNT"] == 0
    assert c["SLOTS"] % 2 == 0 and c["DPB"] % c["DTL"] == 0
    return c


CFG = make_cfg(4096, 11008, 2048, 4)


def alibi_slopes(n_heads):
    cp2 = 2 ** math.floor(math.log2(n_heads))
    base = 2 ** (-(2 ** (-(math.log2(cp2) - 3))))
    slopes = base ** np.arange(1, cp2 + 1, dtype=np.float64)
    if cp2 != n_heads:
        extra_base = 2 ** (-(2 ** (-(math.log2(2 * cp2) - 3))))
        n_rem = min(cp2, n_heads - cp2)
        extra = extra_base ** np.arange(1, 1 + 2 * n_rem, 2, dtype=np.float64)
        slopes = np.concatenate([slopes, extra])
    return slopes.astype(np.float32)


class Sem:
    def __init__(self, stack, nc, name):
        self.h = stack.enter_context(nc.semaphore(name))
        self.n = 0

    def inc(self, instr, k=1):
        instr.then_inc(self.h, k)
        self.n += k
        return self.n

    def wait(self, eng, target=None):
        t = self.n if target is None else target
        if t > 0:
            eng.wait_ge(self.h, t)


class SemP:

    def __init__(self, stack, nc, name):
        self.s = [Sem(stack, nc, f"{name}_a"), Sem(stack, nc, f"{name}_b")]

    def inc(self, slot, instr, k=16):
        return self.s[slot % 2].inc(instr, k)

    def wait(self, eng, slot, target):
        self.s[slot % 2].wait(eng, target)

    def totals(self):
        return (self.s[0].n, self.s[1].n)

    def wait_totals(self, eng, t=None):
        if t is None:
            self.s[0].wait(eng)
            self.s[1].wait(eng)
        else:
            self.s[0].wait(eng, t[0])
            self.s[1].wait(eng, t[1])


def build_program(cfg, debug_stop=None):
    D, I, S = cfg["D"], cfg["I"], cfg["S"]
    DK, NT, H, NQT = cfg["DK"], cfg["NT"], cfg["H"], cfg["NQT"]
    SLOTS, R, RH = cfg["SLOTS"], cfg["R"], cfg["RH"]
    DKC, DTL, CT = cfg["DKC"], cfg["DTL"], cfg["CT"]
    SNT, NSTG, DBLK, DPB, CPB = cfg["SNT"], cfg["NSTG"], cfg["DBLK"], cfg["DPB"], cfg["CPB"]
    KCH, SPH, QW = cfg["KCH"], cfg["SPH"], cfg["QW"]
    GPS = DBLK * DPB
    slopes = alibi_slopes(H)
    TS = 128 * 128

    nc = bass.Bass("TRN2", target_bir_lowering=False, dynamic_dma_scratch_size=4096)

    xin = nc.dram_tensor("xin", [D, R], F32, kind="ExternalInput")
    gq = nc.dram_tensor("gq", [DKC * 128, NT * 16], I32, kind="ExternalInput")
    gs = nc.dram_tensor("gs", [DKC, NT * 128], F32, kind="ExternalInput")
    gz = nc.dram_tensor("gz", [DKC, NT * 16], I32, kind="ExternalInput")
    uq = nc.dram_tensor("uq", [DKC * 128, NT * 16], I32, kind="ExternalInput")
    us = nc.dram_tensor("us", [DKC, NT * 128], F32, kind="ExternalInput")
    uz = nc.dram_tensor("uz", [DKC, NT * 16], I32, kind="ExternalInput")
    dq = nc.dram_tensor("dq", [I, DTL * 16], I32, kind="ExternalInput")
    ds = nc.dram_tensor("ds", [NT, DTL * 128], F32, kind="ExternalInput")
    dz = nc.dram_tensor("dz", [NT, DTL * 16], I32, kind="ExternalInput")
    wat = nc.dram_tensor("wat", [DKC * 128, 3 * D], F32, kind="ExternalInput")
    wpr = nc.dram_tensor("wpr", [DKC * 128, D], F32, kind="ExternalInput")
    bacols = nc.dram_tensor("bacols", [128, CT], F32, kind="ExternalInput")
    bpcols = nc.dram_tensor("bpcols", [128, DK], F32, kind="ExternalInput")
    ident_in = nc.dram_tensor("ident_in", [128, 128], BF16, kind="ExternalInput")
    masks_in = nc.dram_tensor("masks_in", [128, SLOTS * 2 * 128], BF16, kind="ExternalInput")
    qsb_in = nc.dram_tensor("qsb_in", [1, R], F32, kind="ExternalInput")
    kcol_in = nc.dram_tensor("kcol_in", [128, NQT], F32, kind="ExternalInput")

    yout = nc.dram_tensor("yout", [D, R], F32, kind="ExternalOutput")

    xres = nc.dram_tensor("xres", [D, R], F32)
    wg_sl = nc.dram_tensor("wg_sl", [DKC * NT * 128, 128], BF16)
    wg_all = nc.dram_tensor("wg_all", [DK * NT * 128, 128], BF16, addr_space="Shared")
    wu_sl = nc.dram_tensor("wu_sl", [DKC * NT * 128, 128], BF16)
    wu_all = nc.dram_tensor("wu_all", [DK * NT * 128, 128], BF16, addr_space="Shared")
    wd_sl = nc.dram_tensor("wd_sl", [NT * DTL * 128, 128], BF16)
    wd_all = nc.dram_tensor("wd_all", [NCORES * NT * DTL * 128, 128], BF16, addr_space="Shared")
    wat_sl = nc.dram_tensor("wat_sl", [DKC * CT * 128, 128], BF16)
    wat_all = nc.dram_tensor("wat_all", [DK * CT * 128, 128], BF16, addr_space="Shared")
    wpr_sl = nc.dram_tensor("wpr_sl", [DKC * DK * 128, 128], BF16)
    wpr_all = nc.dram_tensor("wpr_all", [DK * DK * 128, 128], BF16, addr_space="Shared")
    zs_g = nc.dram_tensor("zs_g", [DKC, NT * 128], F32)
    zs_u = nc.dram_tensor("zs_u", [DKC, NT * 128], F32)
    zs_d = nc.dram_tensor("zs_d", [NT, DTL * 128], F32)
    q_own = nc.dram_tensor("q_own", [D, R], BF16)
    kt_own = nc.dram_tensor("kt_own", [D, R], BF16)
    kt_all = nc.dram_tensor("kt_all", [2 * D, R], BF16)
    v_own = nc.dram_tensor("v_own", [R, D], BF16)
    v_all = nc.dram_tensor("v_all", [2 * R, D], BF16)
    m_bnc = nc.dram_tensor("m_bnc", [1, R], F32)
    r_bnc = nc.dram_tensor("r_bnc", [1, R], F32)
    c_bnc = nc.dram_tensor("c_bnc", [1, 1], F32)
    rh_bnc = nc.dram_tensor("rh_bnc", [H, R], F32)

    PAIRS = [[0, 1], [2, 3], [4, 5], [6, 7]]
    ALL8 = [list(range(NCORES))]

    with ExitStack() as st:
        _uid = [0]

        def un(n):
            _uid[0] += 1
            return f"{n}_{_uid[0]}"

        sb = lambda name, shape, dt: st.enter_context(nc.sbuf_tensor(name, shape, dt))
        sem = lambda name: Sem(st, nc, name)
        semp = lambda name: SemP(st, nc, name)

        acc = sb("acc", [128, DK * R], F16)
        m_rep = sb("m_rep", [128, R], F32)
        r_rep = sb("r_rep", [128, R], F32)
        ident = sb("ident_sb", [128, 128], BF16)
        ones_col = sb("ones_col", [128, 1], BF16)
        ones_row = sb("ones_row", [1, 128], BF16)
        ba_sb = sb("ba_sb", [128, CT], F32)
        bp_sb = sb("bp_sb", [128, DK], F32)
        lnrow_a = sb("lnrow_a", [1, R], F32)
        lnrow_b = sb("lnrow_b", [1, R], F32)

        s_deq = semp("s_deq")
        s_dx = semp("s_dx")
        s_dw = semp("s_dw")
        s_dwd = semp("s_dwd")
        s_out = semp("s_out")
        s_rb = semp("s_rb")
        s_sto = semp("s_sto")
        s_sv = semp("s_sv")
        s_zs = semp("s_zs")
        s_cld = semp("s_cld")
        s_pst = semp("s_pst")
        s_bnc = sem("s_bnc")
        s_g = sem("s_g"); s_u = sem("s_u"); s_eg = sem("s_eg"); s_eu = sem("s_eu")
        s_h = sem("s_h"); s_sil = sem("s_sil"); s_dmm = sem("s_dmm")
        s_dev = sem("s_dev"); s_dac = sem("s_dac"); s_sums = sem("s_sums")
        s_lnv = sem("s_lnv"); s_lna = sem("s_lna"); s_fp = sem("s_fp")
        s_q = sem("s_q"); s_qe = sem("s_qe"); s_e1 = sem("s_e1")
        s_vt = sem("s_vt"); s_vs = sem("s_vs"); s_pj = sem("s_pj")
        s_p1 = sem("s_p1"); s_m = sem("s_m"); s_st = sem("s_st")
        s_ex = sem("s_ex"); s_mk = sem("s_mk"); s_os = sem("s_os")
        s_pv = sem("s_pv"); s_rr = sem("s_rr"); s_pc = sem("s_pc")
        s_b2 = sem("s_b2"); s_dqv = sem("s_dqv"); ch_v = sem("ch_v")
        s_ba = sem("s_ba"); s_baa = sem("s_baa"); s_cpa = sem("s_cpa")
        s_cc = [sem(f"s_cc{i}") for i in range(8)]

        with nc.Block() as blk:
            @blk.sync
            def _(sy):
                s_deq.inc(0, sy.dma_start(ident[:, :], ident_in[:, :]))
                s_deq.inc(0, sy.dma_start(ba_sb[:, :], bacols[:, :]))
                s_deq.inc(0, sy.dma_start(bp_sb[:, :], bpcols[:, :]))

            @blk.vector
            def _(v):
                s_deq.wait_totals(v)
                v.memset(ones_col[:, :], 1.0)
                v.memset(ones_row[:, :], 1.0)

        zs_specs = [(gz, gs, zs_g, DKC, NT * 128, 1.0),
                    (uz, us, zs_u, DKC, NT * 128, 1.0),
                    (dz, ds, zs_d, NT, DTL * 128, 0.5)]
        with ExitStack() as ps:
            psb = lambda n, sh, dt: ps.enter_context(nc.sbuf_tensor(un(n), sh, dt))
            W0 = min(2752, max(NT * 128, DTL * 128))
            srow = [psb("srow", [1, W0], F32) for i in range(2)]
            zrow = [psb("zrow", [1, W0 // 8], I32) for i in range(2)]
            zi = psb("zi", [1, W0], I32)
            zf = psb("zf", [1, W0], F32)
            zs = [psb("zs", [1, W0], F32) for i in range(2)]
            groups = []
            for (zdr, sdr, odr, ngr, w, scl) in zs_specs:
                GB = max(1, min(ngr, W0 // w))
                for g0 in range(0, ngr, GB):
                    gb = min(GB, ngr - g0)
                    wtot = gb * w
                    for c0 in range(0, wtot, W0):
                        groups.append((zdr, sdr, odr, g0, gb, w, c0,
                                       min(W0, wtot - c0), scl))
            with nc.Block() as blk:
                b_dqv = s_dqv.n
                b_zs = s_zs.totals()
                ld_t = {}
                zst_t = {}

                @blk.sync
                def _(sy):
                    for gi, (zdr, sdr, odr, g0, gb, w, c0, cur, scl) in enumerate(groups):
                        if gi >= 2:
                            s_dqv.wait(sy, b_dqv + (gi - 1))
                        s_deq.inc(gi, sy.dma_start(
                            srow[gi % 2][0:1, 0:cur],
                            bass.AP(sdr, g0 * w + c0, [[1, 1], [1, cur]])))
                        ld_t[gi] = s_deq.inc(gi, sy.dma_start(
                            zrow[gi % 2][0:1, 0:cur // 8],
                            bass.AP(zdr, (g0 * w + c0) // 8, [[1, 1], [1, cur // 8]])))

                @blk.vector
                def _(v):
                    for gi, (zdr, sdr, odr, g0, gb, w, c0, cur, scl) in enumerate(groups):
                        s_deq.wait(v, gi, ld_t[gi])
                        if gi >= 2:
                            s_zs.wait(v, gi, b_zs[gi % 2] + 16 * ((gi - 2) // 2 + 1))
                        b = gi % 2
                        t0 = 0
                        for pos in range(8):
                            sh = 4 * AWQ_ORDER[pos]
                            t0 = ch_v.inc(v.tensor_scalar(
                                bass.AP(zi, pos, [[W0, 1], [8, cur // 8]]),
                                zrow[b][0:1, 0:cur // 8], sh, 15,
                                ALU.logical_shift_right, ALU.bitwise_and))
                        ch_v.wait(v, t0)
                        t1 = ch_v.inc(v.tensor_copy(zf[0:1, 0:cur], zi[0:1, 0:cur]))
                        ch_v.wait(v, t1)
                        ins = v.tensor_tensor(zs[b][0:1, 0:cur], zf[0:1, 0:cur],
                                              srow[b][0:1, 0:cur], ALU.mult)
                        if scl != 1.0:
                            t2 = ch_v.inc(ins)
                            ch_v.wait(v, t2)
                            ins = v.tensor_scalar(zs[b][0:1, 0:cur], zs[b][0:1, 0:cur],
                                                  scl, None, ALU.mult)
                        s_dqv.inc(ins)

                @blk.scalar
                def _(sc):
                    for gi, (zdr, sdr, odr, g0, gb, w, c0, cur, scl) in enumerate(groups):
                        s_dqv.wait(sc, b_dqv + (gi + 1))
                        zst_t[gi] = s_zs.inc(gi, sc.dma_start(
                            bass.AP(odr, g0 * w + c0, [[1, 1], [1, cur]]),
                            zs[gi % 2][0:1, 0:cur]))
        zs_total = s_zs.totals()

        def emit_dequant_q(qdr, sdr, zsdr, odr, ngroups, width, scl, name,
                           want_casts=False):
            GB = max(1, min(ngroups, 2048 // width)) if width <= 1024 else 1
            CW = min(width * GB, 2560)
            with ExitStack() as ps:
                psb = lambda n, sh, dt: ps.enter_context(nc.sbuf_tensor(un(n), sh, dt))
                qwb = [psb("qw", [128, CW // 8], I32) for i in range(2)]
                srep = [psb("sr", [128, CW], F32) for i in range(2)]
                zsrep = [psb("zr", [128, CW], F32) for i in range(2)]
                ti = psb("ti", [128, CW], I32)
                tf = psb("tf", [128, CW], F32)
                wbf = [psb("wb", [128, CW], BF16) for i in range(2)]
                chunks = []
                for g0 in range(0, ngroups, GB):
                    gb = min(GB, ngroups - g0)
                    wtot = gb * width
                    for c0 in range(0, wtot, CW):
                        chunks.append((g0, gb, c0, min(CW, wtot - c0)))
                if want_casts:
                    cbase = cast_cur[0]
                    ctake = min(len(chunks), NCAST - cbase)
                    cast_cur[0] += ctake
                else:
                    cbase, ctake = 0, 0
                with nc.Block() as blk:
                    b_dqv = s_dqv.n
                    b_sto = s_sto.totals()
                    ld_t = {}

                    @blk.sync
                    def _(sy):
                        s_zs.wait_totals(sy, zs_total)
                        s_sto.wait_totals(sy, b_sto)
                        for ci, (g0, gb, c0, cur) in enumerate(chunks):
                            ct = cbase + ci
                            if ci < ctake:
                                (cwdr, codr, cgg, cnc, cc0, ccur) = cast_chunks[ct]
                                if ct >= 2:
                                    s_cpa.wait(sy, ct - 1)
                                cld_t[ct] = s_cld.inc(ct, sy.dma_start(
                                    pcf[ct % 2][:, 0:ccur],
                                    cwdr[cgg * 128:(cgg + 1) * 128,
                                         cc0:cc0 + ccur]))
                            if ci >= 2:
                                s_dqv.wait(sy, b_dqv + (ci - 1))
                            if gb > 1:
                                s_deq.inc(ci, sy.dma_start(
                                    bass.AP(qwb[ci % 2], 0,
                                            [[CW // 8, 128], [width // 8, gb],
                                             [1, width // 8]]),
                                    bass.AP(qdr, g0 * 128 * (width // 8),
                                            [[width // 8, 128],
                                             [128 * (width // 8), gb],
                                             [1, width // 8]])))
                            else:
                                s_deq.inc(ci, sy.dma_start(
                                    qwb[ci % 2][:, 0:cur // 8],
                                    bass.AP(qdr, g0 * 128 * (width // 8) + c0 // 8,
                                            [[width // 8, 128], [1, cur // 8]])))
                            s_deq.inc(ci, sy.dma_start(
                                srep[ci % 2][:, 0:cur],
                                bass.AP(sdr, g0 * width + c0, [[0, 128], [1, cur]])))
                            ld_t[ci] = s_deq.inc(ci, sy.dma_start(
                                zsrep[ci % 2][:, 0:cur],
                                bass.AP(zsdr, g0 * width + c0, [[0, 128], [1, cur]])))

                    @blk.vector
                    def _(v):
                        s_sto.wait_totals(v, b_sto)
                        for ci, (g0, gb, c0, cur) in enumerate(chunks):
                            s_deq.wait(v, ci, ld_t[ci])
                            if ci >= 2:
                                s_sto.wait(v, ci, b_sto[ci % 2] + 16 * (ci // 2))
                            b = ci % 2
                            t0 = 0
                            for pos in range(8):
                                sh = 4 * AWQ_ORDER[pos]
                                t0 = ch_v.inc(v.tensor_scalar(
                                    bass.AP(ti, pos, [[CW, 128], [8, cur // 8]]),
                                    qwb[b][:, 0:cur // 8], sh, 15,
                                    ALU.logical_shift_right, ALU.bitwise_and))
                            ch_v.wait(v, t0)
                            t1 = ch_v.inc(v.tensor_scalar(tf[:, 0:cur], ti[:, 0:cur],
                                                          scl, None, ALU.mult))
                            ch_v.wait(v, t1)
                            t2 = ch_v.inc(v.tensor_tensor(tf[:, 0:cur], tf[:, 0:cur],
                                                          srep[b][:, 0:cur], ALU.mult))
                            ch_v.wait(v, t2)
                            s_dqv.inc(v.tensor_tensor(wbf[b][:, 0:cur], tf[:, 0:cur],
                                                      zsrep[b][:, 0:cur], ALU.subtract))

                    @blk.scalar
                    def _(sc):
                        s_sto.wait_totals(sc, b_sto)
                        for ci, (g0, gb, c0, cur) in enumerate(chunks):
                            s_dqv.wait(sc, b_dqv + (ci + 1))
                            nt0 = g0 * (width // 128) + c0 // 128
                            nts = cur // 128
                            s_sto.inc(ci, sc.dma_start(
                                bass.AP(odr, nt0 * TS,
                                        [[128, 128], [TS, nts], [1, 128]]),
                                bass.AP(wbf[ci % 2], 0,
                                        [[CW, 128], [128, nts], [1, 128]])))
                            ct = cbase + ci
                            if ci < ctake:
                                (cwdr, codr, cgg, cnc, cc0, ccur) = cast_chunks[ct]
                                s_cld.wait(sc, ct, cld_t[ct])
                                if ct >= 2:
                                    s_pst.wait(sc, ct, pst_t[ct - 2])
                                t9 = s_cpa.inc(sc.activation(
                                    pcb[ct % 2][:, 0:ccur],
                                    pcf[ct % 2][:, 0:ccur], AF.Copy))
                                s_cpa.wait(sc, t9)
                                pst_t[ct] = s_pst.inc(ct, sc.dma_start(
                                    bass.AP(codr,
                                            (cgg * cnc + cc0 // 128) * TS,
                                            [[128, 128], [TS, ccur // 128],
                                             [1, 128]]),
                                    bass.AP(pcb[ct % 2], 0,
                                            [[CWP, 128], [128, ccur // 128],
                                             [1, 128]])))

        def emit_cast(wdr, odr, ncols, name):
            CW = min(ncols * 128, 6144)
            nch = (ncols * 128 + CW - 1) // CW
            with ExitStack() as ps:
                psb = lambda n, sh, dt: ps.enter_context(nc.sbuf_tensor(un(n), sh, dt))
                wf = [psb("wf", [128, CW], F32) for i in range(2)]
                wb = [psb("wb", [128, CW], BF16) for i in range(2)]
                chunks = [(g, ch) for g in range(DKC) for ch in range(nch)]
                with nc.Block() as blk:
                    b_dqv = s_dqv.n
                    b_sto = s_sto.totals()
                    ld_t = {}

                    @blk.sync
                    def _(sy):
                        s_sto.wait_totals(sy, b_sto)
                        for ci, (g, ch) in enumerate(chunks):
                            c0 = ch * CW
                            cur = min(CW, ncols * 128 - c0)
                            if ci >= 2:
                                s_dqv.wait(sy, b_dqv + (ci - 1))
                            ld_t[ci] = s_deq.inc(ci, sy.dma_start(
                                wf[ci % 2][:, 0:cur],
                                wdr[g * 128:(g + 1) * 128, c0:c0 + cur]))

                    @blk.vector
                    def _(v):
                        s_sto.wait_totals(v, b_sto)
                        for ci, (g, ch) in enumerate(chunks):
                            c0 = ch * CW
                            cur = min(CW, ncols * 128 - c0)
                            s_deq.wait(v, ci, ld_t[ci])
                            if ci >= 2:
                                s_sto.wait(v, ci, b_sto[ci % 2] + 16 * (ci // 2))
                            s_dqv.inc(v.tensor_copy(wb[ci % 2][:, 0:cur],
                                                    wf[ci % 2][:, 0:cur]))

                    @blk.scalar
                    def _(sc):
                        s_sto.wait_totals(sc, b_sto)
                        for ci, (g, ch) in enumerate(chunks):
                            c0 = ch * CW
                            cur = min(CW, ncols * 128 - c0)
                            s_dqv.wait(sc, b_dqv + (ci + 1))
                            nt0 = c0 // 128
                            nts = cur // 128
                            s_sto.inc(ci, sc.dma_start(
                                bass.AP(odr, (g * ncols + nt0) * TS,
                                        [[128, 128], [TS, nts], [1, 128]]),
                                bass.AP(wb[ci % 2], 0,
                                        [[CW, 128], [128, nts], [1, 128]])))

        def issue_ag(cci, src_t, dst_t):
            done = s_sto.totals()
            with nc.Block() as blk:
                @blk.gpsimd
                def _(g):
                    s_sto.wait_totals(g, done)
                    s_cc[cci].inc(g.collective_compute(
                        "AllGather", ALU.bypass, replica_groups=ALL8,
                        ins=[src_t.ap().opt()], outs=[dst_t.ap().opt()]))

        CWP = 2048
        cast_chunks = []
        for (cwdr, codr, cnc) in ((wat, wat_sl, CT), (wpr, wpr_sl, DK)):
            for cgg in range(DKC):
                for cch in range((cnc * 128 + CWP - 1) // CWP):
                    cc0 = cch * CWP
                    cast_chunks.append((cwdr, codr, cgg, cnc, cc0,
                                        min(CWP, cnc * 128 - cc0)))
        NCAST = len(cast_chunks)
        cld_t, pst_t = {}, {}
        cast_cur = [0]
        with ExitStack() as cast_ps:
            pcf = [cast_ps.enter_context(
                nc.sbuf_tensor(un("pcf"), [128, CWP], F32)) for i in range(2)]
            pcb = [cast_ps.enter_context(
                nc.sbuf_tensor(un("pcb"), [128, CWP], BF16)) for i in range(2)]
            emit_dequant_q(gq, gs, zs_g, wg_sl, DKC, NT * 128, 1.0, "g",
                           want_casts=True)
            issue_ag(0, wg_sl, wg_all)
            emit_dequant_q(uq, us, zs_u, wu_sl, DKC, NT * 128, 1.0, "u",
                           want_casts=True)
            issue_ag(1, wu_sl, wu_all)
            emit_dequant_q(dq, ds, zs_d, wd_sl, NT, DTL * 128, 0.5, "d",
                           want_casts=True)
            issue_ag(2, wd_sl, wd_all)
            assert cast_cur[0] == NCAST, (cast_cur[0], NCAST)
            with nc.Block() as blk:
                @blk.gpsimd
                def _(g):
                    s_pst.wait_totals(g, s_pst.totals())
                    s_cc[3].inc(g.collective_compute(
                        "AllGather", ALU.bypass, replica_groups=ALL8,
                        ins=[wat_sl.ap().opt()], outs=[wat_all.ap().opt()]))
                    s_cc[4].inc(g.collective_compute(
                        "AllGather", ALU.bypass, replica_groups=ALL8,
                        ins=[wpr_sl.ap().opt()], outs=[wpr_all.ap().opt()]))

        def emit_casts_late():
            pass

        def emit_pass1(src, src_wait=None):
            with ExitStack() as ps:
                psb = lambda n, sh, dt: ps.enter_context(nc.sbuf_tensor(un(n), sh, dt))
                xt = [psb("p1x", [128, R], F32) for i in range(2)]
                tb = [psb("p1b", [128, R], BF16) for i in range(2)]
                with nc.Block() as blk, \
                        nc.psum_tensor(un("ps_sumA"), [1, RH], F32) as ps_sumA, \
                        nc.psum_tensor(un("ps_sumB"), [1, RH], F32) as ps_sumB:
                    b_lnv, b_sums, b_bnc = s_lnv.n, s_sums.n, s_bnc.n
                    ld_t = {}

                    @blk.sync
                    def _(sy):
                        if src_wait is not None:
                            s_out.wait_totals(sy, src_wait)
                        for dt in range(DK):
                            if dt >= 2:
                                s_lnv.wait(sy, b_lnv + dt - 1)
                            ld_t[dt] = s_dx.inc(dt, sy.dma_start(
                                xt[dt % 2][:, :], src[dt * 128:(dt + 1) * 128, :]))

                    @blk.vector
                    def _(v):
                        for dt in range(DK):
                            s_dx.wait(v, dt, ld_t[dt])
                            if dt >= 2:
                                s_sums.wait(v, b_sums + dt - 1)
                            s_lnv.inc(v.tensor_copy(tb[dt % 2][:, :], xt[dt % 2][:, :]))
                        s_sums.wait(v, b_sums + DK)
                        t9 = ch_v.inc(v.tensor_scalar(lnrow_a[0:1, 0:RH], ps_sumA[0:1, :],
                                                      -1.0 / D, None, ALU.mult))
                        ch_v.wait(v, t9)
                        s_lnv.inc(v.tensor_scalar(lnrow_a[0:1, RH:R], ps_sumB[0:1, :],
                                                  -1.0 / D, None, ALU.mult))

                    @blk.tensor
                    def _(t):
                        for dt in range(DK):
                            s_lnv.wait(t, b_lnv + dt + 1)
                            t.matmul(ps_sumA[0:1, :], ones_col[:, 0:1],
                                     tb[dt % 2][:, 0:RH], start=(dt == 0),
                                     stop=(dt == DK - 1))
                            s_sums.inc(t.matmul(ps_sumB[0:1, :], ones_col[:, 0:1],
                                                tb[dt % 2][:, RH:R], start=(dt == 0),
                                                stop=(dt == DK - 1)))

                    @blk.scalar
                    def _(sc):
                        s_lnv.wait(sc, b_lnv + DK + 1)
                        ch_v.wait(sc, 0)
                        s_bnc.inc(sc.dma_start(m_bnc[0:1, :], lnrow_a[0:1, :]), 16)
                        s_bnc.wait(sc, b_bnc + 16)
                        s_bnc.inc(sc.dma_start(
                            m_rep[:, :], bass.AP(m_bnc, 0, [[0, 128], [1, R]])), 16)
            return s_bnc.n

        def emit_pass2(src, xb, mrep_t):
            with ExitStack() as ps:
                psb = lambda n, sh, dt: ps.enter_context(nc.sbuf_tensor(un(n), sh, dt))
                xt = [psb("p2x", [128, R], F32) for i in range(2)]
                sq = [psb("p2s", [128, R], BF16) for i in range(2)]
                with nc.Block() as blk, \
                        nc.psum_tensor(un("ps_ssqA"), [1, RH], F32) as ps_ssqA, \
                        nc.psum_tensor(un("ps_ssqB"), [1, RH], F32) as ps_ssqB:
                    b_lnv, b_lna, b_sums = s_lnv.n, s_lna.n, s_sums.n
                    ld_t = {}

                    @blk.sync
                    def _(sy):
                        for dt in range(DK):
                            if dt >= 2:
                                s_lnv.wait(sy, b_lnv + dt - 1)
                            ld_t[dt] = s_dx.inc(dt, sy.dma_start(
                                xt[dt % 2][:, :], src[dt * 128:(dt + 1) * 128, :]))

                    @blk.vector
                    def _(v):
                        s_bnc.wait(v, mrep_t)
                        for dt in range(DK):
                            s_dx.wait(v, dt, ld_t[dt])
                            s_lnv.inc(v.tensor_tensor(xb[:, dt * R:(dt + 1) * R],
                                                      xt[dt % 2][:, :], m_rep[:, :],
                                                      ALU.add))
                        s_sums.wait(v, b_sums + DK)
                        t9 = ch_v.inc(v.tensor_scalar(lnrow_a[0:1, 0:RH], ps_ssqA[0:1, :],
                                                      1.0 / D, EPS, ALU.mult, ALU.add))
                        ch_v.wait(v, t9)
                        s_lnv.inc(v.tensor_scalar(lnrow_a[0:1, RH:R], ps_ssqB[0:1, :],
                                                  1.0 / D, EPS, ALU.mult, ALU.add))

                    @blk.scalar
                    def _(sc):
                        for dt in range(DK):
                            s_lnv.wait(sc, b_lnv + dt + 1)
                            if dt >= 2:
                                s_sums.wait(sc, b_sums + dt - 1)
                            s_lna.inc(sc.activation(sq[dt % 2][:, :],
                                                    xb[:, dt * R:(dt + 1) * R],
                                                    AF.Square))
                        s_lnv.wait(sc, b_lnv + DK + 1)
                        s_lna.inc(sc.activation(lnrow_b[0:1, :], lnrow_a[0:1, :],
                                                AF.Sqrt))

                    @blk.tensor
                    def _(t):
                        for dt in range(DK):
                            s_lna.wait(t, b_lna + dt + 1)
                            t.matmul(ps_ssqA[0:1, :], ones_col[:, 0:1],
                                     sq[dt % 2][:, 0:RH], start=(dt == 0),
                                     stop=(dt == DK - 1))
                            s_sums.inc(t.matmul(ps_ssqB[0:1, :], ones_col[:, 0:1],
                                                sq[dt % 2][:, RH:R], start=(dt == 0),
                                                stop=(dt == DK - 1)))

                with nc.Block() as blk:
                    b_lnv2, b_bnc2 = s_lnv.n, s_bnc.n

                    @blk.vector
                    def _(v):
                        s_lna.wait(v)
                        s_lnv.inc(v.reciprocal(lnrow_a[0:1, :], lnrow_b[0:1, :]))

                    @blk.scalar
                    def _(sc):
                        s_lnv.wait(sc, b_lnv2 + 1)
                        s_bnc.inc(sc.dma_start(r_bnc[0:1, :], lnrow_a[0:1, :]), 16)
                        s_bnc.wait(sc, b_bnc2 + 16)
                        s_bnc.inc(sc.dma_start(
                            r_rep[:, :], bass.AP(r_bnc, 0, [[0, 128], [1, R]])), 16)

                rrep_t = s_bnc.n
                with nc.Block() as blk:
                    @blk.vector
                    def _(v):
                        s_bnc.wait(v, rrep_t)
                        for dt in range(DK):
                            v.tensor_tensor(xb[:, dt * R:(dt + 1) * R],
                                            xb[:, dt * R:(dt + 1) * R],
                                            r_rep[:, :], ALU.mult)
            return s_bnc.n

        def emit_final(src, dst, src_wait=None):
            with ExitStack() as ps:
                psb = lambda n, sh, dt: ps.enter_context(nc.sbuf_tensor(un(n), sh, dt))
                xt = [psb("fpx", [128, R], F32) for i in range(2)]
                to = [psb("fpo", [128, R], F32) for i in range(2)]
                with nc.Block() as blk:
                    b_fp = s_fp.n
                    b_out = s_out.totals()
                    ld_t = {}

                    @blk.sync
                    def _(sy):
                        if src_wait is not None:
                            s_out.wait_totals(sy, src_wait)
                        for dt in range(DK):
                            if dt >= 2:
                                s_fp.wait(sy, b_fp + dt - 1)
                            ld_t[dt] = s_dx.inc(dt, sy.dma_start(
                                xt[dt % 2][:, :], src[dt * 128:(dt + 1) * 128, :]))
                            if dt >= 1:
                                s_fp.wait(sy, b_fp + dt)
                                s_out.inc(dt - 1, sy.dma_start(
                                    dst[(dt - 1) * 128:dt * 128, :],
                                    to[(dt - 1) % 2][:, :]))
                        s_fp.wait(sy, b_fp + DK)
                        s_out.inc(DK - 1, sy.dma_start(
                            dst[(DK - 1) * 128:DK * 128, :], to[(DK - 1) % 2][:, :]))

                    @blk.vector
                    def _(v):
                        for dt in range(DK):
                            s_dx.wait(v, dt, ld_t[dt])
                            if dt >= 2:
                                s_out.wait(v, dt, b_out[dt % 2] + 16 * (dt // 2))
                            s_fp.inc(v.tensor_tensor(to[dt % 2][:, :], xt[dt % 2][:, :],
                                                     acc[:, dt * R:(dt + 1) * R],
                                                     ALU.add))
            return s_out.totals()

        def emit_ffn(xb, rrep_t):
            with ExitStack() as ps:
                psb = lambda n, sh, dt: ps.enter_context(nc.sbuf_tensor(un(n), sh, dt))
                gsl = [psb("gsl", [128, DK * 128], BF16) for i in range(2)]
                usl = [psb("usl", [128, DK * 128], BF16) for i in range(2)]
                dsl = [psb("dsl", [128, SNT * DPB * 128], BF16) for i in range(2)]
                hstg = [psb("hst", [128, SNT * R], BF16) for i in range(2)]
                sil = [psb("sil", [128, R], BF16) for i in range(2)]
                tmp_u = [psb("tmu", [128, R], BF16) for i in range(2)]
                tmp_d = [psb("tmd", [128, R], BF16) for i in range(2)]
                with ExitStack() as pps, nc.Block() as blk:
                    P_gA = pps.enter_context(nc.psum_tensor(un("P_gA"), [128, RH], F32))
                    P_gB = pps.enter_context(nc.psum_tensor(un("P_gB"), [128, RH], F32))
                    P_uA = pps.enter_context(nc.psum_tensor(un("P_uA"), [128, RH], F32))
                    P_uB = pps.enter_context(nc.psum_tensor(un("P_uB"), [128, RH], F32))
                    P_dA = [pps.enter_context(nc.psum_tensor(un("P_dA"), [128, RH], F32))
                            for i in range(2)]
                    P_dB = [pps.enter_context(nc.psum_tensor(un("P_dB"), [128, RH], F32))
                            for i in range(2)]
                    bs = dict(g=s_g.n, u=s_u.n, eg=s_eg.n, eu=s_eu.n, h=s_h.n,
                              sil=s_sil.n, dmm=s_dmm.n, dev=s_dev.n, dac=s_dac.n)
                    w_t, wd_t = {}, {}

                    @blk.sync
                    def _(sy):
                        s_cc[0].wait(sy, 1)

                        def down_slabs(stg):
                            for b in range(DBLK):
                                j2 = stg * DBLK + b
                                if j2 == 0:
                                    s_cc[2].wait(sy, 1)
                                if j2 >= 2:
                                    s_dmm.wait(sy, bs["dmm"] + DPB * (j2 - 1))
                                for cc in range(CPB):
                                    c = b * CPB + cc
                                    wd_t[j2] = s_dwd.inc(j2, sy.dma_start(
                                        bass.AP(dsl[j2 % 2], cc * SNT * DTL * 128,
                                                [[SNT * DPB * 128, 128],
                                                 [128, SNT * DTL], [1, 128]]),
                                        bass.AP(wd_all, ((c * NT + stg * SNT) * DTL) * TS,
                                                [[128, 128], [TS, SNT * DTL], [1, 128]])))

                        for stg in range(NSTG):
                            for nt in range(stg * SNT, (stg + 1) * SNT):
                                if nt >= 2:
                                    s_g.wait(sy, bs["g"] + nt - 1)
                                    s_u.wait(sy, bs["u"] + nt - 1)
                                for hf in range(2):
                                    s_dw.inc(nt, sy.dma_start(
                                        bass.AP(gsl[nt % 2], hf * KCH * 128,
                                                [[DK * 128, 128], [128, KCH], [1, 128]]),
                                        bass.AP(wg_all, (hf * KCH * NT + nt) * TS,
                                                [[128, 128], [NT * TS, KCH], [1, 128]])))
                                if nt == 0:
                                    s_cc[1].wait(sy, 1)
                                for hf in range(2):
                                    w_t[nt] = s_dw.inc(nt, sy.dma_start(
                                        bass.AP(usl[nt % 2], hf * KCH * 128,
                                                [[DK * 128, 128], [128, KCH], [1, 128]]),
                                        bass.AP(wu_all, (hf * KCH * NT + nt) * TS,
                                                [[128, 128], [NT * TS, KCH], [1, 128]])))
                            if stg >= 1:
                                down_slabs(stg - 1)
                        down_slabs(NSTG - 1)

                    @blk.tensor
                    def _(t):
                        jj = [0]

                        def down_groups(stg):
                            for b in range(DBLK):
                                j2 = stg * DBLK + b
                                for dtb in range(DPB):
                                    j = jj[0]
                                    s_dwd.wait(t, j2, wd_t[j2])
                                    s_h.wait(t, bs["h"] + (stg + 1) * SNT)
                                    if j >= 2:
                                        s_dev.wait(t, bs["dev"] + 2 * (j - 1))
                                    ins = None
                                    for il in range(SNT):
                                        dcol = ((dtb // DTL) * SNT * DTL
                                                + il * DTL + (dtb % DTL))
                                        lhsT = dsl[j2 % 2][:, dcol * 128:
                                                           (dcol + 1) * 128]
                                        t.matmul(P_dA[j % 2][:, :], lhsT,
                                                 hstg[stg % 2][:, il * R: il * R + RH],
                                                 start=(il == 0), stop=(il == SNT - 1))
                                        ins = t.matmul(
                                            P_dB[j % 2][:, :], lhsT,
                                            hstg[stg % 2][:, il * R + RH:(il + 1) * R],
                                            start=(il == 0), stop=(il == SNT - 1))
                                    s_dmm.inc(ins)
                                    jj[0] += 1

                        for stg in range(NSTG):
                            for nt in range(stg * SNT, (stg + 1) * SNT):
                                s_dw.wait(t, nt, w_t[nt])
                                if nt >= 1:
                                    s_sil.wait(t, bs["sil"] + 2 * nt)
                                ins = None
                                for kc in range(DK):
                                    lhsT = gsl[nt % 2][:, kc * 128:(kc + 1) * 128]
                                    t.matmul(P_gA[:, :], lhsT,
                                             xb[:, kc * R: kc * R + RH],
                                             start=(kc == 0), stop=(kc == DK - 1))
                                    ins = t.matmul(P_gB[:, :], lhsT,
                                                   xb[:, kc * R + RH:(kc + 1) * R],
                                                   start=(kc == 0), stop=(kc == DK - 1))
                                s_g.inc(ins)
                                if nt >= 1:
                                    s_eu.wait(t, bs["eu"] + 2 * nt)
                                for kc in range(DK):
                                    lhsT = usl[nt % 2][:, kc * 128:(kc + 1) * 128]
                                    t.matmul(P_uA[:, :], lhsT,
                                             xb[:, kc * R: kc * R + RH],
                                             start=(kc == 0), stop=(kc == DK - 1))
                                    ins = t.matmul(P_uB[:, :], lhsT,
                                                   xb[:, kc * R + RH:(kc + 1) * R],
                                                   start=(kc == 0), stop=(kc == DK - 1))
                                s_u.inc(ins)
                            if stg >= 1:
                                down_groups(stg - 1)
                        down_groups(NSTG - 1)

                    @blk.scalar
                    def _(sc):
                        jd = [0]

                        def devacs(stg):
                            for b in range(DBLK):
                                for dtb in range(DPB):
                                    j = jd[0]
                                    s_dmm.wait(sc, bs["dmm"] + j + 1)
                                    if j >= 2:
                                        s_dac.wait(sc, bs["dac"] + j - 1)
                                    s_dev.inc(sc.activation(tmp_d[j % 2][:, 0:RH],
                                                            P_dA[j % 2][:, :], AF.Copy))
                                    s_dev.inc(sc.activation(tmp_d[j % 2][:, RH:R],
                                                            P_dB[j % 2][:, :], AF.Copy))
                                    jd[0] += 1

                        for stg in range(NSTG):
                            for nt in range(stg * SNT, (stg + 1) * SNT):
                                s_g.wait(sc, bs["g"] + nt + 1)
                                if nt >= 2:
                                    s_h.wait(sc, bs["h"] + nt - 1)
                                s_sil.inc(sc.activation(sil[nt % 2][:, 0:RH],
                                                        P_gA[:, :], AF.Silu))
                                s_sil.inc(sc.activation(sil[nt % 2][:, RH:R],
                                                        P_gB[:, :], AF.Silu))
                                s_u.wait(sc, bs["u"] + nt + 1)
                                s_eu.inc(sc.activation(tmp_u[nt % 2][:, 0:RH],
                                                       P_uA[:, :], AF.Copy))
                                s_eu.inc(sc.activation(tmp_u[nt % 2][:, RH:R],
                                                       P_uB[:, :], AF.Copy))
                            if stg >= 1:
                                devacs(stg - 1)
                        devacs(NSTG - 1)

                    @blk.vector
                    def _(v):
                        ja = [0]

                        def daccs(stg):
                            for b in range(DBLK):
                                for dtb in range(DPB):
                                    j = ja[0]
                                    gdt = b * DPB + dtb
                                    s_dev.wait(v, bs["dev"] + 2 * (j + 1))
                                    if stg == 0:
                                        ins = v.tensor_copy(acc[:, gdt * R:(gdt + 1) * R],
                                                            tmp_d[j % 2][:, :])
                                    else:
                                        ins = v.tensor_tensor(
                                            acc[:, gdt * R:(gdt + 1) * R],
                                            acc[:, gdt * R:(gdt + 1) * R],
                                            tmp_d[j % 2][:, :], ALU.add)
                                    s_dac.inc(ins)
                                    ja[0] += 1

                        for stg in range(NSTG):
                            for nt in range(stg * SNT, (stg + 1) * SNT):
                                s_sil.wait(v, bs["sil"] + 2 * (nt + 1))
                                s_eu.wait(v, bs["eu"] + 2 * (nt + 1))
                                stg_of = nt // SNT
                                if stg_of >= 2:
                                    s_dmm.wait(v, bs["dmm"] + GPS * (stg_of - 1))
                                il = nt - stg_of * SNT
                                s_h.inc(v.tensor_tensor(
                                    hstg[stg_of % 2][:, il * R:(il + 1) * R],
                                    sil[nt % 2][:, :], tmp_u[nt % 2][:, :], ALU.mult))
                            if stg >= 1:
                                daccs(stg - 1)
                        daccs(NSTG - 1)

        def emit_qkv(xb, rrep_t):
            VCT0 = 2 * DK
            cts = list(range(DK, 2 * DK)) + list(range(DK)) + list(range(2 * DK, CT))
            with ExitStack() as ps:
                psb = lambda n, sh, dt: ps.enter_context(nc.sbuf_tensor(un(n), sh, dt))
                asl = [psb("asl", [128, DK * 128], BF16) for i in range(2)]
                qkt = [psb("qkt", [128, R], BF16) for i in range(2)]
                vti = [psb("vti", [128, R], BF16) for i in range(2)]
                vst = [psb("vst", [128, R], BF16) for i in range(2)]
                with ExitStack() as pps, nc.Block() as blk:
                    PA = [pps.enter_context(nc.psum_tensor(un("PqA"), [128, RH], F32))
                          for i in range(2)]
                    PB = [pps.enter_context(nc.psum_tensor(un("PqB"), [128, RH], F32))
                          for i in range(2)]
                    P_t = pps.enter_context(nc.psum_tensor(un("P_t"), [128, R], BF16))
                    b = dict(q=s_q.n, qe=s_qe.n, e1=s_e1.n, vt=s_vt.n, vs=s_vs.n)
                    kst_t = [None]
                    b_sto = s_sto.totals()
                    kst_t2 = (
                        (b_sto[0] + 16 * ((DK + 1) // 2), b_sto[1] + 16 * (DK // 2)),
                        (s_sv.totals()[0] + 16 * ((DK + 1) // 2),
                         s_sv.totals()[1] + 16 * (DK // 2)),
                    )
                    b_sv = s_sv.totals()
                    wat_t = {}

                    @blk.sync
                    def _(sy):
                        s_cc[3].wait(sy, 1)
                        for pi, ct in enumerate(cts):
                            if pi >= 2:
                                s_q.wait(sy, b["q"] + pi - 1)
                            for hf in range(2):
                                wat_t[pi] = s_dw.inc(pi, sy.dma_start(
                                    bass.AP(asl[pi % 2], hf * KCH * 128,
                                            [[DK * 128, 128], [128, KCH], [1, 128]]),
                                    bass.AP(wat_all, (hf * KCH * CT + ct) * TS,
                                            [[128, 128], [CT * TS, KCH], [1, 128]])))

                    @blk.tensor
                    def _(t):
                        def transposes(vi):
                            vct = VCT0 + vi
                            s_qe.wait(t, b["qe"] + vct + 1)
                            if vi >= 1:
                                s_vs.wait(t, b["vs"] + vi)
                            ins = None
                            for sl in range(SLOTS):
                                ins = t.transpose(P_t[:, sl * 128:(sl + 1) * 128],
                                                  vti[vi % 2][:, sl * 128:(sl + 1) * 128],
                                                  ident[:, :])
                            s_vt.inc(ins)

                        for pi, ct in enumerate(cts):
                            s_dw.wait(t, pi, wat_t[pi])
                            if pi >= 2:
                                s_qe.wait(t, b["qe"] + pi - 1)
                            ins = None
                            for kc in range(DK):
                                lhsT = asl[pi % 2][:, kc * 128:(kc + 1) * 128]
                                t.matmul(PA[pi % 2][:, :], lhsT,
                                         xb[:, kc * R: kc * R + RH],
                                         start=(kc == 0), stop=(kc == DK - 1))
                                ins = t.matmul(PB[pi % 2][:, :], lhsT,
                                               xb[:, kc * R + RH:(kc + 1) * R],
                                               start=(kc == 0), stop=(kc == DK - 1))
                            s_q.inc(ins)
                            if pi > VCT0:
                                transposes(pi - 1 - VCT0)
                        transposes(DK - 1)

                    @blk.vector
                    def _(v):
                        s_bnc.wait(v, rrep_t)

                        def vstrip(vi):
                            s_vt.wait(v, b["vt"] + vi + 1)
                            if vi >= 2:
                                s_sv.wait(v, vi, b_sv[vi % 2] + 16 * (vi // 2))
                            s_vs.inc(v.tensor_copy(vst[vi % 2][:, :], P_t[:, :]))

                        for pi, ct in enumerate(cts):
                            s_q.wait(v, b["q"] + pi + 1)
                            qsc = (1.0 / 128.0) if ct < DK else 1.0
                            if pi < VCT0:
                                if pi >= 2:
                                    s_sto.wait(v, pi, b_sto[pi % 2] + 16 * (pi // 2))
                                dst = qkt[pi % 2]
                            else:
                                vi = pi - VCT0
                                if vi >= 2:
                                    s_vt.wait(v, b["vt"] + vi - 1)
                                dst = vti[vi % 2]
                            t1 = ch_v.inc(v.tensor_scalar(dst[:, 0:RH], PA[pi % 2][:, :],
                                                          ba_sb[:, ct:ct + 1], qsc,
                                                          ALU.add, ALU.mult))
                            ch_v.wait(v, t1)
                            s_qe.inc(v.tensor_scalar(dst[:, RH:R], PB[pi % 2][:, :],
                                                     ba_sb[:, ct:ct + 1], qsc,
                                                     ALU.add, ALU.mult))
                            if pi > VCT0 + 1:
                                vstrip(pi - 2 - VCT0)
                        vstrip(DK - 2)
                        vstrip(DK - 1)

                    @blk.gpsimd
                    def _(g):
                        kk = kst_t2[0]
                        s_sto.wait_totals(g, kk)
                        s_cc[5].inc(g.collective_compute(
                            "AllGather", ALU.bypass, replica_groups=PAIRS,
                            ins=[kt_own.ap().opt()], outs=[kt_all.ap().opt()]))
                        s_sv.wait_totals(g, kst_t2[1])
                        s_cc[6].inc(g.collective_compute(
                            "AllGather", ALU.bypass, replica_groups=PAIRS,
                            ins=[v_own.ap().opt()], outs=[v_all.ap().opt()]))

                    @blk.scalar
                    def _(sc):
                        s_sto.wait_totals(sc, b_sto)
                        s_sv.wait_totals(sc, b_sv)
                        for pi in range(VCT0):
                            ct = cts[pi]
                            s_qe.wait(sc, b["qe"] + pi + 1)
                            dstd = q_own if ct < DK else kt_own
                            r0 = (ct if ct < DK else ct - DK) * 128
                            s_sto.inc(pi, sc.dma_start(dstd[r0:r0 + 128, :],
                                                       qkt[pi % 2][:, :]))
                            if pi == DK - 1:
                                kst_t[0] = s_sto.totals()
                        for vi in range(DK):
                            s_vs.wait(sc, b["vs"] + vi + 1)
                            s_sv.inc(vi, sc.dma_start(
                                bass.AP(v_own, vi * 128,
                                        [[D, 128], [128 * D, SLOTS], [1, 128]]),
                                bass.AP(vst[vi % 2], 0,
                                        [[R, 128], [128, SLOTS], [1, 128]])))
            return kst_t[0], s_sto.totals(), s_sv.totals()

        def emit_proj(attn_sb):
            with ExitStack() as ps:
                psb = lambda n, sh, dt: ps.enter_context(nc.sbuf_tensor(un(n), sh, dt))
                psl = [psb("psl", [128, DK * 128], BF16) for i in range(2)]
                with ExitStack() as pps, nc.Block() as blk:
                    PA = [pps.enter_context(nc.psum_tensor(un("PpA"), [128, RH], F32))
                          for i in range(2)]
                    PB = [pps.enter_context(nc.psum_tensor(un("PpB"), [128, RH], F32))
                          for i in range(2)]
                    b = dict(q=s_q.n, pj=s_pj.n)
                    wpr_t = {}

                    @blk.sync
                    def _(sy):
                        s_cc[4].wait(sy, 1)
                        for ct in range(DK):
                            if ct >= 2:
                                s_q.wait(sy, b["q"] + ct - 1)
                            for hf in range(2):
                                wpr_t[ct] = s_dw.inc(ct, sy.dma_start(
                                    bass.AP(psl[ct % 2], hf * KCH * 128,
                                            [[DK * 128, 128], [128, KCH], [1, 128]]),
                                    bass.AP(wpr_all, (hf * KCH * DK + ct) * TS,
                                            [[128, 128], [DK * TS, KCH], [1, 128]])))

                    @blk.tensor
                    def _(t):
                        for ct in range(DK):
                            s_dw.wait(t, ct, wpr_t[ct])
                            if ct >= 2:
                                s_pj.wait(t, b["pj"] + 2 * (ct - 1))
                            ins = None
                            for kc in range(DK):
                                lhsT = psl[ct % 2][:, kc * 128:(kc + 1) * 128]
                                t.matmul(PA[ct % 2][:, :], lhsT,
                                         attn_sb[:, kc * R: kc * R + RH],
                                         start=(kc == 0), stop=(kc == DK - 1))
                                ins = t.matmul(PB[ct % 2][:, :], lhsT,
                                               attn_sb[:, kc * R + RH:(kc + 1) * R],
                                               start=(kc == 0), stop=(kc == DK - 1))
                            s_q.inc(ins)

                    @blk.scalar
                    def _(sc):
                        for ct in range(DK):
                            s_q.wait(sc, b["q"] + ct + 1)
                            s_pj.inc(sc.activation(acc[:, ct * R: ct * R + RH],
                                                   PA[ct % 2][:, :], AF.Identity,
                                                   bias=bp_sb[:, ct:ct + 1]))
                            s_pj.inc(sc.activation(acc[:, ct * R + RH:(ct + 1) * R],
                                                   PB[ct % 2][:, :], AF.Identity,
                                                   bias=bp_sb[:, ct:ct + 1]))


        with nc.sbuf_tensor(un("xb1"), [128, DK * R], BF16) as xb:
            mrep_t = emit_pass1(xin)
            rrep_t = emit_pass2(xin, xb, mrep_t)
            emit_ffn(xb, rrep_t)
        fin1_t = emit_final(xin, xres)

        if debug_stop == "ffn1":
            with nc.Block() as blk:
                @blk.sync
                def _(sy):
                    s_out.wait_totals(sy, fin1_t)
            return nc

        with nc.sbuf_tensor(un("xb2"), [128, DK * R], BF16) as xb:
            mrep_t = emit_pass1(xres, src_wait=fin1_t)
            rrep_t = emit_pass2(xres, xb, mrep_t)
            kv_k_t, kv_sto_t, kv_sv_t = emit_qkv(xb, rrep_t)

        with ExitStack() as ascope:
            attn_sb = ascope.enter_context(
                nc.sbuf_tensor(un("attn_sb"), [128, DK * R], BF16))
            inner = ascope.enter_context(ExitStack())
            asb = lambda name, shape, dt: inner.enter_context(
                nc.sbuf_tensor(un(name), shape, dt))
            masks = asb("masks_sb", [128, SLOTS * 2 * 128], BF16)
            qsb = asb("qsb_sb", [1, R], F32)
            kcol = asb("kcol_sb", [128, NQT], F32)
            ktile = [asb("ktl", [128, NQT * 128], BF16) for i in range(2)]
            qtile = [asb("qtl", [128, R], BF16) for i in range(2)]
            vh = [asb("vh", [128, NQT * 128], BF16) for i in range(2)]
            ptile = asb("ptile", [128, NQT * QW], BF16)
            qsrow = [asb("qsrow", [1, R], BF16) for i in range(2)]
            bcols = [asb("bcols", [128, NQT], F32) for i in range(2)]
            rrep_a = [asb("rrep_a", [128, QW], F32) for i in range(2)]
            pvt = [asb("pvt", [128, QW], BF16) for i in range(2)]
            m_all = asb("m_all", [128, H * SLOTS], F32)
            c_bf = asb("c_bf", [128, 1], BF16)
            c_rep = asb("c_rep", [128, 1], F32)
            crow = asb("crow", [1, 1], F32)
            rrow = [asb("rrow", [1, QW], F32) for i in range(2)]

            with ExitStack() as pps, nc.Block() as blk:
                P1 = [pps.enter_context(
                    nc.psum_tensor(un("P1"), [128, NQT * 128], F32)) for i in range(2)]
                b = dict(p1=s_p1.n, m=s_m.n)
                hd_t = {}

                @blk.sync
                def _(sy):
                    s_cc[5].wait(sy, 1)
                    s_sto.wait_totals(sy, kv_sto_t)
                    s_deq.inc(0, sy.dma_start(masks[:, :], masks_in[:, :]))
                    s_deq.inc(1, sy.dma_start(qsb[:, :], qsb_in[:, :]))
                    s_deq.inc(1, sy.dma_start(kcol[:, :], kcol_in[:, :]))
                    for h in range(H):
                        if h >= 2:
                            s_p1.wait(sy, b["p1"] + SLOTS * (h - 1))
                        for pi in range(2):
                            s_dx.inc(h, sy.dma_start(
                                bass.AP(ktile[h % 2], pi * 128,
                                        [[NQT * 128, 128], [256, SLOTS], [1, 128]]),
                                bass.AP(kt_all, (pi * D + h * 128) * R,
                                        [[R, 128], [128, SLOTS], [1, 128]])))
                        hd_t[h] = s_dx.inc(h, sy.dma_start(
                            qtile[h % 2][:, :], q_own[h * 128:(h + 1) * 128, :]))

                @blk.tensor
                def _(t):
                    s_deq.wait_totals(t)
                    for h in range(H):
                        s_dx.wait(t, h, hd_t[h])
                        for s in range(SLOTS):
                            gi = h * SLOTS + s
                            if gi >= 2:
                                s_m.wait(t, b["m"] + gi - 1)
                            w = (2 * s + 2) * 128
                            lhsT = qtile[h % 2][:, s * 128:(s + 1) * 128]
                            ins = None
                            for c0 in range((w + 511) // 512):
                                cw = min(512, w - c0 * 512)
                                ins = t.matmul(P1[gi % 2][:, c0 * 512:c0 * 512 + cw],
                                               lhsT,
                                               ktile[h % 2][:, c0 * 512:c0 * 512 + cw])
                            s_p1.inc(ins)

                @blk.vector
                def _(v):
                    for h in range(H):
                        for s in range(SLOTS):
                            gi = h * SLOTS + s
                            s_p1.wait(v, b["p1"] + gi + 1)
                            w = (2 * s + 2) * 128
                            s_m.inc(v.tensor_reduce(m_all[:, gi:gi + 1],
                                                    P1[gi % 2][:, 0:w], AX.X, ALU.max))

            with ExitStack() as pps, nc.Block() as blk:
                Pc = pps.enter_context(nc.psum_tensor(un("Pc"), [1, 128], BF16))
                b = dict(m=s_m.n, p1=s_p1.n, bnc=s_bnc.n)

                @blk.vector
                def _(v):
                    s_m.wait(v, b["m"])
                    s_m.inc(v.tensor_reduce(c_bf[:, 0:1], m_all[:, :], AX.X, ALU.max))
                    s_p1.wait(v, b["p1"] + 1)
                    s_m.inc(v.tensor_reduce(crow[0:1, 0:1], Pc[0:1, :], AX.X,
                                            ALU.max, negate=True))

                @blk.tensor
                def _(t):
                    s_m.wait(t, b["m"] + 1)
                    s_p1.inc(t.transpose(Pc[0:1, :], c_bf[:, 0:1], ident[:, :]))

                @blk.scalar
                def _(sc):
                    s_m.wait(sc, b["m"] + 2)
                    s_bnc.inc(sc.dma_start(c_bnc[0:1, :], crow[0:1, 0:1]), 16)
                    s_bnc.wait(sc, b["bnc"] + 16)
                    s_bnc.inc(sc.dma_start(c_rep[:, :],
                                           bass.AP(c_bnc, 0, [[0, 128], [1, 1]])), 16)

            with ExitStack() as pps, nc.Block() as blk:
                P2 = [pps.enter_context(nc.psum_tensor(un("P2"), [128, QW], F32))
                      for i in range(4)]
                Ps = [pps.enter_context(nc.psum_tensor(un("Ps"), [1, QW], F32))
                      for i in range(2)]
                Ppv = [pps.enter_context(nc.psum_tensor(un("Ppv"), [128, QW], F32))
                       for i in range(2)]
                b = dict(st=s_st.n, ex=s_ex.n, mk=s_mk.n, os=s_os.n, pv=s_pv.n,
                         rr=s_rr.n, pc=s_pc.n, bnc=s_bnc.n, b2=s_b2.n,
                         ba=s_ba.n, baa=s_baa.n)
                sched = []
                for h in range(H):
                    for qh in range(2):
                        kts = []
                        for kt in range(NQT):
                            smin = kt // 2
                            s0 = max(smin, qh * SPH)
                            s1 = min(SLOTS, (qh + 1) * SPH)
                            if s1 > s0:
                                kts.append((kt, s0, s1))
                        sched.append((h, qh, kts))
                flat = []
                for idx, (h, qh, kts) in enumerate(sched):
                    for ki, (kt, s0, s1) in enumerate(kts):
                        flat.append((idx, ki, kt, s0, s1))
                gpos_of = {(idx, ki): gp for gp, (idx, ki, kt, s0, s1) in enumerate(flat)}
                st_f, ba_f, baa_f, ex_f, mk_f, os_f, pv_f = [], [], [], [], [], [], []
                nst, nba, nbaa, nex, nmk, nos, npv = (b["st"], b["ba"], b["baa"],
                                                      b["ex"], b["mk"], b["os"], b["pv"])
                rr_t, pc_t = {}, {}
                nrr, npc = b["rr"], b["pc"]
                for gp, (idx, ki, kt, s0, s1) in enumerate(flat):
                    h, qh, kts = sched[idx]
                    nst += 1
                    st_f.append(nst)
                    ba_f.append(nba)
                    baa_f.append(nbaa)
                    nex += 1
                    ex_f.append(nex)
                    sm = kt // 2
                    if qh * SPH <= sm < (qh + 1) * SPH:
                        nmk += 1
                    mk_f.append(nmk)
                    nos += 1
                    os_f.append(nos)
                    npv += 1
                    pv_f.append(npv)
                    if ki == len(kts) - 1:
                        nrr += 1
                        rr_t[(idx, "recip")] = nrr
                        npc += 1
                        pc_t[idx] = npc
                        nrr += 1
                        rr_t[(idx, "evac")] = nrr
                pv_last = {}
                os_last = {}
                for gp, (idx, ki, kt, s0, s1) in enumerate(flat):
                    pv_last[idx] = pv_f[gp]
                    os_last[idx] = os_f[gp]
                bc_t, bc_w, hd_t = {}, {}, {}

                @blk.sync
                def _(sy):
                    s_cc[6].wait(sy, 1)
                    for idx, (h, qh, kts) in enumerate(sched):
                        if qh == 0:
                            if h >= 2:
                                s_pv.wait(sy, pv_last[2 * (h - 2) + 1])
                            for pi in range(2):
                                s_dx.inc(h, sy.dma_start(
                                    bass.AP(ktile[h % 2], pi * 128,
                                            [[NQT * 128, 128], [256, SLOTS], [1, 128]]),
                                    bass.AP(kt_all, (pi * D + h * 128) * R,
                                            [[R, 128], [128, SLOTS], [1, 128]])))
                            s_dx.inc(h, sy.dma_start(
                                qtile[h % 2][:, :], q_own[h * 128:(h + 1) * 128, :]))
                            for pi in range(2):
                                hd_t[h] = s_dx.inc(h, sy.dma_start(
                                    bass.AP(vh[h % 2], pi * 128,
                                            [[NQT * 128, 128], [256, SLOTS], [1, 128]]),
                                    bass.AP(v_all, pi * R * D + h * 128,
                                            [[D, 128], [128 * D, SLOTS], [1, 128]])))
                        s_rr.wait(sy, rr_t[(idx, "recip")])
                        bc_w[idx] = s_rb.inc(idx, sy.dma_start(
                            rh_bnc[h:h + 1, qh * QW:(qh + 1) * QW],
                            rrow[idx % 2][0:1, :]))
                        s_rb.wait(sy, idx, bc_w[idx])
                        if idx >= 2:
                            s_rr.wait(sy, rr_t[(idx - 2, "evac")])
                        bc_t[idx] = s_rb.inc(idx, sy.dma_start(
                            rrep_a[idx % 2][:, :],
                            bass.AP(rh_bnc, h * R + qh * QW, [[0, 128], [1, QW]])))

                @blk.tensor
                def _(t):
                    for idx, (h, qh, kts) in enumerate(sched):
                        s_dx.wait(t, h, hd_t[h])
                        s_b2.wait(t, b["b2"] + 2 * (h + 1))
                        for ki, (kt, s0, s1) in enumerate(kts):
                            w = (s1 - s0) * 128
                            gp = gpos_of[(idx, ki)]
                            if gp >= 4:
                                s_ex.wait(t, ex_f[gp - 4])
                            lhsT = ktile[h % 2][:, kt * 128:(kt + 1) * 128]
                            t.matmul(P2[gp % 4][:, 0:w], lhsT,
                                     qtile[h % 2][:, s0 * 128:s1 * 128],
                                     start=True, stop=False)
                            s_st.inc(t.matmul(
                                P2[gp % 4][:, 0:w], ones_row[0:1, :],
                                qsrow[h % 2][0:1, s0 * 128:s1 * 128],
                                start=False, stop=True))
                        for ki, (kt, s0, s1) in enumerate(kts):
                            w = (s1 - s0) * 128
                            gp = gpos_of[(idx, ki)]
                            s_mk.wait(t, mk_f[gp])
                            s_ex.wait(t, ex_f[gp])
                            if ki == 0 and idx >= 2:
                                s_rr.wait(t, rr_t[(idx - 2, "recip")])
                                s_pc.wait(t, pc_t[idx - 2])
                            off = (s0 - qh * SPH) * 128
                            pcol = kt * QW + off
                            s_os.inc(t.matmul(Ps[idx % 2][0:1, off:off + w],
                                              ones_col[:, 0:1],
                                              ptile[:, pcol:pcol + w],
                                              start=(ki == 0),
                                              stop=(ki == len(kts) - 1)))
                            s_pv.inc(t.matmul(Ppv[idx % 2][:, off:off + w],
                                              vh[h % 2][:, kt * 128:(kt + 1) * 128],
                                              ptile[:, pcol:pcol + w],
                                              start=(ki == 0),
                                              stop=(ki == len(kts) - 1)))

                @blk.vector
                def _(v):
                    for idx, (h, qh, kts) in enumerate(sched):
                        if qh == 0:
                            s_bnc.wait(v, b["bnc"])
                            if h >= 2:
                                s_ex.wait(v, ex_f[gpos_of[(2 * (h - 2) + 1,
                                                           len(sched[2 * (h - 2) + 1][2]) - 1)]])
                            s_b2.inc(v.tensor_scalar(qsrow[h % 2][0:1, :], qsb[0:1, :],
                                                     -float(slopes[h]), c_rep[0:1, 0:1],
                                                     ALU.mult, ALU.add))
                            s_b2.inc(v.tensor_scalar(bcols[h % 2][:, :], kcol[:, :],
                                                     float(slopes[h]), None, ALU.mult))
                        for ki, (kt, s0, s1) in enumerate(kts):
                            gp = gpos_of[(idx, ki)]
                            sm = kt // 2
                            if qh * SPH <= sm < (qh + 1) * SPH:
                                s_st.wait(v, st_f[gp])
                                poff = (sm - s0) * 128
                                mcol = (sm * 2 + (kt - 2 * sm)) * 128
                                s_mk.inc(v.tensor_tensor(
                                    P2[gp % 4][:, poff:poff + 128],
                                    P2[gp % 4][:, poff:poff + 128],
                                    masks[:, mcol:mcol + 128], ALU.add))
                        s_os.wait(v, os_last[idx])
                        if idx >= 2:
                            s_rb.wait(v, idx, bc_w[idx - 2])
                        s_rr.inc(v.reciprocal(rrow[idx % 2][0:1, :], Ps[idx % 2][0:1, :]))
                        s_pc.wait(v, pc_t[idx])
                        s_rb.wait(v, idx, bc_t[idx])
                        s_rr.inc(v.tensor_tensor(
                            attn_sb[:, h * R + qh * QW: h * R + (qh + 1) * QW],
                            pvt[idx % 2][:, :], rrep_a[idx % 2][:, :], ALU.mult))

                @blk.scalar
                def _(sc):
                    for idx, (h, qh, kts) in enumerate(sched):
                        for ki, (kt, s0, s1) in enumerate(kts):
                            gp = gpos_of[(idx, ki)]
                            s_st.wait(sc, st_f[gp])
                            s_mk.wait(sc, mk_f[gp])
                            s_b2.wait(sc, b["b2"] + 2 * (h + 1))
                            if idx >= 1 and ki == 0:
                                s_pv.wait(sc, pv_last[idx - 1])
                            w = (s1 - s0) * 128
                            pcol = kt * QW + (s0 - qh * SPH) * 128
                            s_ex.inc(sc.activation(ptile[:, pcol:pcol + w],
                                                   P2[gp % 4][:, 0:w], AF.Exp,
                                                   bias=bcols[h % 2][:, kt:kt + 1]))
                        s_pv.wait(sc, pv_last[idx])
                        if idx >= 2:
                            s_rr.wait(sc, rr_t[(idx - 2, "evac")])
                        s_pc.inc(sc.activation(pvt[idx % 2][:, :], Ppv[idx % 2][:, :],
                                               AF.Copy))

            inner.close()
            emit_proj(attn_sb)

        fin2_t = emit_final(xres, xres)

        with nc.sbuf_tensor(un("xb3"), [128, DK * R], BF16) as xb:
            mrep_t = emit_pass1(xres, src_wait=fin2_t)
            rrep_t = emit_pass2(xres, xb, mrep_t)
            emit_ffn(xb, rrep_t)
        out_t = emit_final(xres, yout)

        with nc.Block() as blk:
            @blk.sync
            def _(sy):
                s_out.wait_totals(sy, out_t)

    return nc


def shard_inputs(cfg, inp):
    D, I, S, B = cfg["D"], cfg["I"], cfg["S"], cfg["B"]
    DK, NT, NQT, SLOTS, R = cfg["DK"], cfg["NT"], cfg["NQT"], cfg["SLOTS"], cfg["R"]
    DKC, DTL, CT, H = cfg["DKC"], cfg["DTL"], cfg["CT"], cfg["H"]
    x = np.asarray(inp["L0_attn_hidden_states"], np.float32)
    gq = np.asarray(inp["L0_ffn_gate_qweight"])
    gs = np.asarray(inp["L0_ffn_gate_scales"], np.float32)
    gz = np.asarray(inp["L0_ffn_gate_qzeros"])
    uq = np.asarray(inp["L0_ffn_up_qweight"])
    us = np.asarray(inp["L0_ffn_up_scales"], np.float32)
    uz = np.asarray(inp["L0_ffn_up_qzeros"])
    dq = np.asarray(inp["L0_ffn_down_qweight"])
    dsc = np.asarray(inp["L0_ffn_down_scales"], np.float32)
    dz = np.asarray(inp["L0_ffn_down_qzeros"])
    wat = np.asarray(inp["L0_attn_w_attn"], np.float32)
    ba = np.asarray(inp["L0_attn_b_attn"], np.float32)
    wpr = np.asarray(inp["L0_attn_w_proj"], np.float32)
    bp = np.asarray(inp["L0_attn_b_proj"], np.float32)

    ident = np.eye(128, dtype=ml_dtypes.bfloat16)
    bac = np.ascontiguousarray(ba.reshape(CT, 128).T)
    bpc = np.ascontiguousarray(bp.reshape(DK, 128).T)

    in_maps = []
    for c in range(NCORES):
        b, p = c // 2, c % 2
        xr = x[b].reshape(NQT, 128, D)[p::2].reshape(R, D)
        xin_c = np.ascontiguousarray(xr.T)
        kp = np.arange(128)[:, None]
        qj = np.arange(128)[None, :]
        mtiles = []
        for s in range(SLOTS):
            for which in range(2):
                d = which - p
                mtiles.append(np.where(kp + d * 128 <= qj, 0.0, -30000.0))
        masks = np.concatenate(mtiles, axis=1).astype(ml_dtypes.bfloat16)
        qsbase = (((np.arange(R) // 128) * 2 + p) * 128 + 64).astype(np.float32)[None, :]
        kcolv = (np.arange(NQT)[None, :] * 128 + np.arange(128)[:, None]).astype(np.float32)
        im = {
            "xin": xin_c,
            "gq": np.ascontiguousarray(gq[c * DKC * 128:(c + 1) * DKC * 128]),
            "gs": np.ascontiguousarray(gs[c * DKC:(c + 1) * DKC]),
            "gz": np.ascontiguousarray(gz[c * DKC:(c + 1) * DKC]),
            "uq": np.ascontiguousarray(uq[c * DKC * 128:(c + 1) * DKC * 128]),
            "us": np.ascontiguousarray(us[c * DKC:(c + 1) * DKC]),
            "uz": np.ascontiguousarray(uz[c * DKC:(c + 1) * DKC]),
            "dq": np.ascontiguousarray(dq[:, c * DTL * 16:(c + 1) * DTL * 16]),
            "ds": np.ascontiguousarray(dsc[:, c * DTL * 128:(c + 1) * DTL * 128]),
            "dz": np.ascontiguousarray(dz[:, c * DTL * 16:(c + 1) * DTL * 16]),
            "wat": np.ascontiguousarray(wat[c * DKC * 128:(c + 1) * DKC * 128]),
            "wpr": np.ascontiguousarray(wpr[c * DKC * 128:(c + 1) * DKC * 128]),
            "bacols": bac,
            "bpcols": bpc,
            "ident_in": ident,
            "masks_in": masks,
            "qsb_in": qsbase,
            "kcol_in": kcolv,
        }
        in_maps.append(im)
    return in_maps


def assemble_output(cfg, results):
    D, S, B = cfg["D"], cfg["S"], cfg["B"]
    NQT, SLOTS = cfg["NQT"], cfg["SLOTS"]
    out = np.empty((B, S, D), np.float32)
    for c in range(NCORES):
        b, p = c // 2, c % 2
        y = np.asarray(results[c]["yout"])
        arr = np.ascontiguousarray(y.T).reshape(SLOTS, 128, D)
        out[b].reshape(NQT, 128, D)[p::2] = arr
    return out


def kernel(**inputs):
    cfg = CFG
    nc = build_program(cfg)
    in_maps = shard_inputs(cfg, inputs)
    res = run_bass_kernel_spmd(nc, in_maps, core_ids=list(range(NCORES)))
    return assemble_output(cfg, res.results)


# version 22 (final)
# speedup vs baseline: 1.0715x; 1.0158x over previous
import math
import numpy as np
import ml_dtypes
from contextlib import ExitStack

import concourse.bass as bass
import concourse.mybir as mybir
from concourse.bass_utils import run_bass_kernel_spmd

F32 = mybir.dt.float32
BF16 = mybir.dt.bfloat16
F16 = mybir.dt.float16
I32 = mybir.dt.int32
AF = mybir.ActivationFunctionType
ALU = mybir.AluOpType
AX = mybir.AxisListType

AWQ_ORDER = [0, 4, 1, 5, 2, 6, 3, 7]
NCORES = 8
EPS = 1e-5


def make_cfg(D, I, S, B):
    c = {}
    c["D"], c["I"], c["S"], c["B"] = D, I, S, B
    c["DK"] = D // 128
    c["NT"] = I // 128
    c["H"] = D // 128
    c["NQT"] = S // 128
    c["SLOTS"] = c["NQT"] // 2
    c["R"] = c["SLOTS"] * 128
    c["RH"] = c["R"] // 2
    c["DKC"] = c["DK"] // NCORES
    c["DTL"] = (D // NCORES) // 128
    c["CT"] = 3 * c["DK"]
    c["SNT"] = 4 if c["NT"] % 4 == 0 else 2
    c["NSTG"] = c["NT"] // c["SNT"]
    c["DBLK"] = 4 if (c["DK"] % (4 * c["DTL"]) == 0 and c["DK"] >= 4 * c["DTL"]) else 2
    c["DPB"] = c["DK"] // c["DBLK"]
    c["CPB"] = c["DPB"] // c["DTL"]
    c["KCH"] = c["DK"] // 2
    c["SPH"] = c["SLOTS"] // 2
    c["QW"] = c["SPH"] * 128
    assert c["DK"] % NCORES == 0 and c["NT"] % c["SNT"] == 0
    assert c["SLOTS"] % 2 == 0 and c["DPB"] % c["DTL"] == 0
    return c


CFG = make_cfg(4096, 11008, 2048, 4)


def alibi_slopes(n_heads):
    cp2 = 2 ** math.floor(math.log2(n_heads))
    base = 2 ** (-(2 ** (-(math.log2(cp2) - 3))))
    slopes = base ** np.arange(1, cp2 + 1, dtype=np.float64)
    if cp2 != n_heads:
        extra_base = 2 ** (-(2 ** (-(math.log2(2 * cp2) - 3))))
        n_rem = min(cp2, n_heads - cp2)
        extra = extra_base ** np.arange(1, 1 + 2 * n_rem, 2, dtype=np.float64)
        slopes = np.concatenate([slopes, extra])
    return slopes.astype(np.float32)


class Sem:
    def __init__(self, stack, nc, name):
        self.h = stack.enter_context(nc.semaphore(name))
        self.n = 0

    def inc(self, instr, k=1):
        instr.then_inc(self.h, k)
        self.n += k
        return self.n

    def wait(self, eng, target=None):
        t = self.n if target is None else target
        if t > 0:
            eng.wait_ge(self.h, t)


class SemP:

    def __init__(self, stack, nc, name):
        self.s = [Sem(stack, nc, f"{name}_a"), Sem(stack, nc, f"{name}_b")]

    def inc(self, slot, instr, k=16):
        return self.s[slot % 2].inc(instr, k)

    def wait(self, eng, slot, target):
        self.s[slot % 2].wait(eng, target)

    def totals(self):
        return (self.s[0].n, self.s[1].n)

    def wait_totals(self, eng, t=None):
        if t is None:
            self.s[0].wait(eng)
            self.s[1].wait(eng)
        else:
            self.s[0].wait(eng, t[0])
            self.s[1].wait(eng, t[1])


def build_program(cfg, debug_stop=None):
    D, I, S = cfg["D"], cfg["I"], cfg["S"]
    DK, NT, H, NQT = cfg["DK"], cfg["NT"], cfg["H"], cfg["NQT"]
    SLOTS, R, RH = cfg["SLOTS"], cfg["R"], cfg["RH"]
    DKC, DTL, CT = cfg["DKC"], cfg["DTL"], cfg["CT"]
    SNT, NSTG, DBLK, DPB, CPB = cfg["SNT"], cfg["NSTG"], cfg["DBLK"], cfg["DPB"], cfg["CPB"]
    KCH, SPH, QW = cfg["KCH"], cfg["SPH"], cfg["QW"]
    GPS = DBLK * DPB
    slopes = alibi_slopes(H)
    TS = 128 * 128

    nc = bass.Bass("TRN2", target_bir_lowering=False, dynamic_dma_scratch_size=4096)

    xin = nc.dram_tensor("xin", [D, R], F32, kind="ExternalInput")
    gq = nc.dram_tensor("gq", [DKC * 128, NT * 16], I32, kind="ExternalInput")
    gs = nc.dram_tensor("gs", [DKC, NT * 128], F32, kind="ExternalInput")
    gz = nc.dram_tensor("gz", [DKC, NT * 16], I32, kind="ExternalInput")
    uq = nc.dram_tensor("uq", [DKC * 128, NT * 16], I32, kind="ExternalInput")
    us = nc.dram_tensor("us", [DKC, NT * 128], F32, kind="ExternalInput")
    uz = nc.dram_tensor("uz", [DKC, NT * 16], I32, kind="ExternalInput")
    dq = nc.dram_tensor("dq", [I, DTL * 16], I32, kind="ExternalInput")
    ds = nc.dram_tensor("ds", [NT, DTL * 128], F32, kind="ExternalInput")
    dz = nc.dram_tensor("dz", [NT, DTL * 16], I32, kind="ExternalInput")
    wat = nc.dram_tensor("wat", [DKC * 128, 3 * D], F32, kind="ExternalInput")
    wpr = nc.dram_tensor("wpr", [DKC * 128, D], F32, kind="ExternalInput")
    bacols = nc.dram_tensor("bacols", [128, CT], F32, kind="ExternalInput")
    bpcols = nc.dram_tensor("bpcols", [128, DK], F32, kind="ExternalInput")
    ident_in = nc.dram_tensor("ident_in", [128, 128], BF16, kind="ExternalInput")
    masks_in = nc.dram_tensor("masks_in", [128, SLOTS * 2 * 128], BF16, kind="ExternalInput")
    qsb_in = nc.dram_tensor("qsb_in", [1, R], F32, kind="ExternalInput")
    kcol_in = nc.dram_tensor("kcol_in", [128, NQT], F32, kind="ExternalInput")

    yout = nc.dram_tensor("yout", [D, R], F32, kind="ExternalOutput")

    xres = nc.dram_tensor("xres", [D, R], F32)
    wg_sl = nc.dram_tensor("wg_sl", [DKC * NT * 128, 128], BF16)
    wg_all = nc.dram_tensor("wg_all", [DK * NT * 128, 128], BF16, addr_space="Shared")
    wu_sl = nc.dram_tensor("wu_sl", [DKC * NT * 128, 128], BF16)
    wu_all = nc.dram_tensor("wu_all", [DK * NT * 128, 128], BF16, addr_space="Shared")
    wd_sl = nc.dram_tensor("wd_sl", [NT * DTL * 128, 128], BF16)
    wd_all = nc.dram_tensor("wd_all", [NCORES * NT * DTL * 128, 128], BF16, addr_space="Shared")
    wat_sl = nc.dram_tensor("wat_sl", [DKC * CT * 128, 128], BF16)
    wat_all = nc.dram_tensor("wat_all", [DK * CT * 128, 128], BF16, addr_space="Shared")
    wpr_sl = nc.dram_tensor("wpr_sl", [DKC * DK * 128, 128], BF16)
    wpr_all = nc.dram_tensor("wpr_all", [DK * DK * 128, 128], BF16, addr_space="Shared")
    zs_g = nc.dram_tensor("zs_g", [DKC, NT * 128], F32)
    zs_u = nc.dram_tensor("zs_u", [DKC, NT * 128], F32)
    zs_d = nc.dram_tensor("zs_d", [NT, DTL * 128], F32)
    q_own = nc.dram_tensor("q_own", [D, R], BF16)
    kt_own = nc.dram_tensor("kt_own", [D, R], BF16)
    kt_all = nc.dram_tensor("kt_all", [2 * D, R], BF16)
    v_own = nc.dram_tensor("v_own", [R, D], BF16)
    v_all = nc.dram_tensor("v_all", [2 * R, D], BF16)
    m_bnc = nc.dram_tensor("m_bnc", [1, R], F32)
    r_bnc = nc.dram_tensor("r_bnc", [1, R], F32)
    c_bnc = nc.dram_tensor("c_bnc", [1, 1], F32)
    rh_bnc = nc.dram_tensor("rh_bnc", [H, R], F32)

    PAIRS = [[0, 1], [2, 3], [4, 5], [6, 7]]
    ALL8 = [list(range(NCORES))]

    with ExitStack() as st:
        _uid = [0]

        def un(n):
            _uid[0] += 1
            return f"{n}_{_uid[0]}"

        sb = lambda name, shape, dt: st.enter_context(nc.sbuf_tensor(name, shape, dt))
        sem = lambda name: Sem(st, nc, name)
        semp = lambda name: SemP(st, nc, name)

        acc = sb("acc", [128, DK * R], F16)
        m_rep = sb("m_rep", [128, R], F32)
        r_rep = sb("r_rep", [128, R], F32)
        ident = sb("ident_sb", [128, 128], BF16)
        ones_col = sb("ones_col", [128, 1], BF16)
        ones_row = sb("ones_row", [1, 128], BF16)
        ba_sb = sb("ba_sb", [128, CT], F32)
        bp_sb = sb("bp_sb", [128, DK], F32)
        lnrow_a = sb("lnrow_a", [1, R], F32)
        lnrow_b = sb("lnrow_b", [1, R], F32)

        s_deq = semp("s_deq")
        s_dx = semp("s_dx")
        s_dw = semp("s_dw")
        s_dwd = semp("s_dwd")
        s_out = semp("s_out")
        s_rb = semp("s_rb")
        s_sto = semp("s_sto")
        s_sv = semp("s_sv")
        s_zs = semp("s_zs")
        s_cld = semp("s_cld")
        s_pst = semp("s_pst")
        s_bnc = sem("s_bnc")
        s_g = sem("s_g"); s_u = sem("s_u"); s_eg = sem("s_eg"); s_eu = sem("s_eu")
        s_h = sem("s_h"); s_sil = sem("s_sil"); s_dmm = sem("s_dmm")
        s_dev = sem("s_dev"); s_dac = sem("s_dac"); s_sums = sem("s_sums")
        s_lnv = sem("s_lnv"); s_lna = sem("s_lna"); s_fp = sem("s_fp")
        s_q = sem("s_q"); s_qe = sem("s_qe"); s_e1 = sem("s_e1")
        s_vt = sem("s_vt"); s_vs = sem("s_vs"); s_pj = sem("s_pj")
        s_p1 = sem("s_p1"); s_m = sem("s_m"); s_st = sem("s_st")
        s_ex = sem("s_ex"); s_mk = sem("s_mk"); s_os = sem("s_os")
        s_pv = sem("s_pv"); s_rr = sem("s_rr"); s_pc = sem("s_pc")
        s_b2 = sem("s_b2"); s_dqv = sem("s_dqv"); ch_v = sem("ch_v")
        s_ba = sem("s_ba"); s_baa = sem("s_baa"); s_cpa = sem("s_cpa")
        s_cc = [sem(f"s_cc{i}") for i in range(8)]

        with nc.Block() as blk:
            @blk.sync
            def _(sy):
                s_deq.inc(0, sy.dma_start(ident[:, :], ident_in[:, :]))
                s_deq.inc(0, sy.dma_start(ba_sb[:, :], bacols[:, :]))
                s_deq.inc(0, sy.dma_start(bp_sb[:, :], bpcols[:, :]))

            @blk.vector
            def _(v):
                s_deq.wait_totals(v)
                v.memset(ones_col[:, :], 1.0)
                v.memset(ones_row[:, :], 1.0)

        zs_specs = [(gz, gs, zs_g, DKC, NT * 128, 1.0),
                    (uz, us, zs_u, DKC, NT * 128, 1.0),
                    (dz, ds, zs_d, NT, DTL * 128, 0.5)]
        with ExitStack() as ps:
            psb = lambda n, sh, dt: ps.enter_context(nc.sbuf_tensor(un(n), sh, dt))
            W0 = min(2752, max(NT * 128, DTL * 128))
            srow = [psb("srow", [128, W0], F32) for i in range(2)]
            zrow = [psb("zrow", [128, W0 // 8], I32) for i in range(2)]
            zi = psb("zi", [128, W0], I32)
            zf = psb("zf", [128, W0], F32)
            zs = [psb("zs", [128, W0], F32) for i in range(2)]
            groups = []
            for (zdr, sdr, odr, ngr, w, scl) in zs_specs:
                for c0 in range(0, w, W0):
                    groups.append((zdr, sdr, odr, ngr, w, c0,
                                   min(W0, w - c0), scl))
            with nc.Block() as blk:
                b_dqv = s_dqv.n
                b_zs = s_zs.totals()
                ld_t = {}
                zst_t = {}

                @blk.sync
                def _(sy):
                    for gi, (zdr, sdr, odr, nr, w, c0, cur, scl) in enumerate(groups):
                        if gi >= 2:
                            s_dqv.wait(sy, b_dqv + (gi - 1))
                        s_deq.inc(gi, sy.dma_start(
                            srow[gi % 2][0:nr, 0:cur],
                            bass.AP(sdr, c0, [[w, nr], [1, cur]])))
                        ld_t[gi] = s_deq.inc(gi, sy.dma_start(
                            zrow[gi % 2][0:nr, 0:cur // 8],
                            bass.AP(zdr, c0 // 8, [[w // 8, nr], [1, cur // 8]])))

                @blk.vector
                def _(v):
                    for gi, (zdr, sdr, odr, nr, w, c0, cur, scl) in enumerate(groups):
                        s_deq.wait(v, gi, ld_t[gi])
                        if gi >= 2:
                            s_zs.wait(v, gi, b_zs[gi % 2] + 16 * ((gi - 2) // 2 + 1))
                        b = gi % 2
                        t0 = 0
                        for pos in range(8):
                            sh = 4 * AWQ_ORDER[pos]
                            t0 = ch_v.inc(v.tensor_scalar(
                                bass.AP(zi, pos, [[W0, nr], [8, cur // 8]]),
                                zrow[b][0:nr, 0:cur // 8], sh, 15,
                                ALU.logical_shift_right, ALU.bitwise_and))
                        ch_v.wait(v, t0)
                        t1 = ch_v.inc(v.tensor_copy(zf[0:nr, 0:cur],
                                                    zi[0:nr, 0:cur]))
                        ch_v.wait(v, t1)
                        ins = v.tensor_tensor(zs[b][0:nr, 0:cur], zf[0:nr, 0:cur],
                                              srow[b][0:nr, 0:cur], ALU.mult)
                        if scl != 1.0:
                            t2 = ch_v.inc(ins)
                            ch_v.wait(v, t2)
                            ins = v.tensor_scalar(zs[b][0:nr, 0:cur],
                                                  zs[b][0:nr, 0:cur],
                                                  scl, None, ALU.mult)
                        s_dqv.inc(ins)

                @blk.scalar
                def _(sc):
                    for gi, (zdr, sdr, odr, nr, w, c0, cur, scl) in enumerate(groups):
                        s_dqv.wait(sc, b_dqv + (gi + 1))
                        zst_t[gi] = s_zs.inc(gi, sc.dma_start(
                            bass.AP(odr, c0, [[w, nr], [1, cur]]),
                            zs[gi % 2][0:nr, 0:cur]))
        zs_total = s_zs.totals()

        def emit_dequant_q(qdr, sdr, zsdr, odr, ngroups, width, scl, name,
                           want_casts=False):
            GB = max(1, min(ngroups, 2048 // width)) if width <= 1024 else 1
            CW = min(width * GB, 2560)
            with ExitStack() as ps:
                psb = lambda n, sh, dt: ps.enter_context(nc.sbuf_tensor(un(n), sh, dt))
                qwb = [psb("qw", [128, CW // 8], I32) for i in range(2)]
                srep = [psb("sr", [128, CW], F32) for i in range(2)]
                zsrep = [psb("zr", [128, CW], F32) for i in range(2)]
                ti = psb("ti", [128, CW], I32)
                tf = psb("tf", [128, CW], F32)
                wbf = [psb("wb", [128, CW], BF16) for i in range(2)]
                chunks = []
                for g0 in range(0, ngroups, GB):
                    gb = min(GB, ngroups - g0)
                    wtot = gb * width
                    for c0 in range(0, wtot, CW):
                        chunks.append((g0, gb, c0, min(CW, wtot - c0)))
                if want_casts:
                    cbase = cast_cur[0]
                    ctake = min(len(chunks), NCAST - cbase)
                    cast_cur[0] += ctake
                else:
                    cbase, ctake = 0, 0
                with nc.Block() as blk:
                    b_dqv = s_dqv.n
                    b_sto = s_sto.totals()
                    ld_t = {}

                    @blk.sync
                    def _(sy):
                        s_zs.wait_totals(sy, zs_total)
                        s_sto.wait_totals(sy, b_sto)
                        for ci, (g0, gb, c0, cur) in enumerate(chunks):
                            ct = cbase + ci
                            if ci < ctake:
                                (cwdr, codr, cgg, cnc, cc0, ccur) = cast_chunks[ct]
                                if ct >= 2:
                                    s_cpa.wait(sy, ct - 1)
                                cld_t[ct] = s_cld.inc(ct, sy.dma_start(
                                    pcf[ct % 2][:, 0:ccur],
                                    cwdr[cgg * 128:(cgg + 1) * 128,
                                         cc0:cc0 + ccur]))
                            if ci >= 2:
                                s_dqv.wait(sy, b_dqv + (ci - 1))
                            if gb > 1:
                                s_deq.inc(ci, sy.dma_start(
                                    bass.AP(qwb[ci % 2], 0,
                                            [[CW // 8, 128], [width // 8, gb],
                                             [1, width // 8]]),
                                    bass.AP(qdr, g0 * 128 * (width // 8),
                                            [[width // 8, 128],
                                             [128 * (width // 8), gb],
                                             [1, width // 8]])))
                            else:
                                s_deq.inc(ci, sy.dma_start(
                                    qwb[ci % 2][:, 0:cur // 8],
                                    bass.AP(qdr, g0 * 128 * (width // 8) + c0 // 8,
                                            [[width // 8, 128], [1, cur // 8]])))
                            s_deq.inc(ci, sy.dma_start(
                                srep[ci % 2][:, 0:cur],
                                bass.AP(sdr, g0 * width + c0, [[0, 128], [1, cur]])))
                            ld_t[ci] = s_deq.inc(ci, sy.dma_start(
                                zsrep[ci % 2][:, 0:cur],
                                bass.AP(zsdr, g0 * width + c0, [[0, 128], [1, cur]])))

                    @blk.vector
                    def _(v):
                        s_sto.wait_totals(v, b_sto)
                        for ci, (g0, gb, c0, cur) in enumerate(chunks):
                            s_deq.wait(v, ci, ld_t[ci])
                            if ci >= 2:
                                s_sto.wait(v, ci, b_sto[ci % 2] + 16 * (ci // 2))
                            b = ci % 2
                            t0 = 0
                            for pos in range(8):
                                sh = 4 * AWQ_ORDER[pos]
                                t0 = ch_v.inc(v.tensor_scalar(
                                    bass.AP(ti, pos, [[CW, 128], [8, cur // 8]]),
                                    qwb[b][:, 0:cur // 8], sh, 15,
                                    ALU.logical_shift_right, ALU.bitwise_and))
                            ch_v.wait(v, t0)
                            t1 = ch_v.inc(v.tensor_scalar(tf[:, 0:cur], ti[:, 0:cur],
                                                          scl, None, ALU.mult))
                            ch_v.wait(v, t1)
                            t2 = ch_v.inc(v.tensor_tensor(tf[:, 0:cur], tf[:, 0:cur],
                                                          srep[b][:, 0:cur], ALU.mult))
                            ch_v.wait(v, t2)
                            s_dqv.inc(v.tensor_tensor(wbf[b][:, 0:cur], tf[:, 0:cur],
                                                      zsrep[b][:, 0:cur], ALU.subtract))

                    @blk.scalar
                    def _(sc):
                        s_sto.wait_totals(sc, b_sto)
                        for ci, (g0, gb, c0, cur) in enumerate(chunks):
                            s_dqv.wait(sc, b_dqv + (ci + 1))
                            nt0 = g0 * (width // 128) + c0 // 128
                            nts = cur // 128
                            s_sto.inc(ci, sc.dma_start(
                                bass.AP(odr, nt0 * TS,
                                        [[128, 128], [TS, nts], [1, 128]]),
                                bass.AP(wbf[ci % 2], 0,
                                        [[CW, 128], [128, nts], [1, 128]])))
                            ct = cbase + ci
                            if ci < ctake:
                                (cwdr, codr, cgg, cnc, cc0, ccur) = cast_chunks[ct]
                                s_cld.wait(sc, ct, cld_t[ct])
                                if ct >= 2:
                                    s_pst.wait(sc, ct, pst_t[ct - 2])
                                t9 = s_cpa.inc(sc.activation(
                                    pcb[ct % 2][:, 0:ccur],
                                    pcf[ct % 2][:, 0:ccur], AF.Copy))
                                s_cpa.wait(sc, t9)
                                pst_t[ct] = s_pst.inc(ct, sc.dma_start(
                                    bass.AP(codr,
                                            (cgg * cnc + cc0 // 128) * TS,
                                            [[128, 128], [TS, ccur // 128],
                                             [1, 128]]),
                                    bass.AP(pcb[ct % 2], 0,
                                            [[CWP, 128], [128, ccur // 128],
                                             [1, 128]])))

        def emit_cast(wdr, odr, ncols, name):
            CW = min(ncols * 128, 6144)
            nch = (ncols * 128 + CW - 1) // CW
            with ExitStack() as ps:
                psb = lambda n, sh, dt: ps.enter_context(nc.sbuf_tensor(un(n), sh, dt))
                wf = [psb("wf", [128, CW], F32) for i in range(2)]
                wb = [psb("wb", [128, CW], BF16) for i in range(2)]
                chunks = [(g, ch) for g in range(DKC) for ch in range(nch)]
                with nc.Block() as blk:
                    b_dqv = s_dqv.n
                    b_sto = s_sto.totals()
                    ld_t = {}

                    @blk.sync
                    def _(sy):
                        s_sto.wait_totals(sy, b_sto)
                        for ci, (g, ch) in enumerate(chunks):
                            c0 = ch * CW
                            cur = min(CW, ncols * 128 - c0)
                            if ci >= 2:
                                s_dqv.wait(sy, b_dqv + (ci - 1))
                            ld_t[ci] = s_deq.inc(ci, sy.dma_start(
                                wf[ci % 2][:, 0:cur],
                                wdr[g * 128:(g + 1) * 128, c0:c0 + cur]))

                    @blk.vector
                    def _(v):
                        s_sto.wait_totals(v, b_sto)
                        for ci, (g, ch) in enumerate(chunks):
                            c0 = ch * CW
                            cur = min(CW, ncols * 128 - c0)
                            s_deq.wait(v, ci, ld_t[ci])
                            if ci >= 2:
                                s_sto.wait(v, ci, b_sto[ci % 2] + 16 * (ci // 2))
                            s_dqv.inc(v.tensor_copy(wb[ci % 2][:, 0:cur],
                                                    wf[ci % 2][:, 0:cur]))

                    @blk.scalar
                    def _(sc):
                        s_sto.wait_totals(sc, b_sto)
                        for ci, (g, ch) in enumerate(chunks):
                            c0 = ch * CW
                            cur = min(CW, ncols * 128 - c0)
                            s_dqv.wait(sc, b_dqv + (ci + 1))
                            nt0 = c0 // 128
                            nts = cur // 128
                            s_sto.inc(ci, sc.dma_start(
                                bass.AP(odr, (g * ncols + nt0) * TS,
                                        [[128, 128], [TS, nts], [1, 128]]),
                                bass.AP(wb[ci % 2], 0,
                                        [[CW, 128], [128, nts], [1, 128]])))

        def issue_ag(cci, src_t, dst_t):
            done = s_sto.totals()
            with nc.Block() as blk:
                @blk.gpsimd
                def _(g):
                    s_sto.wait_totals(g, done)
                    s_cc[cci].inc(g.collective_compute(
                        "AllGather", ALU.bypass, replica_groups=ALL8,
                        ins=[src_t.ap().opt()], outs=[dst_t.ap().opt()]))

        CWP = 2048
        cast_chunks = []
        for (cwdr, codr, cnc) in ((wat, wat_sl, CT), (wpr, wpr_sl, DK)):
            for cgg in range(DKC):
                for cch in range((cnc * 128 + CWP - 1) // CWP):
                    cc0 = cch * CWP
                    cast_chunks.append((cwdr, codr, cgg, cnc, cc0,
                                        min(CWP, cnc * 128 - cc0)))
        NCAST = len(cast_chunks)
        cld_t, pst_t = {}, {}
        cast_cur = [0]
        with ExitStack() as cast_ps:
            pcf = [cast_ps.enter_context(
                nc.sbuf_tensor(un("pcf"), [128, CWP], F32)) for i in range(2)]
            pcb = [cast_ps.enter_context(
                nc.sbuf_tensor(un("pcb"), [128, CWP], BF16)) for i in range(2)]
            emit_dequant_q(gq, gs, zs_g, wg_sl, DKC, NT * 128, 1.0, "g",
                           want_casts=True)
            issue_ag(0, wg_sl, wg_all)
            emit_dequant_q(uq, us, zs_u, wu_sl, DKC, NT * 128, 1.0, "u",
                           want_casts=True)
            issue_ag(1, wu_sl, wu_all)
            emit_dequant_q(dq, ds, zs_d, wd_sl, NT, DTL * 128, 0.5, "d",
                           want_casts=True)
            issue_ag(2, wd_sl, wd_all)
            assert cast_cur[0] == NCAST, (cast_cur[0], NCAST)
            with nc.Block() as blk:
                @blk.gpsimd
                def _(g):
                    s_pst.wait_totals(g, s_pst.totals())
                    s_cc[3].inc(g.collective_compute(
                        "AllGather", ALU.bypass, replica_groups=ALL8,
                        ins=[wat_sl.ap().opt()], outs=[wat_all.ap().opt()]))
                    s_cc[4].inc(g.collective_compute(
                        "AllGather", ALU.bypass, replica_groups=ALL8,
                        ins=[wpr_sl.ap().opt()], outs=[wpr_all.ap().opt()]))

        def emit_casts_late():
            pass

        def emit_pass1(src, src_wait=None):
            with ExitStack() as ps:
                psb = lambda n, sh, dt: ps.enter_context(nc.sbuf_tensor(un(n), sh, dt))
                xt = [psb("p1x", [128, R], F32) for i in range(2)]
                tb = [psb("p1b", [128, R], BF16) for i in range(2)]
                with nc.Block() as blk, \
                        nc.psum_tensor(un("ps_sumA"), [1, RH], F32) as ps_sumA, \
                        nc.psum_tensor(un("ps_sumB"), [1, RH], F32) as ps_sumB:
                    b_lnv, b_sums, b_bnc = s_lnv.n, s_sums.n, s_bnc.n
                    ld_t = {}

                    @blk.sync
                    def _(sy):
                        if src_wait is not None:
                            s_out.wait_totals(sy, src_wait)
                        for dt in range(DK):
                            if dt >= 2:
                                s_lnv.wait(sy, b_lnv + dt - 1)
                            ld_t[dt] = s_dx.inc(dt, sy.dma_start(
                                xt[dt % 2][:, :], src[dt * 128:(dt + 1) * 128, :]))

                    @blk.vector
                    def _(v):
                        for dt in range(DK):
                            s_dx.wait(v, dt, ld_t[dt])
                            if dt >= 2:
                                s_sums.wait(v, b_sums + dt - 1)
                            s_lnv.inc(v.tensor_copy(tb[dt % 2][:, :], xt[dt % 2][:, :]))
                        s_sums.wait(v, b_sums + DK)
                        t9 = ch_v.inc(v.tensor_scalar(lnrow_a[0:1, 0:RH], ps_sumA[0:1, :],
                                                      -1.0 / D, None, ALU.mult))
                        ch_v.wait(v, t9)
                        s_lnv.inc(v.tensor_scalar(lnrow_a[0:1, RH:R], ps_sumB[0:1, :],
                                                  -1.0 / D, None, ALU.mult))

                    @blk.tensor
                    def _(t):
                        for dt in range(DK):
                            s_lnv.wait(t, b_lnv + dt + 1)
                            t.matmul(ps_sumA[0:1, :], ones_col[:, 0:1],
                                     tb[dt % 2][:, 0:RH], start=(dt == 0),
                                     stop=(dt == DK - 1))
                            s_sums.inc(t.matmul(ps_sumB[0:1, :], ones_col[:, 0:1],
                                                tb[dt % 2][:, RH:R], start=(dt == 0),
                                                stop=(dt == DK - 1)))

                    @blk.scalar
                    def _(sc):
                        s_lnv.wait(sc, b_lnv + DK + 1)
                        ch_v.wait(sc, 0)
                        s_bnc.inc(sc.dma_start(m_bnc[0:1, :], lnrow_a[0:1, :]), 16)
                        s_bnc.wait(sc, b_bnc + 16)
                        s_bnc.inc(sc.dma_start(
                            m_rep[:, :], bass.AP(m_bnc, 0, [[0, 128], [1, R]])), 16)
            return s_bnc.n

        def emit_pass2(src, xb, mrep_t):
            with ExitStack() as ps:
                psb = lambda n, sh, dt: ps.enter_context(nc.sbuf_tensor(un(n), sh, dt))
                xt = [psb("p2x", [128, R], F32) for i in range(2)]
                sq = [psb("p2s", [128, R], BF16) for i in range(2)]
                with nc.Block() as blk, \
                        nc.psum_tensor(un("ps_ssqA"), [1, RH], F32) as ps_ssqA, \
                        nc.psum_tensor(un("ps_ssqB"), [1, RH], F32) as ps_ssqB:
                    b_lnv, b_lna, b_sums = s_lnv.n, s_lna.n, s_sums.n
                    ld_t = {}

                    @blk.sync
                    def _(sy):
                        for dt in range(DK):
                            if dt >= 2:
                                s_lnv.wait(sy, b_lnv + dt - 1)
                            ld_t[dt] = s_dx.inc(dt, sy.dma_start(
                                xt[dt % 2][:, :], src[dt * 128:(dt + 1) * 128, :]))

                    @blk.vector
                    def _(v):
                        s_bnc.wait(v, mrep_t)
                        for dt in range(DK):
                            s_dx.wait(v, dt, ld_t[dt])
                            s_lnv.inc(v.tensor_tensor(xb[:, dt * R:(dt + 1) * R],
                                                      xt[dt % 2][:, :], m_rep[:, :],
                                                      ALU.add))
                        s_sums.wait(v, b_sums + DK)
                        t9 = ch_v.inc(v.tensor_scalar(lnrow_a[0:1, 0:RH], ps_ssqA[0:1, :],
                                                      1.0 / D, EPS, ALU.mult, ALU.add))
                        ch_v.wait(v, t9)
                        s_lnv.inc(v.tensor_scalar(lnrow_a[0:1, RH:R], ps_ssqB[0:1, :],
                                                  1.0 / D, EPS, ALU.mult, ALU.add))

                    @blk.scalar
                    def _(sc):
                        for dt in range(DK):
                            s_lnv.wait(sc, b_lnv + dt + 1)
                            if dt >= 2:
                                s_sums.wait(sc, b_sums + dt - 1)
                            s_lna.inc(sc.activation(sq[dt % 2][:, :],
                                                    xb[:, dt * R:(dt + 1) * R],
                                                    AF.Square))
                        s_lnv.wait(sc, b_lnv + DK + 1)
                        s_lna.inc(sc.activation(lnrow_b[0:1, :], lnrow_a[0:1, :],
                                                AF.Sqrt))

                    @blk.tensor
                    def _(t):
                        for dt in range(DK):
                            s_lna.wait(t, b_lna + dt + 1)
                            t.matmul(ps_ssqA[0:1, :], ones_col[:, 0:1],
                                     sq[dt % 2][:, 0:RH], start=(dt == 0),
                                     stop=(dt == DK - 1))
                            s_sums.inc(t.matmul(ps_ssqB[0:1, :], ones_col[:, 0:1],
                                                sq[dt % 2][:, RH:R], start=(dt == 0),
                                                stop=(dt == DK - 1)))

                with nc.Block() as blk:
                    b_lnv2, b_bnc2 = s_lnv.n, s_bnc.n

                    @blk.vector
                    def _(v):
                        s_lna.wait(v)
                        s_lnv.inc(v.reciprocal(lnrow_a[0:1, :], lnrow_b[0:1, :]))

                    @blk.scalar
                    def _(sc):
                        s_lnv.wait(sc, b_lnv2 + 1)
                        s_bnc.inc(sc.dma_start(r_bnc[0:1, :], lnrow_a[0:1, :]), 16)
                        s_bnc.wait(sc, b_bnc2 + 16)
                        s_bnc.inc(sc.dma_start(
                            r_rep[:, :], bass.AP(r_bnc, 0, [[0, 128], [1, R]])), 16)

                rrep_t = s_bnc.n
                with nc.Block() as blk:
                    @blk.vector
                    def _(v):
                        s_bnc.wait(v, rrep_t)
                        for dt in range(DK):
                            v.tensor_tensor(xb[:, dt * R:(dt + 1) * R],
                                            xb[:, dt * R:(dt + 1) * R],
                                            r_rep[:, :], ALU.mult)
            return s_bnc.n

        def emit_final(src, dst, src_wait=None):
            with ExitStack() as ps:
                psb = lambda n, sh, dt: ps.enter_context(nc.sbuf_tensor(un(n), sh, dt))
                xt = [psb("fpx", [128, R], F32) for i in range(2)]
                to = [psb("fpo", [128, R], F32) for i in range(2)]
                with nc.Block() as blk:
                    b_fp = s_fp.n
                    b_out = s_out.totals()
                    ld_t = {}

                    @blk.sync
                    def _(sy):
                        if src_wait is not None:
                            s_out.wait_totals(sy, src_wait)
                        for dt in range(DK):
                            if dt >= 2:
                                s_fp.wait(sy, b_fp + dt - 1)
                            ld_t[dt] = s_dx.inc(dt, sy.dma_start(
                                xt[dt % 2][:, :], src[dt * 128:(dt + 1) * 128, :]))
                            if dt >= 1:
                                s_fp.wait(sy, b_fp + dt)
                                s_out.inc(dt - 1, sy.dma_start(
                                    dst[(dt - 1) * 128:dt * 128, :],
                                    to[(dt - 1) % 2][:, :]))
                        s_fp.wait(sy, b_fp + DK)
                        s_out.inc(DK - 1, sy.dma_start(
                            dst[(DK - 1) * 128:DK * 128, :], to[(DK - 1) % 2][:, :]))

                    @blk.vector
                    def _(v):
                        for dt in range(DK):
                            s_dx.wait(v, dt, ld_t[dt])
                            if dt >= 2:
                                s_out.wait(v, dt, b_out[dt % 2] + 16 * (dt // 2))
                            s_fp.inc(v.tensor_tensor(to[dt % 2][:, :], xt[dt % 2][:, :],
                                                     acc[:, dt * R:(dt + 1) * R],
                                                     ALU.add))
            return s_out.totals()

        def emit_ffn(xb, rrep_t):
            with ExitStack() as ps:
                psb = lambda n, sh, dt: ps.enter_context(nc.sbuf_tensor(un(n), sh, dt))
                gsl = [psb("gsl", [128, DK * 128], BF16) for i in range(2)]
                usl = [psb("usl", [128, DK * 128], BF16) for i in range(2)]
                dsl = [psb("dsl", [128, SNT * DPB * 128], BF16) for i in range(2)]
                hstg = [psb("hst", [128, SNT * R], BF16) for i in range(2)]
                sil = [psb("sil", [128, R], BF16) for i in range(2)]
                tmp_u = [psb("tmu", [128, R], BF16) for i in range(2)]
                tmp_d = [psb("tmd", [128, R], BF16) for i in range(2)]
                with ExitStack() as pps, nc.Block() as blk:
                    P_gA = pps.enter_context(nc.psum_tensor(un("P_gA"), [128, RH], F32))
                    P_gB = pps.enter_context(nc.psum_tensor(un("P_gB"), [128, RH], F32))
                    P_uA = pps.enter_context(nc.psum_tensor(un("P_uA"), [128, RH], F32))
                    P_uB = pps.enter_context(nc.psum_tensor(un("P_uB"), [128, RH], F32))
                    P_dA = [pps.enter_context(nc.psum_tensor(un("P_dA"), [128, RH], F32))
                            for i in range(2)]
                    P_dB = [pps.enter_context(nc.psum_tensor(un("P_dB"), [128, RH], F32))
                            for i in range(2)]
                    bs = dict(g=s_g.n, u=s_u.n, eg=s_eg.n, eu=s_eu.n, h=s_h.n,
                              sil=s_sil.n, dmm=s_dmm.n, dev=s_dev.n, dac=s_dac.n)
                    w_t, wd_t = {}, {}

                    @blk.sync
                    def _(sy):
                        s_cc[0].wait(sy, 1)

                        def down_slabs(stg):
                            for b in range(DBLK):
                                j2 = stg * DBLK + b
                                if j2 == 0:
                                    s_cc[2].wait(sy, 1)
                                if j2 >= 2:
                                    s_dmm.wait(sy, bs["dmm"] + DPB * (j2 - 1))
                                for cc in range(CPB):
                                    c = b * CPB + cc
                                    wd_t[j2] = s_dwd.inc(j2, sy.dma_start(
                                        bass.AP(dsl[j2 % 2], cc * SNT * DTL * 128,
                                                [[SNT * DPB * 128, 128],
                                                 [128, SNT * DTL], [1, 128]]),
                                        bass.AP(wd_all, ((c * NT + stg * SNT) * DTL) * TS,
                                                [[128, 128], [TS, SNT * DTL], [1, 128]])))

                        for stg in range(NSTG):
                            for nt in range(stg * SNT, (stg + 1) * SNT):
                                if nt >= 2:
                                    s_g.wait(sy, bs["g"] + nt - 1)
                                    s_u.wait(sy, bs["u"] + nt - 1)
                                for hf in range(2):
                                    s_dw.inc(nt, sy.dma_start(
                                        bass.AP(gsl[nt % 2], hf * KCH * 128,
                                                [[DK * 128, 128], [128, KCH], [1, 128]]),
                                        bass.AP(wg_all, (hf * KCH * NT + nt) * TS,
                                                [[128, 128], [NT * TS, KCH], [1, 128]])))
                                if nt == 0:
                                    s_cc[1].wait(sy, 1)
                                for hf in range(2):
                                    w_t[nt] = s_dw.inc(nt, sy.dma_start(
                                        bass.AP(usl[nt % 2], hf * KCH * 128,
                                                [[DK * 128, 128], [128, KCH], [1, 128]]),
                                        bass.AP(wu_all, (hf * KCH * NT + nt) * TS,
                                                [[128, 128], [NT * TS, KCH], [1, 128]])))
                            if stg >= 1:
                                down_slabs(stg - 1)
                        down_slabs(NSTG - 1)

                    @blk.tensor
                    def _(t):
                        jj = [0]

                        def down_groups(stg):
                            for b in range(DBLK):
                                j2 = stg * DBLK + b
                                for dtb in range(DPB):
                                    j = jj[0]
                                    s_dwd.wait(t, j2, wd_t[j2])
                                    s_h.wait(t, bs["h"] + (stg + 1) * SNT)
                                    if j >= 2:
                                        s_dev.wait(t, bs["dev"] + 2 * (j - 1))
                                    ins = None
                                    for il in range(SNT):
                                        dcol = ((dtb // DTL) * SNT * DTL
                                                + il * DTL + (dtb % DTL))
                                        lhsT = dsl[j2 % 2][:, dcol * 128:
                                                           (dcol + 1) * 128]
                                        t.matmul(P_dA[j % 2][:, :], lhsT,
                                                 hstg[stg % 2][:, il * R: il * R + RH],
                                                 start=(il == 0), stop=(il == SNT - 1))
                                        ins = t.matmul(
                                            P_dB[j % 2][:, :], lhsT,
                                            hstg[stg % 2][:, il * R + RH:(il + 1) * R],
                                            start=(il == 0), stop=(il == SNT - 1))
                                    s_dmm.inc(ins)
                                    jj[0] += 1

                        for stg in range(NSTG):
                            for nt in range(stg * SNT, (stg + 1) * SNT):
                                s_dw.wait(t, nt, w_t[nt])
                                if nt >= 1:
                                    s_sil.wait(t, bs["sil"] + 2 * nt)
                                ins = None
                                for kc in range(DK):
                                    lhsT = gsl[nt % 2][:, kc * 128:(kc + 1) * 128]
                                    t.matmul(P_gA[:, :], lhsT,
                                             xb[:, kc * R: kc * R + RH],
                                             start=(kc == 0), stop=(kc == DK - 1))
                                    ins = t.matmul(P_gB[:, :], lhsT,
                                                   xb[:, kc * R + RH:(kc + 1) * R],
                                                   start=(kc == 0), stop=(kc == DK - 1))
                                s_g.inc(ins)
                                if nt >= 1:
                                    s_eu.wait(t, bs["eu"] + 2 * nt)
                                for kc in range(DK):
                                    lhsT = usl[nt % 2][:, kc * 128:(kc + 1) * 128]
                                    t.matmul(P_uA[:, :], lhsT,
                                             xb[:, kc * R: kc * R + RH],
                                             start=(kc == 0), stop=(kc == DK - 1))
                                    ins = t.matmul(P_uB[:, :], lhsT,
                                                   xb[:, kc * R + RH:(kc + 1) * R],
                                                   start=(kc == 0), stop=(kc == DK - 1))
                                s_u.inc(ins)
                            if stg >= 1:
                                down_groups(stg - 1)
                        down_groups(NSTG - 1)

                    @blk.scalar
                    def _(sc):
                        jd = [0]

                        def devacs(stg):
                            for b in range(DBLK):
                                for dtb in range(DPB):
                                    j = jd[0]
                                    s_dmm.wait(sc, bs["dmm"] + j + 1)
                                    if j >= 2:
                                        s_dac.wait(sc, bs["dac"] + j - 1)
                                    s_dev.inc(sc.activation(tmp_d[j % 2][:, 0:RH],
                                                            P_dA[j % 2][:, :], AF.Copy))
                                    s_dev.inc(sc.activation(tmp_d[j % 2][:, RH:R],
                                                            P_dB[j % 2][:, :], AF.Copy))
                                    jd[0] += 1

                        for stg in range(NSTG):
                            for nt in range(stg * SNT, (stg + 1) * SNT):
                                s_g.wait(sc, bs["g"] + nt + 1)
                                if nt >= 2:
                                    s_h.wait(sc, bs["h"] + nt - 1)
                                s_sil.inc(sc.activation(sil[nt % 2][:, 0:RH],
                                                        P_gA[:, :], AF.Silu))
                                s_sil.inc(sc.activation(sil[nt % 2][:, RH:R],
                                                        P_gB[:, :], AF.Silu))
                                s_u.wait(sc, bs["u"] + nt + 1)
                                s_eu.inc(sc.activation(tmp_u[nt % 2][:, 0:RH],
                                                       P_uA[:, :], AF.Copy))
                                s_eu.inc(sc.activation(tmp_u[nt % 2][:, RH:R],
                                                       P_uB[:, :], AF.Copy))
                            if stg >= 1:
                                devacs(stg - 1)
                        devacs(NSTG - 1)

                    @blk.vector
                    def _(v):
                        ja = [0]

                        def daccs(stg):
                            for b in range(DBLK):
                                for dtb in range(DPB):
                                    j = ja[0]
                                    gdt = b * DPB + dtb
                                    s_dev.wait(v, bs["dev"] + 2 * (j + 1))
                                    if stg == 0:
                                        ins = v.tensor_copy(acc[:, gdt * R:(gdt + 1) * R],
                                                            tmp_d[j % 2][:, :])
                                    else:
                                        ins = v.tensor_tensor(
                                            acc[:, gdt * R:(gdt + 1) * R],
                                            acc[:, gdt * R:(gdt + 1) * R],
                                            tmp_d[j % 2][:, :], ALU.add)
                                    s_dac.inc(ins)
                                    ja[0] += 1

                        for stg in range(NSTG):
                            for nt in range(stg * SNT, (stg + 1) * SNT):
                                s_sil.wait(v, bs["sil"] + 2 * (nt + 1))
                                s_eu.wait(v, bs["eu"] + 2 * (nt + 1))
                                stg_of = nt // SNT
                                if stg_of >= 2:
                                    s_dmm.wait(v, bs["dmm"] + GPS * (stg_of - 1))
                                il = nt - stg_of * SNT
                                s_h.inc(v.tensor_tensor(
                                    hstg[stg_of % 2][:, il * R:(il + 1) * R],
                                    sil[nt % 2][:, :], tmp_u[nt % 2][:, :], ALU.mult))
                            if stg >= 1:
                                daccs(stg - 1)
                        daccs(NSTG - 1)

        def emit_qkv(xb, rrep_t):
            VCT0 = 2 * DK
            cts = list(range(DK, 2 * DK)) + list(range(DK)) + list(range(2 * DK, CT))
            with ExitStack() as ps:
                psb = lambda n, sh, dt: ps.enter_context(nc.sbuf_tensor(un(n), sh, dt))
                asl = [psb("asl", [128, DK * 128], BF16) for i in range(2)]
                qkt = [psb("qkt", [128, R], BF16) for i in range(2)]
                vti = [psb("vti", [128, R], BF16) for i in range(2)]
                vst = [psb("vst", [128, R], BF16) for i in range(2)]
                with ExitStack() as pps, nc.Block() as blk:
                    PA = [pps.enter_context(nc.psum_tensor(un("PqA"), [128, RH], F32))
                          for i in range(2)]
                    PB = [pps.enter_context(nc.psum_tensor(un("PqB"), [128, RH], F32))
                          for i in range(2)]
                    P_t = pps.enter_context(nc.psum_tensor(un("P_t"), [128, R], BF16))
                    b = dict(q=s_q.n, qe=s_qe.n, e1=s_e1.n, vt=s_vt.n, vs=s_vs.n)
                    kst_t = [None]
                    b_sto = s_sto.totals()
                    kst_t2 = (
                        (b_sto[0] + 16 * ((DK + 1) // 2), b_sto[1] + 16 * (DK // 2)),
                        (s_sv.totals()[0] + 16 * ((DK + 1) // 2),
                         s_sv.totals()[1] + 16 * (DK // 2)),
                    )
                    b_sv = s_sv.totals()
                    wat_t = {}

                    @blk.sync
                    def _(sy):
                        s_cc[3].wait(sy, 1)
                        for pi, ct in enumerate(cts):
                            if pi >= 2:
                                s_q.wait(sy, b["q"] + pi - 1)
                            for hf in range(2):
                                wat_t[pi] = s_dw.inc(pi, sy.dma_start(
                                    bass.AP(asl[pi % 2], hf * KCH * 128,
                                            [[DK * 128, 128], [128, KCH], [1, 128]]),
                                    bass.AP(wat_all, (hf * KCH * CT + ct) * TS,
                                            [[128, 128], [CT * TS, KCH], [1, 128]])))

                    @blk.tensor
                    def _(t):
                        def transposes(vi):
                            vct = VCT0 + vi
                            s_qe.wait(t, b["qe"] + vct + 1)
                            if vi >= 1:
                                s_vs.wait(t, b["vs"] + vi)
                            ins = None
                            for sl in range(SLOTS):
                                ins = t.transpose(P_t[:, sl * 128:(sl + 1) * 128],
                                                  vti[vi % 2][:, sl * 128:(sl + 1) * 128],
                                                  ident[:, :])
                            s_vt.inc(ins)

                        for pi, ct in enumerate(cts):
                            s_dw.wait(t, pi, wat_t[pi])
                            if pi >= 2:
                                s_qe.wait(t, b["qe"] + pi - 1)
                            ins = None
                            for kc in range(DK):
                                lhsT = asl[pi % 2][:, kc * 128:(kc + 1) * 128]
                                t.matmul(PA[pi % 2][:, :], lhsT,
                                         xb[:, kc * R: kc * R + RH],
                                         start=(kc == 0), stop=(kc == DK - 1))
                                ins = t.matmul(PB[pi % 2][:, :], lhsT,
                                               xb[:, kc * R + RH:(kc + 1) * R],
                                               start=(kc == 0), stop=(kc == DK - 1))
                            s_q.inc(ins)
                            if pi > VCT0:
                                transposes(pi - 1 - VCT0)
                        transposes(DK - 1)

                    @blk.vector
                    def _(v):
                        s_bnc.wait(v, rrep_t)

                        def vstrip(vi):
                            s_vt.wait(v, b["vt"] + vi + 1)
                            if vi >= 2:
                                s_sv.wait(v, vi, b_sv[vi % 2] + 16 * (vi // 2))
                            s_vs.inc(v.tensor_copy(vst[vi % 2][:, :], P_t[:, :]))

                        for pi, ct in enumerate(cts):
                            s_q.wait(v, b["q"] + pi + 1)
                            qsc = (1.0 / 128.0) if ct < DK else 1.0
                            if pi < VCT0:
                                if pi >= 2:
                                    s_sto.wait(v, pi, b_sto[pi % 2] + 16 * (pi // 2))
                                dst = qkt[pi % 2]
                            else:
                                vi = pi - VCT0
                                if vi >= 2:
                                    s_vt.wait(v, b["vt"] + vi - 1)
                                dst = vti[vi % 2]
                            t1 = ch_v.inc(v.tensor_scalar(dst[:, 0:RH], PA[pi % 2][:, :],
                                                          ba_sb[:, ct:ct + 1], qsc,
                                                          ALU.add, ALU.mult))
                            ch_v.wait(v, t1)
                            s_qe.inc(v.tensor_scalar(dst[:, RH:R], PB[pi % 2][:, :],
                                                     ba_sb[:, ct:ct + 1], qsc,
                                                     ALU.add, ALU.mult))
                            if pi > VCT0 + 1:
                                vstrip(pi - 2 - VCT0)
                        vstrip(DK - 2)
                        vstrip(DK - 1)

                    @blk.gpsimd
                    def _(g):
                        kk = kst_t2[0]
                        s_sto.wait_totals(g, kk)
                        s_cc[5].inc(g.collective_compute(
                            "AllGather", ALU.bypass, replica_groups=PAIRS,
                            ins=[kt_own.ap().opt()], outs=[kt_all.ap().opt()]))
                        s_sv.wait_totals(g, kst_t2[1])
                        s_cc[6].inc(g.collective_compute(
                            "AllGather", ALU.bypass, replica_groups=PAIRS,
                            ins=[v_own.ap().opt()], outs=[v_all.ap().opt()]))

                    @blk.scalar
                    def _(sc):
                        s_sto.wait_totals(sc, b_sto)
                        s_sv.wait_totals(sc, b_sv)
                        for pi in range(VCT0):
                            ct = cts[pi]
                            s_qe.wait(sc, b["qe"] + pi + 1)
                            dstd = q_own if ct < DK else kt_own
                            r0 = (ct if ct < DK else ct - DK) * 128
                            s_sto.inc(pi, sc.dma_start(dstd[r0:r0 + 128, :],
                                                       qkt[pi % 2][:, :]))
                            if pi == DK - 1:
                                kst_t[0] = s_sto.totals()
                        for vi in range(DK):
                            s_vs.wait(sc, b["vs"] + vi + 1)
                            s_sv.inc(vi, sc.dma_start(
                                bass.AP(v_own, vi * 128,
                                        [[D, 128], [128 * D, SLOTS], [1, 128]]),
                                bass.AP(vst[vi % 2], 0,
                                        [[R, 128], [128, SLOTS], [1, 128]])))
            return kst_t[0], s_sto.totals(), s_sv.totals()

        def emit_proj(attn_sb):
            with ExitStack() as ps:
                psb = lambda n, sh, dt: ps.enter_context(nc.sbuf_tensor(un(n), sh, dt))
                psl = [psb("psl", [128, DK * 128], BF16) for i in range(2)]
                with ExitStack() as pps, nc.Block() as blk:
                    PA = [pps.enter_context(nc.psum_tensor(un("PpA"), [128, RH], F32))
                          for i in range(2)]
                    PB = [pps.enter_context(nc.psum_tensor(un("PpB"), [128, RH], F32))
                          for i in range(2)]
                    b = dict(q=s_q.n, pj=s_pj.n)
                    wpr_t = {}

                    @blk.sync
                    def _(sy):
                        s_cc[4].wait(sy, 1)
                        for ct in range(DK):
                            if ct >= 2:
                                s_q.wait(sy, b["q"] + ct - 1)
                            for hf in range(2):
                                wpr_t[ct] = s_dw.inc(ct, sy.dma_start(
                                    bass.AP(psl[ct % 2], hf * KCH * 128,
                                            [[DK * 128, 128], [128, KCH], [1, 128]]),
                                    bass.AP(wpr_all, (hf * KCH * DK + ct) * TS,
                                            [[128, 128], [DK * TS, KCH], [1, 128]])))

                    @blk.tensor
                    def _(t):
                        for ct in range(DK):
                            s_dw.wait(t, ct, wpr_t[ct])
                            if ct >= 2:
                                s_pj.wait(t, b["pj"] + 2 * (ct - 1))
                            ins = None
                            for kc in range(DK):
                                lhsT = psl[ct % 2][:, kc * 128:(kc + 1) * 128]
                                t.matmul(PA[ct % 2][:, :], lhsT,
                                         attn_sb[:, kc * R: kc * R + RH],
                                         start=(kc == 0), stop=(kc == DK - 1))
                                ins = t.matmul(PB[ct % 2][:, :], lhsT,
                                               attn_sb[:, kc * R + RH:(kc + 1) * R],
                                               start=(kc == 0), stop=(kc == DK - 1))
                            s_q.inc(ins)

                    @blk.scalar
                    def _(sc):
                        for ct in range(DK):
                            s_q.wait(sc, b["q"] + ct + 1)
                            s_pj.inc(sc.activation(acc[:, ct * R: ct * R + RH],
                                                   PA[ct % 2][:, :], AF.Identity,
                                                   bias=bp_sb[:, ct:ct + 1]))
                            s_pj.inc(sc.activation(acc[:, ct * R + RH:(ct + 1) * R],
                                                   PB[ct % 2][:, :], AF.Identity,
                                                   bias=bp_sb[:, ct:ct + 1]))


        with nc.sbuf_tensor(un("xb1"), [128, DK * R], BF16) as xb:
            mrep_t = emit_pass1(xin)
            rrep_t = emit_pass2(xin, xb, mrep_t)
            emit_ffn(xb, rrep_t)
        fin1_t = emit_final(xin, xres)

        if debug_stop == "ffn1":
            with nc.Block() as blk:
                @blk.sync
                def _(sy):
                    s_out.wait_totals(sy, fin1_t)
            return nc

        with nc.sbuf_tensor(un("xb2"), [128, DK * R], BF16) as xb:
            mrep_t = emit_pass1(xres, src_wait=fin1_t)
            rrep_t = emit_pass2(xres, xb, mrep_t)
            kv_k_t, kv_sto_t, kv_sv_t = emit_qkv(xb, rrep_t)

        with ExitStack() as ascope:
            attn_sb = ascope.enter_context(
                nc.sbuf_tensor(un("attn_sb"), [128, DK * R], BF16))
            inner = ascope.enter_context(ExitStack())
            asb = lambda name, shape, dt: inner.enter_context(
                nc.sbuf_tensor(un(name), shape, dt))
            masks = asb("masks_sb", [128, SLOTS * 2 * 128], BF16)
            qsb = asb("qsb_sb", [1, R], F32)
            kcol = asb("kcol_sb", [128, NQT], F32)
            ktile = [asb("ktl", [128, NQT * 128], BF16) for i in range(2)]
            qtile = [asb("qtl", [128, R], BF16) for i in range(2)]
            vh = [asb("vh", [128, NQT * 128], BF16) for i in range(2)]
            ptile = asb("ptile", [128, NQT * QW], BF16)
            qsrow = [asb("qsrow", [1, R], BF16) for i in range(2)]
            bcols = [asb("bcols", [128, NQT], F32) for i in range(2)]
            rrep_a = [asb("rrep_a", [128, QW], F32) for i in range(2)]
            pvt = [asb("pvt", [128, QW], BF16) for i in range(2)]
            m_all = asb("m_all", [128, H * SLOTS], F32)
            c_bf = asb("c_bf", [128, 1], BF16)
            c_rep = asb("c_rep", [128, 1], F32)
            crow = asb("crow", [1, 1], F32)
            rrow = [asb("rrow", [1, QW], F32) for i in range(2)]

            with ExitStack() as pps, nc.Block() as blk:
                P1 = [pps.enter_context(
                    nc.psum_tensor(un("P1"), [128, NQT * 128], F32)) for i in range(2)]
                b = dict(p1=s_p1.n, m=s_m.n)
                hd_t = {}

                @blk.sync
                def _(sy):
                    s_cc[5].wait(sy, 1)
                    s_sto.wait_totals(sy, kv_sto_t)
                    s_deq.inc(0, sy.dma_start(masks[:, :], masks_in[:, :]))
                    s_deq.inc(1, sy.dma_start(qsb[:, :], qsb_in[:, :]))
                    s_deq.inc(1, sy.dma_start(kcol[:, :], kcol_in[:, :]))
                    for h in range(H):
                        if h >= 2:
                            s_p1.wait(sy, b["p1"] + SLOTS * (h - 1))
                        for pi in range(2):
                            s_dx.inc(h, sy.dma_start(
                                bass.AP(ktile[h % 2], pi * 128,
                                        [[NQT * 128, 128], [256, SLOTS], [1, 128]]),
                                bass.AP(kt_all, (pi * D + h * 128) * R,
                                        [[R, 128], [128, SLOTS], [1, 128]])))
                        hd_t[h] = s_dx.inc(h, sy.dma_start(
                            qtile[h % 2][:, :], q_own[h * 128:(h + 1) * 128, :]))

                @blk.tensor
                def _(t):
                    s_deq.wait_totals(t)
                    for h in range(H):
                        s_dx.wait(t, h, hd_t[h])
                        for s in range(SLOTS):
                            gi = h * SLOTS + s
                            if gi >= 2:
                                s_m.wait(t, b["m"] + gi - 1)
                            w = (2 * s + 2) * 128
                            lhsT = qtile[h % 2][:, s * 128:(s + 1) * 128]
                            ins = None
                            for c0 in range((w + 511) // 512):
                                cw = min(512, w - c0 * 512)
                                ins = t.matmul(P1[gi % 2][:, c0 * 512:c0 * 512 + cw],
                                               lhsT,
                                               ktile[h % 2][:, c0 * 512:c0 * 512 + cw])
                            s_p1.inc(ins)

                @blk.vector
                def _(v):
                    for h in range(H):
                        for s in range(SLOTS):
                            gi = h * SLOTS + s
                            s_p1.wait(v, b["p1"] + gi + 1)
                            w = (2 * s + 2) * 128
                            s_m.inc(v.tensor_reduce(m_all[:, gi:gi + 1],
                                                    P1[gi % 2][:, 0:w], AX.X, ALU.max))

            with ExitStack() as pps, nc.Block() as blk:
                Pc = pps.enter_context(nc.psum_tensor(un("Pc"), [1, 128], BF16))
                b = dict(m=s_m.n, p1=s_p1.n, bnc=s_bnc.n)

                @blk.vector
                def _(v):
                    s_m.wait(v, b["m"])
                    s_m.inc(v.tensor_reduce(c_bf[:, 0:1], m_all[:, :], AX.X, ALU.max))
                    s_p1.wait(v, b["p1"] + 1)
                    s_m.inc(v.tensor_reduce(crow[0:1, 0:1], Pc[0:1, :], AX.X,
                                            ALU.max, negate=True))

                @blk.tensor
                def _(t):
                    s_m.wait(t, b["m"] + 1)
                    s_p1.inc(t.transpose(Pc[0:1, :], c_bf[:, 0:1], ident[:, :]))

                @blk.scalar
                def _(sc):
                    s_m.wait(sc, b["m"] + 2)
                    s_bnc.inc(sc.dma_start(c_bnc[0:1, :], crow[0:1, 0:1]), 16)
                    s_bnc.wait(sc, b["bnc"] + 16)
                    s_bnc.inc(sc.dma_start(c_rep[:, :],
                                           bass.AP(c_bnc, 0, [[0, 128], [1, 1]])), 16)

            with ExitStack() as pps, nc.Block() as blk:
                P2 = [pps.enter_context(nc.psum_tensor(un("P2"), [128, QW], F32))
                      for i in range(4)]
                Ps = [pps.enter_context(nc.psum_tensor(un("Ps"), [1, QW], F32))
                      for i in range(2)]
                Ppv = [pps.enter_context(nc.psum_tensor(un("Ppv"), [128, QW], F32))
                       for i in range(2)]
                b = dict(st=s_st.n, ex=s_ex.n, mk=s_mk.n, os=s_os.n, pv=s_pv.n,
                         rr=s_rr.n, pc=s_pc.n, bnc=s_bnc.n, b2=s_b2.n,
                         ba=s_ba.n, baa=s_baa.n)
                sched = []
                for h in range(H):
                    for qh in range(2):
                        kts = []
                        for kt in range(NQT):
                            smin = kt // 2
                            s0 = max(smin, qh * SPH)
                            s1 = min(SLOTS, (qh + 1) * SPH)
                            if s1 > s0:
                                kts.append((kt, s0, s1))
                        sched.append((h, qh, kts))
                flat = []
                for idx, (h, qh, kts) in enumerate(sched):
                    for ki, (kt, s0, s1) in enumerate(kts):
                        flat.append((idx, ki, kt, s0, s1))
                gpos_of = {(idx, ki): gp for gp, (idx, ki, kt, s0, s1) in enumerate(flat)}
                st_f, ba_f, baa_f, ex_f, mk_f, os_f, pv_f = [], [], [], [], [], [], []
                nst, nba, nbaa, nex, nmk, nos, npv = (b["st"], b["ba"], b["baa"],
                                                      b["ex"], b["mk"], b["os"], b["pv"])
                rr_t, pc_t = {}, {}
                nrr, npc = b["rr"], b["pc"]
                for gp, (idx, ki, kt, s0, s1) in enumerate(flat):
                    h, qh, kts = sched[idx]
                    nst += 1
                    st_f.append(nst)
                    ba_f.append(nba)
                    baa_f.append(nbaa)
                    nex += 1
                    ex_f.append(nex)
                    sm = kt // 2
                    if qh * SPH <= sm < (qh + 1) * SPH:
                        nmk += 1
                    mk_f.append(nmk)
                    nos += 1
                    os_f.append(nos)
                    npv += 1
                    pv_f.append(npv)
                    if ki == len(kts) - 1:
                        nrr += 1
                        rr_t[(idx, "recip")] = nrr
                        npc += 1
                        pc_t[idx] = npc
                        nrr += 1
                        rr_t[(idx, "evac")] = nrr
                pv_last = {}
                os_last = {}
                for gp, (idx, ki, kt, s0, s1) in enumerate(flat):
                    pv_last[idx] = pv_f[gp]
                    os_last[idx] = os_f[gp]
                bc_t, bc_w, hd_t = {}, {}, {}

                @blk.sync
                def _(sy):
                    s_cc[6].wait(sy, 1)
                    for idx, (h, qh, kts) in enumerate(sched):
                        if qh == 0:
                            if h >= 2:
                                s_pv.wait(sy, pv_last[2 * (h - 2) + 1])
                            for pi in range(2):
                                s_dx.inc(h, sy.dma_start(
                                    bass.AP(ktile[h % 2], pi * 128,
                                            [[NQT * 128, 128], [256, SLOTS], [1, 128]]),
                                    bass.AP(kt_all, (pi * D + h * 128) * R,
                                            [[R, 128], [128, SLOTS], [1, 128]])))
                            s_dx.inc(h, sy.dma_start(
                                qtile[h % 2][:, :], q_own[h * 128:(h + 1) * 128, :]))
                            for pi in range(2):
                                hd_t[h] = s_dx.inc(h, sy.dma_start(
                                    bass.AP(vh[h % 2], pi * 128,
                                            [[NQT * 128, 128], [256, SLOTS], [1, 128]]),
                                    bass.AP(v_all, pi * R * D + h * 128,
                                            [[D, 128], [128 * D, SLOTS], [1, 128]])))
                        s_rr.wait(sy, rr_t[(idx, "recip")])
                        bc_w[idx] = s_rb.inc(idx, sy.dma_start(
                            rh_bnc[h:h + 1, qh * QW:(qh + 1) * QW],
                            rrow[idx % 2][0:1, :]))
                        s_rb.wait(sy, idx, bc_w[idx])
                        if idx >= 2:
                            s_rr.wait(sy, rr_t[(idx - 2, "evac")])
                        bc_t[idx] = s_rb.inc(idx, sy.dma_start(
                            rrep_a[idx % 2][:, :],
                            bass.AP(rh_bnc, h * R + qh * QW, [[0, 128], [1, QW]])))

                @blk.tensor
                def _(t):
                    for idx, (h, qh, kts) in enumerate(sched):
                        s_dx.wait(t, h, hd_t[h])
                        s_b2.wait(t, b["b2"] + 2 * (h + 1))
                        for ki, (kt, s0, s1) in enumerate(kts):
                            w = (s1 - s0) * 128
                            gp = gpos_of[(idx, ki)]
                            if gp >= 4:
                                s_ex.wait(t, ex_f[gp - 4])
                            lhsT = ktile[h % 2][:, kt * 128:(kt + 1) * 128]
                            t.matmul(P2[gp % 4][:, 0:w], lhsT,
                                     qtile[h % 2][:, s0 * 128:s1 * 128],
                                     start=True, stop=False)
                            s_st.inc(t.matmul(
                                P2[gp % 4][:, 0:w], ones_row[0:1, :],
                                qsrow[h % 2][0:1, s0 * 128:s1 * 128],
                                start=False, stop=True))
                        for ki, (kt, s0, s1) in enumerate(kts):
                            w = (s1 - s0) * 128
                            gp = gpos_of[(idx, ki)]
                            s_mk.wait(t, mk_f[gp])
                            s_ex.wait(t, ex_f[gp])
                            if ki == 0 and idx >= 2:
                                s_rr.wait(t, rr_t[(idx - 2, "recip")])
                                s_pc.wait(t, pc_t[idx - 2])
                            off = (s0 - qh * SPH) * 128
                            pcol = kt * QW + off
                            s_os.inc(t.matmul(Ps[idx % 2][0:1, off:off + w],
                                              ones_col[:, 0:1],
                                              ptile[:, pcol:pcol + w],
                                              start=(ki == 0),
                                              stop=(ki == len(kts) - 1)))
                            s_pv.inc(t.matmul(Ppv[idx % 2][:, off:off + w],
                                              vh[h % 2][:, kt * 128:(kt + 1) * 128],
                                              ptile[:, pcol:pcol + w],
                                              start=(ki == 0),
                                              stop=(ki == len(kts) - 1)))

                @blk.vector
                def _(v):
                    for idx, (h, qh, kts) in enumerate(sched):
                        if qh == 0:
                            s_bnc.wait(v, b["bnc"])
                            if h >= 2:
                                s_ex.wait(v, ex_f[gpos_of[(2 * (h - 2) + 1,
                                                           len(sched[2 * (h - 2) + 1][2]) - 1)]])
                            s_b2.inc(v.tensor_scalar(qsrow[h % 2][0:1, :], qsb[0:1, :],
                                                     -float(slopes[h]), c_rep[0:1, 0:1],
                                                     ALU.mult, ALU.add))
                            s_b2.inc(v.tensor_scalar(bcols[h % 2][:, :], kcol[:, :],
                                                     float(slopes[h]), None, ALU.mult))
                        for ki, (kt, s0, s1) in enumerate(kts):
                            gp = gpos_of[(idx, ki)]
                            sm = kt // 2
                            if qh * SPH <= sm < (qh + 1) * SPH:
                                s_st.wait(v, st_f[gp])
                                poff = (sm - s0) * 128
                                mcol = (sm * 2 + (kt - 2 * sm)) * 128
                                s_mk.inc(v.tensor_tensor(
                                    P2[gp % 4][:, poff:poff + 128],
                                    P2[gp % 4][:, poff:poff + 128],
                                    masks[:, mcol:mcol + 128], ALU.add))
                        s_os.wait(v, os_last[idx])
                        if idx >= 2:
                            s_rb.wait(v, idx, bc_w[idx - 2])
                        s_rr.inc(v.reciprocal(rrow[idx % 2][0:1, :], Ps[idx % 2][0:1, :]))
                        s_pc.wait(v, pc_t[idx])
                        s_rb.wait(v, idx, bc_t[idx])
                        s_rr.inc(v.tensor_tensor(
                            attn_sb[:, h * R + qh * QW: h * R + (qh + 1) * QW],
                            pvt[idx % 2][:, :], rrep_a[idx % 2][:, :], ALU.mult))

                @blk.scalar
                def _(sc):
                    for idx, (h, qh, kts) in enumerate(sched):
                        for ki, (kt, s0, s1) in enumerate(kts):
                            gp = gpos_of[(idx, ki)]
                            s_st.wait(sc, st_f[gp])
                            s_mk.wait(sc, mk_f[gp])
                            s_b2.wait(sc, b["b2"] + 2 * (h + 1))
                            if idx >= 1 and ki == 0:
                                s_pv.wait(sc, pv_last[idx - 1])
                            w = (s1 - s0) * 128
                            pcol = kt * QW + (s0 - qh * SPH) * 128
                            s_ex.inc(sc.activation(ptile[:, pcol:pcol + w],
                                                   P2[gp % 4][:, 0:w], AF.Exp,
                                                   bias=bcols[h % 2][:, kt:kt + 1]))
                        s_pv.wait(sc, pv_last[idx])
                        if idx >= 2:
                            s_rr.wait(sc, rr_t[(idx - 2, "evac")])
                        s_pc.inc(sc.activation(pvt[idx % 2][:, :], Ppv[idx % 2][:, :],
                                               AF.Copy))

            inner.close()
            emit_proj(attn_sb)

        fin2_t = emit_final(xres, xres)

        with nc.sbuf_tensor(un("xb3"), [128, DK * R], BF16) as xb:
            mrep_t = emit_pass1(xres, src_wait=fin2_t)
            rrep_t = emit_pass2(xres, xb, mrep_t)
            emit_ffn(xb, rrep_t)
        out_t = emit_final(xres, yout)

        with nc.Block() as blk:
            @blk.sync
            def _(sy):
                s_out.wait_totals(sy, out_t)

    return nc


def shard_inputs(cfg, inp):
    D, I, S, B = cfg["D"], cfg["I"], cfg["S"], cfg["B"]
    DK, NT, NQT, SLOTS, R = cfg["DK"], cfg["NT"], cfg["NQT"], cfg["SLOTS"], cfg["R"]
    DKC, DTL, CT, H = cfg["DKC"], cfg["DTL"], cfg["CT"], cfg["H"]
    x = np.asarray(inp["L0_attn_hidden_states"], np.float32)
    gq = np.asarray(inp["L0_ffn_gate_qweight"])
    gs = np.asarray(inp["L0_ffn_gate_scales"], np.float32)
    gz = np.asarray(inp["L0_ffn_gate_qzeros"])
    uq = np.asarray(inp["L0_ffn_up_qweight"])
    us = np.asarray(inp["L0_ffn_up_scales"], np.float32)
    uz = np.asarray(inp["L0_ffn_up_qzeros"])
    dq = np.asarray(inp["L0_ffn_down_qweight"])
    dsc = np.asarray(inp["L0_ffn_down_scales"], np.float32)
    dz = np.asarray(inp["L0_ffn_down_qzeros"])
    wat = np.asarray(inp["L0_attn_w_attn"], np.float32)
    ba = np.asarray(inp["L0_attn_b_attn"], np.float32)
    wpr = np.asarray(inp["L0_attn_w_proj"], np.float32)
    bp = np.asarray(inp["L0_attn_b_proj"], np.float32)

    ident = np.eye(128, dtype=ml_dtypes.bfloat16)
    bac = np.ascontiguousarray(ba.reshape(CT, 128).T)
    bpc = np.ascontiguousarray(bp.reshape(DK, 128).T)

    in_maps = []
    for c in range(NCORES):
        b, p = c // 2, c % 2
        xr = x[b].reshape(NQT, 128, D)[p::2].reshape(R, D)
        xin_c = np.ascontiguousarray(xr.T)
        kp = np.arange(128)[:, None]
        qj = np.arange(128)[None, :]
        mtiles = []
        for s in range(SLOTS):
            for which in range(2):
                d = which - p
                mtiles.append(np.where(kp + d * 128 <= qj, 0.0, -30000.0))
        masks = np.concatenate(mtiles, axis=1).astype(ml_dtypes.bfloat16)
        qsbase = (((np.arange(R) // 128) * 2 + p) * 128 + 64).astype(np.float32)[None, :]
        kcolv = (np.arange(NQT)[None, :] * 128 + np.arange(128)[:, None]).astype(np.float32)
        im = {
            "xin": xin_c,
            "gq": np.ascontiguousarray(gq[c * DKC * 128:(c + 1) * DKC * 128]),
            "gs": np.ascontiguousarray(gs[c * DKC:(c + 1) * DKC]),
            "gz": np.ascontiguousarray(gz[c * DKC:(c + 1) * DKC]),
            "uq": np.ascontiguousarray(uq[c * DKC * 128:(c + 1) * DKC * 128]),
            "us": np.ascontiguousarray(us[c * DKC:(c + 1) * DKC]),
            "uz": np.ascontiguousarray(uz[c * DKC:(c + 1) * DKC]),
            "dq": np.ascontiguousarray(dq[:, c * DTL * 16:(c + 1) * DTL * 16]),
            "ds": np.ascontiguousarray(dsc[:, c * DTL * 128:(c + 1) * DTL * 128]),
            "dz": np.ascontiguousarray(dz[:, c * DTL * 16:(c + 1) * DTL * 16]),
            "wat": np.ascontiguousarray(wat[c * DKC * 128:(c + 1) * DKC * 128]),
            "wpr": np.ascontiguousarray(wpr[c * DKC * 128:(c + 1) * DKC * 128]),
            "bacols": bac,
            "bpcols": bpc,
            "ident_in": ident,
            "masks_in": masks,
            "qsb_in": qsbase,
            "kcol_in": kcolv,
        }
        in_maps.append(im)
    return in_maps


def assemble_output(cfg, results):
    D, S, B = cfg["D"], cfg["S"], cfg["B"]
    NQT, SLOTS = cfg["NQT"], cfg["SLOTS"]
    out = np.empty((B, S, D), np.float32)
    for c in range(NCORES):
        b, p = c // 2, c % 2
        y = np.asarray(results[c]["yout"])
        arr = np.ascontiguousarray(y.T).reshape(SLOTS, 128, D)
        out[b].reshape(NQT, 128, D)[p::2] = arr
    return out


def kernel(**inputs):
    cfg = CFG
    nc = build_program(cfg)
    in_maps = shard_inputs(cfg, inputs)
    res = run_bass_kernel_spmd(nc, in_maps, core_ids=list(range(NCORES)))
    return assemble_output(cfg, res.results)
